# version 23
# baseline (speedup 1.0000x reference)
import os
import numpy as np
import concourse.bass as bass
import concourse.mybir as mybir
from contextlib import ExitStack

F32 = mybir.dt.float32
BF16 = mybir.dt.bfloat16
FP8 = mybir.dt.float8e4
U8 = mybir.dt.uint8
U32 = mybir.dt.uint32
AF = mybir.ActivationFunctionType
ALU = mybir.AluOpType
AX = mybir.AxisListType.X

T, NT, TS = 4096, 32, 128
D, KC = 1024, 8
H, DH, M = 16, 64, 256
FF, NFC = 4096, 32
EPS_LN = 1e-5
EPS_K = 1e-4
NORM = float(64.0 ** (-0.25))
C_DIAG = float(0.5 * (NORM * NORM))
NEG_BIG = -3.0e38


class Ev:
    __slots__ = ("sem", "val")
    def __init__(self, sem, val):
        self.sem, self.val = sem, val


class FW:
    ENGS = ("sync", "gpsimd", "tensor", "vector", "scalar")
    DSEM = {"sync": "syncd", "gpsimd": "gpsimdd"}

    def __init__(self, nc, es):
        self.nc = nc
        self.es = es
        self.streams = {e: [] for e in self.ENGS}
        self.sems = {}
        self.cnt = {}
        for s in self.ENGS:
            self.sems[s] = es.enter_context(nc.semaphore(f"S_{s}"))
            self.cnt[s] = 0

    def _dma_key(self, key):
        k = "d_" + key
        if k not in self.sems:
            self.sems[k] = self.es.enter_context(self.nc.semaphore(k))
            self.cnt[k] = 0
        return k

    def _rec(self, eng, build, key):
        evs = []
        if key is not None:
            step = 16 if key.startswith("d_") else 1
            self.cnt[key] += step
            evs.append(Ev(key, self.cnt[key]))
        is_dma = bool(key is not None and key.startswith("d_"))
        self.streams[eng].append(["op", build, evs, is_dma])
        return evs[0] if evs else None

    def w(self, eng, *evs):
        for ev in evs:
            if ev is not None:
                self.streams[eng].append(("wait", ev))

    def dma(self, eng, out, in_, key, slow=False):
        k = self._dma_key(key)
        if slow:
            ev = self._rec(eng, lambda E: E.dma_start(out, in_, allow_slow_non_contiguous=True), k)
        else:
            ev = self._rec(eng, lambda E: E.dma_start(out, in_), k)
        self.streams[eng][-1][3] = True
        return ev

    def dmat(self, out, in_, key):
        ev = self._rec("sync", lambda E: E.dma_start_transpose(out, in_),
                       self._dma_key(key))
        self.streams["sync"][-1][3] = True
        return ev

    def mm(self, out, lhsT, rhs, start=True, stop=True, ev=False):
        return self._rec("tensor",
                         lambda E: E.matmul(out, lhsT, rhs, start=start, stop=stop),
                         "tensor" if ev else None)

    def tr(self, out, in_, ident, ev=False):
        return self._rec("tensor", lambda E: E.transpose(out, in_, ident),
                         "tensor" if ev else None)

    def v(self, build, ev=True):
        return self._rec("vector", build, "vector" if ev else None)

    def s(self, build, ev=True):
        return self._rec("scalar", build, "scalar" if ev else None)

    def g(self, build, ev=True):
        if self.cnt["gpsimd"] > 0:
            self.w("gpsimd", Ev("gpsimd", self.cnt["gpsimd"]))
        return self._rec("gpsimd", build, "gpsimd" if ev else None)

    def mark(self, eng):
        for item in reversed(self.streams[eng]):
            if item[0] == "op":
                if item[3] or len(item[2]) >= 1:
                    key = eng
                    if self.cnt[key] > 0:
                        self.w(eng, Ev(key, self.cnt[key]))
                    self.cnt[key] += 1
                    ev = Ev(key, self.cnt[key])
                    self.streams[eng].append(["op", lambda E: E.nop(), [ev], False])
                    return ev
                key = eng
                self.cnt[key] += 1
                ev = Ev(key, self.cnt[key])
                item[2].append(ev)
                return ev
        raise RuntimeError(f"mark({eng}) with no prior op")

    def barrier(self):
        marks = {e: self.mark(e) for e in self.ENGS}
        for e in self.ENGS:
            self.w(e, *[marks[o] for o in self.ENGS if o != e])
        return marks

    def emit(self):
        with self.nc.Block() as block:
            for eng in self.ENGS:
                stream = self.streams[eng]
                sems = self.sems

                def section(E, stream=stream):
                    hi = {}
                    for item in stream:
                        if item[0] == "wait":
                            ev = item[1]
                            if hi.get(ev.sem, -1) >= ev.val:
                                continue
                            E.wait_ge(sems[ev.sem], ev.val)
                            hi[ev.sem] = ev.val
                        else:
                            _, build, evs, _k = item
                            inst = build(E)
                            for ev in evs:
                                inst.then_inc(sems[ev.sem], 16 if ev.sem.startswith("d_") else 1)

                getattr(block, eng)(section)


class NS:
    pass


def ln_stats(fw, c, x_ap, col, e_x, n=D):
    st = c.sstat
    mu = st[:, col:col + 1]
    r = st[:, col + 1:col + 2]
    t1 = st[:, col + 2:col + 3]
    t2 = st[:, col + 3:col + 4]
    ssum = st[:, col + 4:col + 5]
    ssq = st[:, col + 5:col + 6]
    fw.w("scalar", *e_x)
    e_sq = fw.s(lambda E: E.activation(c.junk[:, 0:n], x_ap, AF.Square, accum_out=ssq))
    fw.w("vector", *e_x)
    e_sm = fw.v(lambda E: E.reduce_sum(ssum, x_ap, AX))
    fw.w("vector", e_sm)
    e_mu = fw.v(lambda E: E.tensor_single_scalar(mu, ssum, 1.0 / n, ALU.mult))
    fw.w("vector", e_mu)
    e1 = fw.v(lambda E: E.tensor_tensor(t1, mu, mu, ALU.mult))
    fw.w("vector", e_sq, e1)
    e2 = fw.v(lambda E: E.tensor_scalar(t2, ssq, 1.0 / n, EPS_LN, ALU.mult, ALU.add))
    fw.w("vector", e2)
    e3 = fw.v(lambda E: E.tensor_tensor(t2, t2, t1, ALU.subtract))
    fw.w("scalar", e3)
    e5 = fw.s(lambda E: E.activation(t1, t2, AF.Ln))
    fw.w("scalar", e5)
    e_r = fw.s(lambda E: E.activation(r, t1, AF.Exp, scale=-0.5))
    return mu, r, e_mu, e_r


MK_PHASES = int(os.environ.get("MK_PHASES", "9"))


def build_nc():
    nc = bass.Bass("TRN2", target_bir_lowering=False)
    es = ExitStack()
    c = NS()

    dvideo = nc.dram_tensor("video", [NT, TS, D], F32, kind="ExternalInput")
    dwq = nc.dram_tensor("wq", [D, D], F32, kind="ExternalInput")
    dwk = nc.dram_tensor("wk", [D, D], F32, kind="ExternalInput")
    dwv = nc.dram_tensor("wv", [D, D], F32, kind="ExternalInput")
    dwo = nc.dram_tensor("wo", [D, D], F32, kind="ExternalInput")
    dw1 = nc.dram_tensor("w1", [FF, D], F32, kind="ExternalInput")
    dw2 = nc.dram_tensor("w2", [D, FF], F32, kind="ExternalInput")
    dw8 = nc.dram_tensor("w8", [D, D], U8, kind="ExternalInput")
    dproj = nc.dram_tensor("proj", [M, DH], F32, kind="ExternalInput")
    dbias = {}
    for nm, sz in [("bq", D), ("bk", D), ("bv", D), ("bo", D), ("b1", FF),
                   ("b2", D), ("g2", D), ("be2", D), ("g3", D), ("be3", D), ("b8", D)]:
        dbias[nm] = nc.dram_tensor(nm, [sz], F32, kind="ExternalInput")
    dws = nc.dram_tensor("ws", [1, 1], F32, kind="ExternalInput")
    dout = nc.dram_tensor("out", [NT, TS, D], F32, kind="ExternalOutput")

    dx1 = nc.dram_tensor("x1s", [NT, TS, D], F32)
    dus = nc.dram_tensor("us", [NT, TS, D], F32)
    dks = nc.dram_tensor("ks", [NT, 128, KC, TS], BF16)
    dvs = nc.dram_tensor("vs", [NT, TS, H, DH + 1], BF16)
    dwb = {}
    for nm, shp in [("wq", [D, D]), ("wk", [D, D]), ("wv", [D, D]), ("wo", [D, D]),
                    ("w1", [FF, D]), ("w2", [D, FF])]:
        dwb[nm] = nc.dram_tensor(nm + "bf", shp, BF16)

    sb = lambda n, s, d: es.enter_context(nc.sbuf_tensor(n, s, d))
    c.ident = sb("ident", [128, 128], BF16)
    c.identf = sb("identf", [128, 128], F32)
    c.ident8 = sb("ident8", [128, 128], FP8)
    c.projn2 = sb("projn2", [128, 2 * M], BF16)
    c.b_fm = sb("b_fm", [128, 2, KC], F32)
    c.b1_fm = sb("b1_fm", [128, NFC], F32)
    c.bcastB = sb("bcastB", [128, 4, D], F32)
    c.wsv = sb("wsv", [128, 2], F32)
    c.sstat = sb("sstat", [128, 32], F32)
    c.junk = sb("junk", [128, D], F32)
    c.st_s1 = sb("st_s1", [128, NT], F32)
    c.st_mu2 = sb("st_mu2", [128, NT], F32)
    c.st_r2 = sb("st_r2", [128, NT], F32)
    c.st_diagk = sb("st_diagk", [128, NT, H], F32)
    c.st_kmax = sb("st_kmax", [128, H], F32)
    c.stabk = sb("stabk", [128, H], F32)
    c.s1k = sb("s1k", [128, NT, H], F32)
    c.s2k = sb("s2k", [128, NT, H], F32)
    c.w8T = sb("w8T", [128, KC, D], FP8)

    esBA = ExitStack()
    c.bcastA = esBA.enter_context(nc.sbuf_tensor("bcastA", [128, 4, D], F32))
    c.ctxT = esBA.enter_context(nc.sbuf_tensor("ctxT", [128, H, 2, DH + 1], BF16))

    fw = FW(nc, es)
    PH = {}

    ev = fw.g(lambda E: E.memset(c.ident[:], 0.0))
    fw.w("gpsimd", ev)
    e_id = fw.g(lambda E: E.affine_select(out=c.ident[:], in_=c.ident[:],
                                          compare_op=ALU.not_equal, fill=1.0, base=0,
                                          pattern=[[-1, 128]], channel_multiplier=1))
    ev = fw.g(lambda E: E.memset(c.identf[:], 0.0))
    fw.w("gpsimd", ev)
    e_idf = fw.g(lambda E: E.affine_select(out=c.identf[:], in_=c.identf[:],
                                           compare_op=ALU.not_equal, fill=1.0, base=0,
                                           pattern=[[-1, 128]], channel_multiplier=1))
    ev = fw.g(lambda E: E.memset(c.ident8[:], 0.0))
    fw.w("gpsimd", ev)
    e_id8 = fw.g(lambda E: E.affine_select(out=c.ident8[:], in_=c.ident8[:],
                                           compare_op=ALU.not_equal, fill=1.0, base=0,
                                           pattern=[[-1, 128]], channel_multiplier=1))
    e_kmax0 = fw.v(lambda E: E.memset(c.st_kmax[:], NEG_BIG))

    e_bq = fw.dma("sync", c.b_fm[:, 0, :], bass.AP(dbias["bq"], 0, [[1, 128], [128, KC]]), "bq", slow=True)
    e_bk = fw.dma("sync", c.b_fm[:, 1, :], bass.AP(dbias["bk"], 0, [[1, 128], [128, KC]]), "bk", slow=True)
    e_b1 = fw.dma("sync", c.b1_fm[:], bass.AP(dbias["b1"], 0, [[1, 128], [128, NFC]]), "b1", slow=True)
    e_bc = []
    for j, nm in enumerate(["g2", "be2", "bv", "bo", "g3", "be3", "b8", "b2"]):
        tgt = c.bcastA[:, j, :] if j < 4 else c.bcastB[:, j - 4, :]
        e_bc.append(fw.dma("sync", tgt,
                           bass.AP(dbias[nm], 0, [[0, 128], [1, D]]), f"bc{j}"))
    fw.w("vector", e_bc[6])
    e_hb8 = fw.v(lambda E: E.tensor_single_scalar(c.bcastB[:, 2, :], c.bcastB[:, 2, :],
                                                  0.5, ALU.mult))
    e_bc[6] = e_hb8
    e_ws = fw.dma("sync", c.wsv[:, 0:1], bass.AP(dws, 0, [[0, 128], [1, 1]]), "ws")
    c.e_ws, c.e_bc, c.e_bq, c.e_bk, c.e_b1 = e_ws, e_bc, e_bq, e_bk, e_b1

    e_cast = {}
    for nm, src in [("wk", dwk), ("wv", dwv), ("wq", dwq), ("wo", dwo),
                    ("w1", dw1), ("w2", dw2)]:
        e_cast[nm] = fw.dma("gpsimd", dwb[nm][:], src[:], "cst_" + nm)

    with nc.sbuf_tensor("projr", [128, 2, DH], F32) as projr, \
         nc.sbuf_tensor("projf", [64, M], F32) as projf, \
         nc.psum_tensor("ppj", [64, M], F32) as ppj:
        e_pl = fw.dma("sync", projr[:], bass.AP(dproj, 0, [[DH, 128], [128 * DH, 2], [1, DH]]), "proj")
        fw.w("tensor", e_pl, e_idf)
        e_t0 = fw.tr(ppj[:, 0:128], projr[:, 0, :], c.identf[:], ev=True)
        e_t1 = fw.tr(ppj[:, 128:256], projr[:, 1, :], c.identf[:], ev=True)
        fw.w("scalar", e_t0, e_t1)
        e_pf = fw.s(lambda E: E.copy(projf[:], ppj[:]))
        fw.w("vector", e_pf)
        e_pz = fw.v(lambda E: E.memset(c.projn2[:], 0.0))
        fw.w("vector", e_pz)
        e_pn0 = fw.v(lambda E: E.tensor_single_scalar(c.projn2[0:64, 0:M], projf[:], NORM, ALU.mult))
        e_pn1 = fw.v(lambda E: E.tensor_single_scalar(c.projn2[64:128, M:2 * M], projf[:], NORM, ALU.mult))

        with nc.sbuf_tensor("w8u", [128, KC, D], U8) as w8u, \
             nc.sbuf_tensor("w8h", [128, KC, D], FP8) as w8h, \
             nc.sbuf_tensor("w8hb", [128, KC, D], BF16) as w8hb, \
             nc.sbuf_tensor("du0", [128, 1024], U32) as du0, \
             nc.sbuf_tensor("du1", [128, 1024], U32) as du1, \
             nc.sbuf_tensor("du2", [128, 1024], U32) as du2, \
             nc.sbuf_tensor("df0", [128, 1024], F32) as df0, \
             nc.sbuf_tensor("df1", [128, 1024], F32) as df1:
            e_w8 = fw.dma("sync", w8u[:], bass.AP(dw8, 0, [[D, 128], [128 * D, KC], [1, D]]), "w8")

            e_dec = []
            for gi in range(KC):
                fw.w("vector", e_w8)
                ev = fw.v(lambda E, gi=gi: E.tensor_copy(du0[:], w8u[:, gi, :]))
                fw.w("vector", ev)
                ev = fw.v(lambda E: E.tensor_single_scalar(du2[:], du0[:], 7,
                                                           ALU.logical_shift_right))
                fw.w("vector", ev)
                ev = fw.v(lambda E: E.tensor_scalar(df0[:], du2[:], -2.0, 1.0,
                                                    ALU.mult, ALU.add))
                e_sign = ev
                ev = fw.v(lambda E: E.tensor_single_scalar(du1[:], du0[:], 0x7F, ALU.bitwise_and))
                fw.w("vector", ev)
                ev = fw.v(lambda E: E.tensor_single_scalar(du2[:], du1[:], 7, ALU.bitwise_and))
                e_m = ev
                ev = fw.v(lambda E: E.tensor_single_scalar(du0[:], du1[:], 8, ALU.is_ge))
                fw.w("vector", ev)
                ev = fw.v(lambda E: E.tensor_single_scalar(du0[:], du0[:], 3, ALU.logical_shift_left))
                fw.w("vector", ev, e_m)
                ev = fw.v(lambda E: E.tensor_tensor(du2[:], du2[:], du0[:], ALU.add))
                fw.w("vector", ev)
                e_af = fw.v(lambda E: E.tensor_copy(df1[:], du2[:]))
                ev = fw.v(lambda E: E.tensor_single_scalar(du1[:], du1[:], 3, ALU.logical_shift_right))
                fw.w("vector", ev)
                ev = fw.v(lambda E: E.tensor_single_scalar(du1[:], du1[:], 1, ALU.max))
                fw.w("vector", ev)
                ev = fw.v(lambda E: E.tensor_single_scalar(du1[:], du1[:], 117, ALU.add))
                fw.w("vector", ev)
                ev = fw.v(lambda E: E.tensor_single_scalar(du1[:], du1[:], 23, ALU.logical_shift_left))
                fw.w("vector", ev, e_af)
                ev = fw.v(lambda E: E.tensor_tensor(df1[:], df1[:], du1.ap().bitcast(F32), ALU.mult))
                fw.w("vector", ev, e_sign)
                ev = fw.v(lambda E: E.tensor_tensor(df1[:], df1[:], df0[:], ALU.mult))
                fw.w("vector", ev)
                ev = fw.v(lambda E, gi=gi: E.tensor_scalar(w8h[:, gi, :], df1[:], 0.5, None, ALU.mult))
                fw.w("vector", ev)
                ev = fw.v(lambda E, gi=gi: E.tensor_copy(w8hb[:, gi, :], w8h[:, gi, :]))
                e_dec.append(ev)
            with nc.psum_tensor("p8t", [128, 512], BF16) as p8t:
                prev = []
                for kc in range(KC):
                    for half in range(2):
                        for j in range(4):
                            dc = half * 4 + j
                            fw.w("tensor", e_dec[dc], e_id, *prev)
                            fw.tr(p8t[:, j * 128:(j + 1) * 128],
                                  w8hb[:, dc, kc * 128:(kc + 1) * 128], c.ident[:])
                        e_tt = fw.mark("tensor")
                        fw.w("scalar", e_tt)
                        prev = [fw.s(lambda E, kc=kc, half=half:
                                     E.copy(c.w8T[:, kc, half * 512:(half + 1) * 512], p8t[:]))]
                e_w8T = prev[0]

    c.e_id8 = e_id8
    PH["setup"] = fw.barrier()

    esP12 = ExitStack()
    c.q8T = esP12.enter_context(nc.sbuf_tensor("q8T", [128, 2, KC, TS], FP8))
    c.WkT = esP12.enter_context(nc.sbuf_tensor("WkT", [128, KC, D], BF16))
    c.WvT = esP12.enter_context(nc.sbuf_tensor("WvT", [128, KC, D], BF16))
    e_wkT, e_wvT = [], []
    for kc in range(KC):
        fw.w("sync", e_cast["wk"], e_cast["wv"])
        e_wkT.append(fw.dmat(c.WkT[:, kc, :], dwb["wk"][:, kc * 128:(kc + 1) * 128], "xb_wk"))
        e_wvT.append(fw.dmat(c.WvT[:, kc, :], dwb["wv"][:, kc * 128:(kc + 1) * 128], "xb_wv"))

    def ffn8_quant(i, vbuf, e_v, statcol, s1_dst, q8T_dst, pq8, e_pq8_free):
        mu, r, e_mu, e_r = ln_stats(fw, c, vbuf, statcol, [e_v])
        fw.w("vector", e_mu, e_r)
        e_ln = fw.v(lambda E: E.tensor_scalar(c.lnx[:], vbuf, mu, r, ALU.subtract, ALU.mult))
        st = c.sstat
        am = st[:, 12:13]
        qs = st[:, 13:14]
        fw.w("vector", e_ln)
        e_am = fw.v(lambda E: E.tensor_reduce(am, c.lnx[:], AX, ALU.max, apply_absolute_value=True))
        fw.w("vector", e_am)
        e_am2 = fw.v(lambda E: E.tensor_single_scalar(am, am, 1e-12, ALU.max))
        fw.w("vector", e_am2)
        e_qs = fw.v(lambda E: E.reciprocal(qs, am))
        fw.w("vector", e_qs)
        e_qs2 = fw.v(lambda E: E.tensor_single_scalar(qs, qs, 224.0, ALU.mult))
        fw.w("vector", c.e_ws)
        e_s1 = fw.v(lambda E: E.tensor_scalar(s1_dst, am, 1.0 / 224.0, c.wsv[:, 0:1],
                                              ALU.mult, ALU.mult))
        fw.w("vector", e_qs2)
        e_q8 = fw.v(lambda E: E.tensor_scalar(c.q8tok[:], c.lnx[:], qs, None, ALU.mult))
        fw.w("vector", e_q8)
        e_q8b = fw.v(lambda E: E.tensor_copy(c.q8bf[:], c.q8tok[:]))
        return e_q8b, e_ln

    def ffn8_trans(e_q8b, q8T_dst, pq8, e_pq8_free):
        fw.w("tensor", e_q8b, e_id, *e_pq8_free)
        for kc in range(KC):
            fw.tr(pq8[:, kc * 128:(kc + 1) * 128], c.q8bf[:, kc * 128:(kc + 1) * 128], c.ident[:])
        e_tt = fw.mark("tensor")
        fw.w("scalar", e_tt)
        e_ev1 = fw.s(lambda E, q8T_dst=q8T_dst, pq8=pq8: E.copy(
            q8T_dst[:, 0:4, :].rearrange("p a b -> p (a b)"), pq8[:, 0:512]))
        e_ev2 = fw.s(lambda E, q8T_dst=q8T_dst, pq8=pq8: E.copy(
            q8T_dst[:, 4:8, :].rearrange("p a b -> p (a b)"), pq8[:, 512:1024]))
        return [e_ev1, e_ev2], e_tt

    c.lnx = esP12.enter_context(nc.sbuf_tensor("lnx", [128, D], F32))
    c.q8tok = esP12.enter_context(nc.sbuf_tensor("q8tok", [128, D], FP8))
    c.q8bf = esP12.enter_context(nc.sbuf_tensor("q8bf", [128, D], BF16))
    PH["p1"] = fw.barrier()

    if MK_PHASES <= 1:
        fw.emit(); esP12.close(); esBA.close(); es.close(); return nc

    with nc.sbuf_tensor("vid2", [128, 2, D], F32) as vid, \
         nc.sbuf_tensor("x1st", [128, 2, D], F32) as x1st, \
         nc.sbuf_tensor("af", [128, 2, D], F32) as af, \
         nc.sbuf_tensor("abf", [128, 2, D], BF16) as abf, \
         nc.sbuf_tensor("aT", [128, 2, KC, TS], BF16) as aT, \
         nc.sbuf_tensor("kst", [128, 2, KC, TS], BF16) as kst, \
         nc.sbuf_tensor("vst", [128, 2, H, DH + 1], BF16) as vst, \
         nc.sbuf_tensor("ksq", [128, D], F32) as ksq, \
         nc.sbuf_tensor("tmp5", [128, 512], F32) as tmp5, \
         nc.psum_tensor("pz0", [128, 512], F32) as pz0, \
         nc.psum_tensor("pz1", [128, 512], F32) as pz1, \
         nc.psum_tensor("pbf", [128, 1024], BF16) as pbf, \
         nc.psum_tensor("pbf2", [128, 1024], BF16) as pbf2, \
         nc.psum_tensor("pk0", [128, 512], F32) as pk0, \
         nc.psum_tensor("pk1", [128, 512], F32) as pk1, \
         nc.psum_tensor("pxk0", [128, 512], F32) as pxk0, \
         nc.psum_tensor("pxk1", [128, 512], F32) as pxk1:
        e_ones0 = fw.v(lambda E: E.memset(vst[:, 0, :, 64:65], 1.0))
        e_ones1 = fw.v(lambda E: E.memset(vst[:, 1, :, 64:65], 1.0))
        ef = NS()
        ef.vid = [None, None]
        ef.x1 = [None, None]
        ef.x1read = [None, None]
        ef.afp = [None, None]
        ef.abfp = [None, None]
        ef.pbf_r = []
        ef.pbf2_r = []
        ef.pq8 = []
        ef.zq = [[], []]
        ef.lnx = None
        ef.q8con = []
        ef.k = [[], []]
        ef.v = [e_ones0, e_ones1]
        ef.z = []
        ef.kmm = []
        ef.vmm = []
        ef.aT_r = [[], []]
        ef.ksq_r = None
        ef.xkmm = [[], []]
        ef.xkred = None
        ef.kst_w = [[], []]

        def quantV(i):
            b = i % 2
            fw.w("sync", ef.vid[b])
            e_v = fw.dma("sync", vid[:, b, :], dvideo[i], f"vid2_{b}")
            fw.w("vector", e_v, ef.lnx, *ef.q8con)
            e_q8b, e_ln = ffn8_quant(i, vid[:, b, :], e_v, 0, c.st_s1[:, i:i + 1],
                                     None, None, None)
            ef.lnx = e_ln
            return e_q8b

        def stageA(i, e_q8b):
            b = i % 2
            evs, _e_tt = ffn8_trans(e_q8b, c.q8T[:, b, :, :], pbf, ef.pq8 + ef.zq[b] + ef.pbf_r)
            ef.q8con = [_e_tt]
            ef.pq8 = evs
            ef.pbf_r = [_e_tt]
            fw.w("tensor", e_w8T, *evs, *ef.z)
            for kc in range(KC):
                fw.mm(pz0[:], c.q8T[:, b, kc, :], c.w8T[:, kc, 0:512],
                      start=(kc == 0), stop=(kc == KC - 1))
            for kc in range(KC):
                fw.mm(pz1[:], c.q8T[:, b, kc, :], c.w8T[:, kc, 512:1024],
                      start=(kc == 0), stop=(kc == KC - 1))
            e_zmm = fw.mark("tensor")
            ef.zq[b] = [e_zmm]
            s1c = c.st_s1[:, i:i + 1]
            fw.w("vector", e_zmm, ef.lnx, ef.x1[b], ef.x1read[b], c.e_bc[6], c.e_bc[2])
            e1 = fw.v(lambda E, s1c=s1c: E.scalar_tensor_tensor(tmp5[:], pz0[:], s1c, c.bcastB[:, 2, 0:512],
                                                       ALU.mult, ALU.add))
            fw.w("vector", e1)
            e2 = fw.v(lambda E, b=b: E.tensor_tensor(x1st[:, b, 0:512], tmp5[:], vid[:, b, 0:512], ALU.add))
            fw.w("vector", e2)
            e3 = fw.v(lambda E, s1c=s1c: E.scalar_tensor_tensor(tmp5[:], pz1[:], s1c, c.bcastB[:, 2, 512:1024],
                                                       ALU.mult, ALU.add))
            fw.w("vector", e3)
            e4 = fw.v(lambda E, b=b: E.tensor_tensor(x1st[:, b, 512:1024], tmp5[:], vid[:, b, 512:1024], ALU.add))
            ef.z = [e2, e4]
            ef.vid[b] = e4
            e_x1 = [e2, e4]
            fw.w("gpsimd", *e_x1)
            e_sp = fw.dma("gpsimd", dx1[i], x1st[:, b, :], f"spx1_{b}")
            ef.x1[b] = e_sp
            mu, r, e_mu, e_r = ln_stats(fw, c, x1st[:, b, :], 6, e_x1)
            fw.w("vector", e_mu, e_r, ef.afp[b])
            e_m2 = fw.v(lambda E, i=i, mu=mu: E.tensor_copy(c.st_mu2[:, i:i + 1], mu))
            e_r2 = fw.v(lambda E, i=i, r=r: E.tensor_copy(c.st_r2[:, i:i + 1], r))
            e_af = fw.v(lambda E, b=b, mu=mu, r=r: E.tensor_scalar(af[:, b, :], x1st[:, b, :], mu, r, ALU.subtract, ALU.mult))
            ef.x1read[b] = e_af
            fw.w("vector", e_af, ef.abfp[b])
            e_abf = fw.v(lambda E, b=b: E.tensor_copy(abf[:, b, :], af[:, b, :]))
            ef.afp[b] = e_abf
            fw.w("tensor", e_abf, *ef.pbf_r)
            for kc in range(KC):
                fw.tr(pbf[:, kc * 128:(kc + 1) * 128], abf[:, b, kc * 128:(kc + 1) * 128], c.ident[:])
            e_aTmm = fw.mark("tensor")
            ef.abfp[b] = e_aTmm
            fw.w("scalar", e_aTmm, *ef.aT_r[b])
            e_aT = fw.s(lambda E, b=b: E.copy(aT[:, b, :, :].rearrange("p a b -> p (a b)"), pbf[:]))
            ef.pbf_r = [e_aT]
            return e_aT

        def stageB(i, e_aT):
            b = i % 2
            e_kev = []
            for half in range(2):
                pk = pk0 if half == 0 else pk1
                fw.w("tensor", e_aT, e_wkT[-1], *ef.kmm, *ef.vmm)
                for j in range(4):
                    dc = half * 4 + j
                    for kc in range(KC):
                        fw.mm(pk[:, j * 128:(j + 1) * 128],
                              c.WkT[:, kc, dc * 128:(dc + 1) * 128], aT[:, b, kc, :],
                              start=(kc == 0), stop=(kc == KC - 1))
                e_kmm = fw.mark("tensor")
                fw.w("scalar", e_kmm, c.e_bk, *ef.k[b], *ef.kst_w[b])
                for j in range(4):
                    dc = half * 4 + j
                    e_kev.append(fw.s(lambda E, b=b, dc=dc, j=j, pk=pk:
                                      E.activation(kst[:, b, dc, :], pk[:, j * 128:(j + 1) * 128],
                                                   AF.Identity, bias=c.b_fm[:, 1, dc:dc + 1])))
                ef.kmm = e_kev[-4:]
            ef.kst_w[b] = e_kev
            fw.w("gpsimd", *e_kev)
            e_ksp = fw.dma("gpsimd", dks[i], kst[:, b, :, :], f"spk_{b}")
            fw.w("tensor", e_kev[-1], e_wvT[-1], *ef.vmm)
            for kc in range(KC):
                fw.mm(pk0[:], aT[:, b, kc, :], c.WvT[:, kc, 0:512],
                      start=(kc == 0), stop=(kc == KC - 1))
            for kc in range(KC):
                fw.mm(pk1[:], aT[:, b, kc, :], c.WvT[:, kc, 512:1024],
                      start=(kc == 0), stop=(kc == KC - 1))
            e_vmm = fw.mark("tensor")
            ef.aT_r[b] = [e_vmm]
            fw.w("vector", e_vmm, ef.v[b])
            e_v0 = fw.v(lambda E, b=b: E.tensor_tensor(
                vst[:, b, 0:8, 0:64], pk0[:].rearrange("p (h d) -> p h d", h=8),
                c.bcastA[:, 2, 0:512].rearrange("p (h d) -> p h d", h=8), ALU.add))
            e_v1 = fw.v(lambda E, b=b: E.tensor_tensor(
                vst[:, b, 8:16, 0:64], pk1[:].rearrange("p (h d) -> p h d", h=8),
                c.bcastA[:, 2, 512:1024].rearrange("p (h d) -> p h d", h=8), ALU.add))
            ef.vmm = [e_v0, e_v1]
            fw.w("gpsimd", e_v0, e_v1)
            e_vsp = fw.dma("gpsimd", dvs[i], vst[:, b, :, :], f"spv_{b}")
            ef.v[b] = e_vsp
            fw.w("tensor", e_kev[-1], *ef.pbf2_r)
            for kc in range(KC):
                fw.tr(pbf2[:, kc * 128:(kc + 1) * 128], kst[:, b, kc, :], c.ident[:])
            e_ktt = fw.mark("tensor")
            fw.w("scalar", e_ktt, ef.ksq_r)
            e_ksq = fw.s(lambda E: E.activation(ksq[:], pbf2[:], AF.Square))
            ef.pbf2_r = [e_ksq]
            fw.w("vector", e_ksq)
            e_dk = fw.v(lambda E, i=i: E.tensor_reduce(
                c.st_diagk[:, i, :, None], ksq[:].rearrange("p (h d) -> p h d", h=H), AX, ALU.add))
            ef.ksq_r = e_dk
            for hp in range(KC):
                pb_ = pxk0 if hp % 2 == 0 else pxk1
                fw.w("tensor", e_kev[hp], *ef.xkmm[hp % 2])
                fw.mm(pb_[:, 0:512], kst[:, b, hp, :], c.projn2[:])
                e_xk = fw.mark("tensor")
                fw.w("vector", e_xk, ef.xkred)
                e_red = fw.v(lambda E, pb_=pb_: E.tensor_reduce(
                    c.sstat[:, 14:16, None], pb_[:].rearrange("p (h m) -> p h m", h=2), AX, ALU.max))
                fw.w("vector", e_red)
                e_max = fw.v(lambda E, hp=hp: E.tensor_tensor(
                    c.st_kmax[:, 2 * hp:2 * hp + 2], c.st_kmax[:, 2 * hp:2 * hp + 2],
                    c.sstat[:, 14:16], ALU.max))
                ef.xkmm[hp % 2] = [e_max]
                ef.xkred = e_max
            e_xk_all = fw.mark("tensor")
            ef.k[b] = [e_ksp, e_xk_all]

        prevA = None
        for i in range(NT):
            e_q8b = quantV(i)
            if prevA is not None:
                stageB(i - 1, prevA)
            prevA = stageA(i, e_q8b)
        stageB(NT - 1, prevA)
        c.e_spills_p2 = [ef.x1[0], ef.x1[1], ef.k[0][0], ef.k[1][0], ef.v[0], ef.v[1]]
        PH["p2"] = fw.barrier()
    esP12.close()
    if MK_PHASES <= 2:
        fw.emit(); esBA.close(); es.close(); return nc

    dstab = nc.dram_tensor("stabs", [H, 1], F32)
    esP3 = ExitStack()
    c.ctx_acc = esP3.enter_context(nc.sbuf_tensor("ctx_acc", [65, KC, 2 * M], F32))
    krd = esP3.enter_context(nc.sbuf_tensor("krd", [128, 2, KC, TS], BF16))
    vrd = esP3.enter_context(nc.sbuf_tensor("vrd", [128, 2, H, DH + 1], BF16))
    E1k = esP3.enter_context(nc.sbuf_tensor("E1k", [128, H, M], F32))
    kp = esP3.enter_context(nc.sbuf_tensor("kp", [128, 2, H, M], BF16))
    stt16 = esP3.enter_context(nc.sbuf_tensor("stt16", [16, 2], F32))
    pxk0 = esP3.enter_context(nc.psum_tensor("pxr0", [128, 512], F32))
    pxk1 = esP3.enter_context(nc.psum_tensor("pxr1", [128, 512], F32))
    pcx0 = esP3.enter_context(nc.psum_tensor("pcx0", [65, 512], F32))
    pcx1 = esP3.enter_context(nc.psum_tensor("pcx1", [65, 512], F32))
    pst = esP3.enter_context(nc.psum_tensor("pst", [16, 128], F32))
    if True:
        e_ctx0 = fw.v(lambda E: E.memset(c.ctx_acc[:], 0.0))
        fw.w("vector", *c.e_spills_p2)
        fw.w("tensor", *c.e_spills_p2)
        fw.w("gpsimd", *c.e_spills_p2)
        fw.w("scalar", *c.e_spills_p2)
        fw.w("tensor", PH["p2"]["vector"], e_idf)
        e_t = fw.tr(pst[:], c.st_kmax[:], c.identf[:], ev=True)
        fw.w("vector", e_t)
        e_red = fw.v(lambda E: E.tensor_reduce(stt16[:, 0:1], pst[:], AX, ALU.max))
        fw.w("gpsimd", e_red)
        e_st1 = fw.dma("gpsimd", dstab[:], stt16[:, 0:1], "stab", slow=True)
        fw.w("sync", e_st1)
        e_st2 = fw.dma("sync", c.stabk[:], bass.AP(dstab, 0, [[0, 128], [1, H]]), "stabb")
        fw.w("vector", e_st2, PH["p2"]["scalar"])
        s1kf = c.s1k.ap().rearrange("p a b -> p (a b)")
        s2kf = c.s2k.ap().rearrange("p a b -> p (a b)")
        e_a = fw.v(lambda E: E.tensor_single_scalar(s1kf, c.st_diagk.ap().rearrange("p a b -> p (a b)"),
                                                    -1.0 / 16.0, ALU.mult))
        fw.w("vector", e_a)
        stabk_b = bass.AP(c.stabk.ap().tensor, 0, [[c.stabk.ap().ap[0][0], 128], [0, NT], [1, H]])
        e_b = fw.v(lambda E: E.tensor_tensor(c.s1k[:], c.s1k[:], stabk_b, ALU.subtract))
        fw.w("scalar", e_b)
        e_c = fw.s(lambda E: E.activation(s1kf, s1kf, AF.Exp))
        fw.w("vector", e_c)
        e_d = fw.v(lambda E: E.reciprocal(s2kf, s1kf))
        fw.w("vector", e_d)
        e_s2k = fw.v(lambda E: E.tensor_single_scalar(s2kf, s2kf, EPS_K, ALU.mult))

        fw.w("sync", *c.e_spills_p2)
        ef = NS()
        ef.kb = [None, None]
        ef.vb = [None, None]
        ef.xk = [[], []]
        ef.E1 = []
        ef.kp_r = [[], []]
        ef.cx = [[], []]
        ctxf = c.ctx_acc

        def stageA(i):
            b = i % 2
            fw.w("sync", ef.kb[b])
            e_k = fw.dma("sync", krd[:, b, :, :], dks[i], f"ldk{b}")
            fw.w("sync", ef.vb[b])
            e_v = fw.dma("sync", vrd[:, b, :, :], dvs[i], f"ldv{b}")
            e_E1s = []
            for hp in range(KC):
                pb_ = pxk0 if hp % 2 == 0 else pxk1
                fw.w("tensor", e_k, *ef.xk[hp % 2])
                fw.mm(pb_[:, 0:512], krd[:, b, hp, :], c.projn2[:])
                e_xk = fw.mark("tensor")
                fw.w("scalar", e_xk, *ef.kp_r[1 - b][2 * hp:2 * hp + 2])
                e_E1 = fw.s(lambda E, hp=hp, pb_=pb_: E.activation(
                    E1k[:, 2 * hp:2 * hp + 2, :].rearrange("p a b -> p (a b)"), pb_[:], AF.Exp))
                ef.xk[hp % 2] = [e_E1]
                e_E1s.append(e_E1)
            e_kps = []
            for h in range(H):
                fw.w("vector", e_E1s[h // 2], e_s2k)
                e_kp = fw.v(lambda E, h=h, i=i, b=b: E.tensor_scalar(
                    kp[:, b, h, :], E1k[:, h, :], c.s2k[:, i, h:h + 1], c.s1k[:, i, h:h + 1],
                    ALU.add, ALU.mult))
                e_kps.append(e_kp)
            ef.kp_r[b] = e_kps
            ef.kb[b] = e_E1s[-1]
            return e_v, e_kps

        def stageB(i, e_v, e_kps):
            b = i % 2
            e_adds = []
            for j in range(KC):
                pb_ = pcx0 if j % 2 == 0 else pcx1
                fw.w("tensor", e_v, e_kps[2 * j], e_kps[2 * j + 1], *ef.cx[j % 2])
                fw.mm(pb_[:, 0:256], vrd[:, b, 2 * j, :], kp[:, b, 2 * j, :])
                fw.mm(pb_[:, 256:512], vrd[:, b, 2 * j + 1, :], kp[:, b, 2 * j + 1, :])
                e_cmm = fw.mark("tensor")
                fw.w("vector", e_cmm, e_ctx0)
                e_add = fw.v(lambda E, j=j, pb_=pb_: E.tensor_tensor(
                    ctxf[:, j, :], ctxf[:, j, :], pb_[:], ALU.add))
                ef.cx[j % 2] = [e_add]
                e_adds.append(e_add)
            ef.vb[b] = e_adds[-1]

        prevB = None
        for i in range(NT):
            av = stageA(i)
            if prevB is not None:
                stageB(i - 1, *prevB)
            prevB = av
        stageB(NT - 1, *prevB)
        with nc.psum_tensor("pct", [128, 144], F32) as pct:
            prev = []
            for j in range(KC):
                for half in range(2):
                    fw.w("tensor", ef.cx[0][0] if ef.cx[0] else None,
                         ef.cx[1][0] if ef.cx[1] else None, *prev)
                    fw.tr(pct[:, 0:65], ctxf[0:65, j, half * 256:half * 256 + 128], c.identf[0:65, 0:65])
                    fw.tr(pct[:, 72:137], ctxf[0:65, j, half * 256 + 128:half * 256 + 256], c.identf[0:65, 0:65])
                    e_t = fw.mark("tensor")
                    fw.w("scalar", e_t)
                    e_e1 = fw.s(lambda E, j=j, half=half: E.copy(
                        c.ctxT[:, 2 * j + half, 0, :], pct[:, 0:65]))
                    e_e2 = fw.s(lambda E, j=j, half=half: E.copy(
                        c.ctxT[:, 2 * j + half, 1, :], pct[:, 72:137]))
                    prev = [e_e1, e_e2]
            e_ctxT = prev
        PH["p3"] = fw.barrier()
    esP3.close()
    if MK_PHASES <= 3:
        fw.emit(); esBA.close(); es.close(); return nc

    esC1w = ExitStack()
    c.WqT = esC1w.enter_context(nc.sbuf_tensor("WqT", [128, KC, D], BF16))
    c.WoT = esC1w.enter_context(nc.sbuf_tensor("WoT", [128, KC, D], BF16))
    c.qstat = esC1w.enter_context(nc.sbuf_tensor("qstat", [128, 64], F32))
    e_wqT, e_woT = [], []
    for kc in range(KC):
        fw.w("sync", e_cast["wq"], e_cast["wo"])
        e_wqT.append(fw.dmat(c.WqT[:, kc, :], dwb["wq"][:, kc * 128:(kc + 1) * 128], "xb_wq"))
        e_woT.append(fw.dmat(c.WoT[:, kc, :], dwb["wo"][:, kc * 128:(kc + 1) * 128], "xb_wo"))

        esC1 = ExitStack()
    x1rd = esC1.enter_context(nc.sbuf_tensor("x1rd", [128, 2, D], F32))
    afc = esC1.enter_context(nc.sbuf_tensor("afc", [128, 5, D], F32))
    abfc = esC1.enter_context(nc.sbuf_tensor("abfc", [128, 2, D], BF16))
    qsqs = esC1.enter_context(nc.sbuf_tensor("qsqs", [128, D], F32))
    aTc = esC1.enter_context(nc.sbuf_tensor("aTc", [128, 2, KC, TS], BF16))
    qfm = esC1.enter_context(nc.sbuf_tensor("qfm", [128, KC, TS], BF16))
    E1q = esC1.enter_context(nc.sbuf_tensor("E1q", [128, 16, M], F32))
    qp = esC1.enter_context(nc.sbuf_tensor("qp", [128, 3, H, M], BF16))
    qpT = esC1.enter_context(nc.sbuf_tensor("qpTs", [128, 4096], BF16))
    atok = esC1.enter_context(nc.sbuf_tensor("atok", [128, H, DH], BF16))
    apre = esC1.enter_context(nc.sbuf_tensor("apre", [128, KC, TS], BF16))
    sfull = esC1.enter_context(nc.sbuf_tensor("sfull", [128, D], F32))
    uful = esC1.enter_context(nc.sbuf_tensor("ufull", [128, 2, D], F32))
    pbfA = esC1.enter_context(nc.psum_tensor("pbfA", [128, 1024], BF16))
    pbfA2 = esC1.enter_context(nc.psum_tensor("pbfA2", [128, 1024], BF16))
    pq0 = esC1.enter_context(nc.psum_tensor("pq0", [128, 512], F32))
    pq1 = esC1.enter_context(nc.psum_tensor("pq1", [128, 512], F32))
    pxq0 = esC1.enter_context(nc.psum_tensor("pxq0", [128, 512], F32))
    pxq1 = esC1.enter_context(nc.psum_tensor("pxq1", [128, 512], F32))
    pxv0 = pxq0.ap().rearrange("p (a b) -> p a b", a=4)
    pxv1 = pxq1.ap().rearrange("p (a b) -> p a b", a=4)
    pzo0 = esC1.enter_context(nc.psum_tensor("pzo0", [128, 512], F32))
    pzo1 = esC1.enter_context(nc.psum_tensor("pzo1", [128, 512], F32))
    if True:
        fw.w("sync", c.e_spills_p2[0], c.e_spills_p2[1])
        ef = NS()
        ef.x1r = [None, None]
        ef.abfp = [None, None]
        ef.tc = 0
        ef.bk = [None, None]
        ef.aT_rd = [[], []]
        ef.pq_ev = [[], []]
        ef.zof = [None, None]
        ef.qsq = None
        ef.E1_rd = None
        ef.qpTm = [None, None, None]
        ef.xq = [[], []]
        ef.po = [[], []]
        ef.e_aT = [None, None]
        ef.qev_last = None
        ef.attn_in = []
        ef.e_zomm = None
        ef.qp_ev = [None, None, None]
        ef.sread = None
        ef.u = [None, None]

        def trbank(emit_fn, evac_fn, mm_waits=(), ev_waits=()):
            p = ef.tc % 2
            bank = pbfA if p == 0 else pbfA2
            fw.w("tensor", ef.bk[p], *mm_waits)
            emit_fn(bank)
            e_mark = fw.mark("tensor")
            fw.w("scalar", e_mark, *ev_waits)
            e_ev = evac_fn(bank)
            ef.bk[p] = e_ev
            ef.tc += 1
            return e_mark, e_ev

        def A1(i):
            b = i % 2
            ba = i % 5
            fw.w("sync", ef.x1r[b])
            e_x1 = fw.dma("sync", x1rd[:, b, :], dx1[i], f"c1x1{b}")
            fw.w("vector", e_x1, PH["p3"]["scalar"])
            e_af = fw.v(lambda E, b=b, ba=ba, i=i: E.tensor_scalar(
                afc[:, ba, :], x1rd[:, b, :], c.st_mu2[:, i:i + 1], c.st_r2[:, i:i + 1],
                ALU.subtract, ALU.mult))
            ef.x1r[b] = e_af
            fw.w("vector", e_af, ef.abfp[b])
            e_abf = fw.v(lambda E, b=b, ba=ba: E.tensor_copy(abfc[:, b, :], afc[:, ba, :]))
            e_aTmm, e_aT = trbank(
                lambda bank, b=b: [fw.tr(bank[:, kc * 128:(kc + 1) * 128],
                                         abfc[:, b, kc * 128:(kc + 1) * 128], c.ident[:])
                                   for kc in range(KC)],
                lambda bank, b=b: fw.s(lambda E, bank=bank, b=b: E.copy(
                    aTc[:, b, :, :].rearrange("p a b -> p (a b)"), bank[:])),
                mm_waits=[e_abf], ev_waits=list(ef.aT_rd[b]))
            ef.abfp[b] = e_aTmm
            ef.e_aT[b] = e_aT

        def B1(i):
            ba = i % 3
            e_qpT = []
            for rnd in range(4):
                e_mark, e_ev = trbank(
                    lambda bank, ba=ba, rnd=rnd: [fw.tr(
                        bank[:, t * 128:(t + 1) * 128],
                        qp[:, ba, (rnd * 8 + t) // 2, ((rnd * 8 + t) % 2) * 128:((rnd * 8 + t) % 2) * 128 + 128],
                        c.ident[:]) for t in range(8)],
                    lambda bank, rnd=rnd: fw.s(lambda E, rnd=rnd, bank=bank: E.copy(
                        qpT[:, rnd * 1024:(rnd + 1) * 1024], bank[:])),
                    mm_waits=[ef.qp_ev[ba]] if rnd == 0 else [])
                e_qpT.append(e_ev)
            ef.qpTm[ba] = e_mark
            e_atoks = []
            for grp in range(4):
                pb_ = pxv0 if grp % 2 == 0 else pxv1
                fw.w("tensor", e_qpT[grp], *e_ctxT, *ef.po[grp % 2])
                for j in range(4):
                    h = grp * 4 + j
                    fw.mm(pb_[:, j, 0:65], qpT[:, h * 256:h * 256 + 128], c.ctxT[:, h, 0, :],
                          start=True, stop=False)
                    fw.mm(pb_[:, j, 0:65], qpT[:, h * 256 + 128:h * 256 + 256], c.ctxT[:, h, 1, :],
                          start=False, stop=True)
                e_omm = fw.mark("tensor")
                fw.w("vector", e_omm)
                e_rd = fw.v(lambda E, pb_=pb_, grp=grp: E.reciprocal(
                    c.sstat[:, 16 + 4 * grp:20 + 4 * grp], pb_[:, :, 64]))
                fw.w("vector", e_rd)
                e_as = []
                for j in range(4):
                    h = grp * 4 + j
                    e_a = fw.v(lambda E, h=h, j=j, grp=grp, pb_=pb_: E.tensor_scalar(
                        atok[:, h, :], pb_[:, j, 0:64], c.sstat[:, 16 + 4 * grp + j:17 + 4 * grp + j],
                        None, ALU.mult))
                    e_as.append(e_a)
                ef.po[grp % 2] = e_as
                e_atoks.extend(e_as)
            ef.attn_in = e_atoks

        def A2(i):
            b = i % 2
            e_qev = []
            e_qmm = None
            for half in range(2):
                pk = pq0 if half == 0 else pq1
                fw.w("tensor", ef.e_aT[b], e_wqT[-1], *ef.pq_ev[half])
                for j in range(4):
                    dc = half * 4 + j
                    for kc in range(KC):
                        fw.mm(pk[:, j * 128:(j + 1) * 128],
                              c.WqT[:, kc, dc * 128:(dc + 1) * 128], aTc[:, b, kc, :],
                              start=(kc == 0), stop=(kc == KC - 1))
                e_qmm = fw.mark("tensor")
                fw.w("scalar", e_qmm, c.e_bq)
                for j in range(4):
                    dc = half * 4 + j
                    e_qev.append(fw.s(lambda E, dc=dc, j=j, pk=pk: E.activation(
                        qfm[:, dc, :], pk[:, j * 128:(j + 1) * 128],
                        AF.Identity, bias=c.b_fm[:, 0, dc:dc + 1])))
            ef.aT_rd[b] = [e_qmm]
            ef.pq_ev = [e_qev[0:4], e_qev[4:8]]
            ef.qev_last = e_qev[-1]
            _, e_qsq = trbank(
                lambda bank: [fw.tr(bank[:, kc * 128:(kc + 1) * 128], qfm[:, kc, :], c.ident[:])
                              for kc in range(KC)],
                lambda bank: fw.s(lambda E, bank=bank: E.activation(qsqs[:], bank[:], AF.Square)),
                mm_waits=[e_qev[-1]], ev_waits=[ef.qsq])
            fw.w("vector", e_qsq)
            e_dq = fw.v(lambda E: E.tensor_reduce(
                c.qstat[:, 0:16, None], qsqs[:].rearrange("p (h d) -> p h d", h=H), AX, ALU.add))
            ef.qsq = e_dq
            ef.E1h = []
            for hp in range(KC):
                pb_ = pxq0 if hp % 2 == 0 else pxq1
                fw.w("tensor", e_qev[hp] if hp < 2 else None, *ef.xq[hp % 2],
                     *(ef.po[hp % 2] if hp < 2 else []))
                fw.mm(pb_[:, 0:512], qfm[:, hp, :], c.projn2[:])
                e_xq = fw.mark("tensor")
                fw.w("scalar", e_xq, ef.E1_rd if hp == 0 else None)
                e_E1 = fw.s(lambda E, hp=hp, pb_=pb_: E.activation(
                    E1q[:, 2 * hp:2 * hp + 2, :].rearrange("p a b -> p (a b)"), pb_[:], AF.Exp))
                ef.xq[hp % 2] = [e_E1]
                ef.E1h.append(e_E1)

        def B2(i):
            e_att, e_apre = trbank(
                lambda bank: [fw.tr(bank[:, kc * 128:(kc + 1) * 128],
                                    atok.ap().rearrange("p h d -> p (h d)")[:, kc * 128:(kc + 1) * 128],
                                    c.ident[:]) for kc in range(KC)],
                lambda bank: fw.s(lambda E, bank=bank: E.copy(
                    apre[:].rearrange("p a b -> p (a b)"), bank[:])),
                mm_waits=list(ef.attn_in))
            fw.w("tensor", e_apre, e_woT[-1], ef.zof[0], ef.zof[1])
            for kc in range(KC):
                fw.mm(pzo0[:], apre[:, kc, :], c.WoT[:, kc, 0:512],
                      start=(kc == 0), stop=(kc == KC - 1))
            for kc in range(KC):
                fw.mm(pzo1[:], apre[:, kc, :], c.WoT[:, kc, 512:1024],
                      start=(kc == 0), stop=(kc == KC - 1))
            ef.e_zomm = fw.mark("tensor")

        def A3(i):
            ba = i % 3
            e_qps = []
            for half in range(2):
                o = 8 * half
                fw.w("vector", ef.E1h[3 + 4 * half])
                e_mx = fw.v(lambda E, o=o: E.tensor_reduce(
                    c.qstat[:, 16 + o:24 + o, None], E1q[:, o:o + 8, :], AX, ALU.max))
                fw.w("scalar", ef.qsq if half == 0 else None, e_mx)
                e_ed = fw.s(lambda E, o=o: E.activation(
                    c.qstat[:, 32 + o:40 + o], c.qstat[:, o:8 + o], AF.Exp, scale=-1.0 / 16.0))
                fw.w("vector", e_mx)
                e_mr = fw.v(lambda E, o=o: E.reciprocal(c.qstat[:, 16 + o:24 + o],
                                                        c.qstat[:, 16 + o:24 + o]))
                fw.w("vector", e_mr, e_ed)
                e_s1 = fw.v(lambda E, o=o: E.tensor_tensor(
                    c.qstat[:, 32 + o:40 + o], c.qstat[:, 32 + o:40 + o],
                    c.qstat[:, 16 + o:24 + o], ALU.mult))
                fw.w("vector", e_s1)
                e_s2 = fw.v(lambda E, o=o: E.reciprocal(c.qstat[:, 48 + o:56 + o],
                                                        c.qstat[:, 32 + o:40 + o]))
                fw.w("vector", e_s2)
                e_s2b = fw.v(lambda E, o=o: E.tensor_single_scalar(
                    c.qstat[:, 48 + o:56 + o], c.qstat[:, 48 + o:56 + o], EPS_K, ALU.mult))
                fw.w("vector", e_s2b, ef.qpTm[ba] if half == 0 else None)
                for hh in range(8):
                    h = o + hh
                    e_qp = fw.v(lambda E, h=h, ba=ba: E.tensor_scalar(
                        qp[:, ba, h, :], E1q[:, h, :],
                        c.qstat[:, 48 + h:49 + h], c.qstat[:, 32 + h:33 + h],
                        ALU.add, ALU.mult))
                    e_qps.append(e_qp)
            ef.qp_ev[ba] = e_qps[-1]
            ef.E1_rd = e_qps[-1]

        def B3(i):
            b = i % 2
            ba = i % 5
            fw.w("vector", ef.e_zomm, ef.sread, c.e_bc[3])
            e_sA = fw.v(lambda E, ba=ba: E.tensor_tensor(sfull[:, 0:512], pzo0[:], afc[:, ba, 0:512], ALU.add))
            e_sB = fw.v(lambda E, ba=ba: E.tensor_tensor(sfull[:, 512:1024], pzo1[:], afc[:, ba, 512:1024], ALU.add))
            ef.zof = [e_sA, e_sB]
            fw.w("vector", e_sA, e_sB)
            e_sC = fw.v(lambda E: E.tensor_tensor(sfull[:], sfull[:], c.bcastA[:, 3, :], ALU.add))
            mu, r, e_mu, e_r = ln_stats(fw, c, sfull[:], 6, [e_sC])
            fw.w("vector", e_mu, e_r, ef.u[b], c.e_bc[0], c.e_bc[1])
            e_u1 = fw.v(lambda E, b=b, mu=mu, r=r: E.tensor_scalar(uful[:, b, :], sfull[:], mu, r,
                                                       ALU.subtract, ALU.mult))
            fw.w("vector", e_u1)
            e_u2 = fw.v(lambda E, b=b: E.tensor_tensor(uful[:, b, :], uful[:, b, :],
                                                       c.bcastA[:, 0, :], ALU.mult))
            fw.w("vector", e_u2)
            e_u3 = fw.v(lambda E, b=b: E.tensor_tensor(uful[:, b, :], uful[:, b, :],
                                                       c.bcastA[:, 1, :], ALU.add))
            ef.sread = e_u1
            fw.w("gpsimd", e_u3)
            e_usp = fw.dma("gpsimd", dus[i], uful[:, b, :], f"spu{b}")
            ef.u[b] = e_usp

        A1(0)
        A1(1)
        for i in range(NT):
            A2(i)
            if i + 2 < NT:
                A1(i + 2)
            if i > 1:
                B1(i - 2)
                B2(i - 2)
            A3(i)
            if i > 1:
                B3(i - 2)
        for i in (NT - 2, NT - 1):
            B1(i)
            B2(i)
            B3(i)
        c.e_spills_c1 = [ef.u[0], ef.u[1]]
        PH["c1"] = fw.barrier()
    esC1.close()
    esC1w.close()
    esBA.close()
    if MK_PHASES <= 4:
        fw.emit(); es.close(); return nc

    c.W1T = es.enter_context(nc.sbuf_tensor("W1T", [128, KC, FF], BF16))
    c.W2T = es.enter_context(nc.sbuf_tensor("W2T", [128, NFC, D], BF16))
    fw.w("sync", c.e_spills_c1[0], c.e_spills_c1[1])
    e_w1T = []
    for kc in range(KC):
        fw.w("sync", e_cast["w1"])
        e_w1T.append(fw.dmat(c.W1T[:, kc, :], dwb["w1"][:, kc * 128:(kc + 1) * 128], "xb_w1"))
    e_w2T = []
    for fc in range(NFC):
        fw.w("sync", e_cast["w2"])
        e_w2T.append(fw.dmat(c.W2T[:, fc, :], dwb["w2"][:, fc * 128:(fc + 1) * 128], "xb_w2"))

        esC2 = ExitStack()
    urd = esC2.enter_context(nc.sbuf_tensor("urd", [128, 3, D], F32))
    x1rd2 = esC2.enter_context(nc.sbuf_tensor("x1rd2", [128, D], F32))
    ubf = esC2.enter_context(nc.sbuf_tensor("ubf", [128, D], BF16))
    uT2 = esC2.enter_context(nc.sbuf_tensor("uT2", [128, KC, 256], BF16))
    q8cT = esC2.enter_context(nc.sbuf_tensor("q8cTs", [128, KC * TS], FP8))
    hsb = esC2.enter_context(nc.sbuf_tensor("hsb", [128, 2, 256], BF16))
    wful2 = esC2.enter_context(nc.sbuf_tensor("wful2", [128, 2, D], F32))
    x2f = esC2.enter_context(nc.sbuf_tensor("x2f", [128, D], F32))
    q8c = esC2.enter_context(nc.sbuf_tensor("q8c", [128, D], FP8))
    q8cb = esC2.enter_context(nc.sbuf_tensor("q8cb", [128, D], BF16))
    pbfB = esC2.enter_context(nc.psum_tensor("pbfB", [128, 1024], BF16))
    ph0 = esC2.enter_context(nc.psum_tensor("ph0", [128, 512], F32))
    ph1 = esC2.enter_context(nc.psum_tensor("ph1", [128, 512], F32))
    pz2 = [esC2.enter_context(nc.psum_tensor(f"pz2_{j}", [128, 512], F32)) for j in range(4)]
    pq8c = esC2.enter_context(nc.psum_tensor("pq8c", [128, 1024], BF16))
    if True:
        fw.w("sync", c.e_spills_c1[0], c.e_spills_c1[1],
             c.e_spills_p2[0], c.e_spills_p2[1])
        ef = NS()
        ef.u = [None] * 3
        ef.x1r = None
        ef.ubf = None
        ef.uT = []
        ef.pbf = []
        ef.h = [[], []]
        ef.hsb = [[], []]
        ef.pz2free = []
        ef.q8 = []
        ef.pq8 = []
        ef.wslot = [None, None]
        ef.wout = [None, None]
        ef.lnx2 = None
        ef.x2 = None
        ef.w = None
        NB = NT // 2

        def loads(j):
            e_us = []
            for s in range(2):
                i = 2 * j + s
                sl = i % 3
                fw.w("sync", ef.u[sl])
                e_us.append(fw.dma("sync", urd[:, sl, :], dus[i], f"c2u{sl}"))
            return e_us

        def uTstage(j, e_us):
            e_uTs = []
            for s in range(2):
                i = 2 * j + s
                sl = i % 3
                fw.w("vector", e_us[s], ef.ubf)
                e_ubf = fw.v(lambda E, sl=sl: E.tensor_copy(ubf[:], urd[:, sl, :]))
                fw.w("tensor", e_ubf, *ef.pbf)
                for kc in range(KC):
                    fw.tr(pbfB[:, kc * 128:(kc + 1) * 128], ubf[:, kc * 128:(kc + 1) * 128], c.ident[:])
                e_uTmm = fw.mark("tensor")
                ef.ubf = e_uTmm
                fw.w("scalar", e_uTmm, *ef.uT)
                e_uT = fw.s(lambda E, s=s: E.copy(uT2[:, :, s * 128:(s + 1) * 128],
                                                  pbfB[:].rearrange("p (a b) -> p a b", a=KC)))
                e_uTs.append(e_uT)
                ef.pbf = [e_uT]
            ef.uT = e_uTs
            return e_uTs

        def ff(j, e_uTs):
            pend = None
            for fc in range(NFC):
                pb_ = ph0 if fc % 2 == 0 else ph1
                off = ((fc // 2) % 2) * 256
                fw.w("tensor", e_uTs[1], e_w1T[-1], *ef.h[fc % 2])
                if fc == 0:
                    fw.w("tensor", *ef.pz2free, e_w2T[-1])
                for kc in range(KC):
                    fw.mm(pb_[:, off:off + 256], c.W1T[:, kc, fc * 128:(fc + 1) * 128],
                          uT2[:, kc, :], start=(kc == 0), stop=(kc == KC - 1))
                e_hmm = fw.mark("tensor")
                slot = fc % 2
                if fc % 2 == 0:
                    fw.w("scalar", e_hmm, c.e_b1, *ef.hsb[slot])
                    e_hev = fw.s(lambda E, slot=slot, off=off, pb_=pb_, fc=fc: E.activation(
                        hsb[:, slot, :], pb_[:, off:off + 256], AF.Relu,
                        bias=c.b1_fm[:, fc:fc + 1]))
                else:
                    fw.w("vector", e_hmm, c.e_b1, *ef.hsb[slot])
                    e_hev = fw.v(lambda E, slot=slot, off=off, pb_=pb_, fc=fc: E.tensor_scalar(
                        hsb[:, slot, :], pb_[:, off:off + 256],
                        c.b1_fm[:, fc:fc + 1], 0.0, ALU.add, ALU.max))
                ef.h[fc % 2] = [e_hev]
                if pend is not None:
                    pfc, pslot, pev = pend
                    fw.w("tensor", pev)
                    for s in range(2):
                        for hh in range(2):
                            fw.mm(pz2[2 * s + hh][:], hsb[:, pslot, s * 128:(s + 1) * 128],
                                  c.W2T[:, pfc, hh * 512:(hh + 1) * 512],
                                  start=(pfc == 0), stop=False)
                    ef.hsb[pslot] = [fw.mark("tensor")]
                pend = (fc, slot, e_hev)
            pfc, pslot, pev = pend
            fw.w("tensor", pev)
            for s in range(2):
                for hh in range(2):
                    fw.mm(pz2[2 * s + hh][:], hsb[:, pslot, s * 128:(s + 1) * 128],
                          c.W2T[:, pfc, hh * 512:(hh + 1) * 512],
                          start=False, stop=True)
            ef.hsb[pslot] = [fw.mark("tensor")]
            return ef.hsb[pslot][0]

        def w_builds(j, e_z2_all, e_us):
            evs = []
            for s in range(2):
                i = 2 * j + s
                sl = i % 3
                fw.w("vector", e_z2_all, ef.wslot[s], c.e_bc[7], e_us[s])
                e_w1 = fw.v(lambda E, s=s, sl=sl: E.tensor_tensor(
                    wful2[:, s, 0:512], pz2[2 * s][:], urd[:, sl, 0:512], ALU.add))
                e_w2 = fw.v(lambda E, s=s, sl=sl: E.tensor_tensor(
                    wful2[:, s, 512:1024], pz2[2 * s + 1][:], urd[:, sl, 512:1024], ALU.add))
                fw.w("vector", e_w1, e_w2)
                e_w3 = fw.v(lambda E, s=s: E.tensor_tensor(
                    wful2[:, s, :], wful2[:, s, :], c.bcastB[:, 3, :], ALU.add))
                ef.u[sl] = e_w3
                evs.extend([e_w1, e_w2])
                ef.wout[s] = e_w3
            ef.pz2free = evs
            return evs

        def tail(j):
            for s in range(2):
                i = 2 * j + s
                wv = wful2[:, s, :]
                fw.w("sync", ef.x1r)
                e_x1 = fw.dma("sync", x1rd2[:], dx1[i], "c2x1")
                mu, r, e_mu, e_r = ln_stats(fw, c, wv, 0, [ef.wout[s]])
                fw.w("vector", e_mu, e_r, ef.x2, c.e_bc[4], c.e_bc[5], e_x1)
                e_x2a = fw.v(lambda E, mu=mu, r=r, wv=wv: E.tensor_scalar(
                    x2f[:], wv, mu, r, ALU.subtract, ALU.mult))
                fw.w("vector", e_x2a)
                e_x2b = fw.v(lambda E: E.tensor_tensor(x2f[:], x2f[:], c.bcastB[:, 0, :], ALU.mult))
                fw.w("vector", e_x2b)
                e_x2c = fw.v(lambda E: E.tensor_tensor(x2f[:], x2f[:], c.bcastB[:, 1, :], ALU.add))
                fw.w("vector", e_x2c)
                e_x2 = fw.v(lambda E: E.tensor_tensor(x2f[:], x2f[:], x1rd2[:], ALU.add))
                ef.x1r = e_x2
                mu2_, r2_, e_mu2, e_r2 = ln_stats(fw, c, x2f[:], 6, [e_x2])
                fw.w("vector", e_mu2, e_r2, ef.lnx2)
                e_ln = fw.v(lambda E, mu2_=mu2_, r2_=r2_, wv=wv: E.tensor_scalar(
                    wv, x2f[:], mu2_, r2_, ALU.subtract, ALU.mult))
                ef.x2 = e_ln
                st = c.sstat
                am = st[:, 12:13]
                qs = st[:, 13:14]
                fw.w("vector", e_ln)
                e_am = fw.v(lambda E, wv=wv: E.tensor_reduce(am, wv, AX, ALU.max,
                                                             apply_absolute_value=True))
                fw.w("vector", e_am)
                e_am2 = fw.v(lambda E: E.tensor_single_scalar(am, am, 1e-12, ALU.max))
                fw.w("vector", e_am2)
                e_qs = fw.v(lambda E: E.reciprocal(qs, am))
                fw.w("vector", e_qs)
                e_qs2 = fw.v(lambda E: E.tensor_single_scalar(qs, qs, 224.0, ALU.mult))
                fw.w("vector", c.e_ws)
                e_s1 = fw.v(lambda E: E.tensor_scalar(st[:, 14:15], am, 1.0 / 224.0,
                                                      c.wsv[:, 0:1], ALU.mult, ALU.mult))
                fw.w("vector", e_qs2, *ef.q8)
                e_q8 = fw.v(lambda E, wv=wv: E.tensor_scalar(q8c[:], wv, qs, None, ALU.mult))
                fw.w("vector", e_q8)
                e_q8b = fw.v(lambda E: E.tensor_copy(q8cb[:], q8c[:]))
                ef.lnx2 = e_q8
                fw.w("tensor", e_q8b, *ef.pq8)
                for kc in range(KC):
                    fw.tr(pq8c[:, kc * 128:(kc + 1) * 128], q8cb[:, kc * 128:(kc + 1) * 128],
                          c.ident[:])
                e_tt8 = fw.mark("tensor")
                ef.q8 = [e_tt8]
                fw.w("scalar", e_tt8, *ef.pq8)
                e_ev8a = fw.s(lambda E: E.copy(q8cT[:, 0:512], pq8c[:, 0:512]))
                e_ev8b = fw.s(lambda E: E.copy(q8cT[:, 512:1024], pq8c[:, 512:1024]))
                fw.w("tensor", e_ev8a, e_ev8b, *(ef.h[0] + ef.h[1]))
                for kc in range(KC):
                    fw.mm(ph0[:], q8cT[:, kc * 128:(kc + 1) * 128], c.w8T[:, kc, 0:512],
                          start=(kc == 0), stop=(kc == KC - 1))
                for kc in range(KC):
                    fw.mm(ph1[:], q8cT[:, kc * 128:(kc + 1) * 128], c.w8T[:, kc, 512:1024],
                          start=(kc == 0), stop=(kc == KC - 1))
                e_zp = fw.mark("tensor")
                ef.pq8 = [e_zp]
                fw.w("vector", e_zp, e_q8b)
                e_o1 = fw.v(lambda E, s=s: E.scalar_tensor_tensor(
                    wful2[:, s, 0:512], ph0[:], st[:, 14:15],
                    c.bcastB[:, 2, 0:512], ALU.mult, ALU.add))
                e_o2 = fw.v(lambda E, s=s: E.scalar_tensor_tensor(
                    wful2[:, s, 512:1024], ph1[:], st[:, 14:15],
                    c.bcastB[:, 2, 512:1024], ALU.mult, ALU.add))
                fw.w("vector", e_o1, e_o2)
                e_o3 = fw.v(lambda E, s=s: E.tensor_tensor(wful2[:, s, :], wful2[:, s, :],
                                                           x2f[:], ALU.add))
                ef.h[0] = [e_o1]
                ef.h[1] = [e_o2]
                fw.w("gpsimd", e_o3)
                e_osp = fw.dma("gpsimd", dout[i], wful2[:, s, :], f"out{s}")
                ef.wslot[s] = e_osp
        prev = None
        for j in range(NB):
            e_us = loads(j)
            e_uTs = uTstage(j, e_us)
            e_z2_all = ff(j, e_uTs)
            if prev is not None:
                tail(prev)
            w_builds(j, e_z2_all, e_us)
            prev = j
        tail(prev)
        fw.w("gpsimd", ef.wslot[0], ef.wslot[1])
        fw.g(lambda E: E.nop())
    esC2.close()
    fw.emit()
    es.close()
    return nc


from concourse.bass_utils import run_bass_kernel_spmd

_NCORES = 8

def kernel(**inputs):
    nc = build_nc()
    video = np.asarray(inputs["L0_attn_video"])
    base = {
        "wq": np.asarray(inputs["L0_attn_Wq"]),
        "wk": np.asarray(inputs["L0_attn_Wk"]),
        "wv": np.asarray(inputs["L0_attn_Wv"]),
        "wo": np.asarray(inputs["L0_attn_Wo"]),
        "w1": np.asarray(inputs["L0_attn_W1"]),
        "w2": np.asarray(inputs["L0_attn_W2"]),
        "w8": np.asarray(inputs["L0_ffn_weight"]).view(np.uint8),
        "proj": np.asarray(inputs["L0_attn_proj"]),
        "bq": np.asarray(inputs["L0_attn_bq"]),
        "bk": np.asarray(inputs["L0_attn_bk"]),
        "bv": np.asarray(inputs["L0_attn_bv"]),
        "bo": np.asarray(inputs["L0_attn_bo"]),
        "b1": np.asarray(inputs["L0_attn_b1"]),
        "b2": np.asarray(inputs["L0_attn_b2"]),
        "g2": np.asarray(inputs["L0_attn_g2"]),
        "be2": np.asarray(inputs["L0_attn_be2"]),
        "g3": np.asarray(inputs["L0_attn_g3"]),
        "be3": np.asarray(inputs["L0_attn_be3"]),
        "ws": np.asarray(inputs["L0_ffn_weight_scale"]).reshape(1, 1),
        "b8": np.asarray(inputs["L0_ffn_bias"]),
    }
    in_maps = []
    for core in range(_NCORES):
        m = dict(base)
        m["video"] = np.ascontiguousarray(video[core]).reshape(NT, TS, D)
        in_maps.append(m)
    res = run_bass_kernel_spmd(nc, in_maps, core_ids=list(range(_NCORES)))
    out = np.stack([np.asarray(res.results[core]["out"]).reshape(T, D)
                    for core in range(_NCORES)], axis=0)
    return out


# version 24
# speedup vs baseline: 1.0191x; 1.0191x over previous
import os
import numpy as np
import concourse.bass as bass
import concourse.mybir as mybir
from contextlib import ExitStack

F32 = mybir.dt.float32
BF16 = mybir.dt.bfloat16
FP8 = mybir.dt.float8e4
U8 = mybir.dt.uint8
U32 = mybir.dt.uint32
AF = mybir.ActivationFunctionType
ALU = mybir.AluOpType
AX = mybir.AxisListType.X

T, NT, TS = 4096, 32, 128
D, KC = 1024, 8
H, DH, M = 16, 64, 256
FF, NFC = 4096, 32
EPS_LN = 1e-5
EPS_K = 1e-4
NORM = float(64.0 ** (-0.25))
C_DIAG = float(0.5 * (NORM * NORM))
NEG_BIG = -3.0e38


class Ev:
    __slots__ = ("sem", "val")
    def __init__(self, sem, val):
        self.sem, self.val = sem, val


class FW:
    ENGS = ("sync", "gpsimd", "tensor", "vector", "scalar")
    DSEM = {"sync": "syncd", "gpsimd": "gpsimdd"}

    def __init__(self, nc, es):
        self.nc = nc
        self.es = es
        self.streams = {e: [] for e in self.ENGS}
        self.sems = {}
        self.cnt = {}
        for s in self.ENGS:
            self.sems[s] = es.enter_context(nc.semaphore(f"S_{s}"))
            self.cnt[s] = 0

    def _dma_key(self, key):
        k = "d_" + key
        if k not in self.sems:
            self.sems[k] = self.es.enter_context(self.nc.semaphore(k))
            self.cnt[k] = 0
        return k

    def _rec(self, eng, build, key):
        evs = []
        if key is not None:
            step = 16 if key.startswith("d_") else 1
            self.cnt[key] += step
            evs.append(Ev(key, self.cnt[key]))
        is_dma = bool(key is not None and key.startswith("d_"))
        self.streams[eng].append(["op", build, evs, is_dma])
        return evs[0] if evs else None

    def w(self, eng, *evs):
        for ev in evs:
            if ev is not None:
                self.streams[eng].append(("wait", ev))

    def dma(self, eng, out, in_, key, slow=False):
        k = self._dma_key(key)
        if slow:
            ev = self._rec(eng, lambda E: E.dma_start(out, in_, allow_slow_non_contiguous=True), k)
        else:
            ev = self._rec(eng, lambda E: E.dma_start(out, in_), k)
        self.streams[eng][-1][3] = True
        return ev

    def dmat(self, out, in_, key):
        ev = self._rec("sync", lambda E: E.dma_start_transpose(out, in_),
                       self._dma_key(key))
        self.streams["sync"][-1][3] = True
        return ev

    def mm(self, out, lhsT, rhs, start=True, stop=True, ev=False):
        return self._rec("tensor",
                         lambda E: E.matmul(out, lhsT, rhs, start=start, stop=stop),
                         "tensor" if ev else None)

    def tr(self, out, in_, ident, ev=False):
        return self._rec("tensor", lambda E: E.transpose(out, in_, ident),
                         "tensor" if ev else None)

    def v(self, build, ev=True):
        return self._rec("vector", build, "vector" if ev else None)

    def s(self, build, ev=True):
        return self._rec("scalar", build, "scalar" if ev else None)

    def g(self, build, ev=True):
        if self.cnt["gpsimd"] > 0:
            self.w("gpsimd", Ev("gpsimd", self.cnt["gpsimd"]))
        return self._rec("gpsimd", build, "gpsimd" if ev else None)

    def mark(self, eng):
        for item in reversed(self.streams[eng]):
            if item[0] == "op":
                if item[3] or len(item[2]) >= 1:
                    key = eng
                    if self.cnt[key] > 0:
                        self.w(eng, Ev(key, self.cnt[key]))
                    self.cnt[key] += 1
                    ev = Ev(key, self.cnt[key])
                    self.streams[eng].append(["op", lambda E: E.nop(), [ev], False])
                    return ev
                key = eng
                self.cnt[key] += 1
                ev = Ev(key, self.cnt[key])
                item[2].append(ev)
                return ev
        raise RuntimeError(f"mark({eng}) with no prior op")

    def barrier(self):
        marks = {e: self.mark(e) for e in self.ENGS}
        for e in self.ENGS:
            self.w(e, *[marks[o] for o in self.ENGS if o != e])
        return marks

    def emit(self):
        with self.nc.Block() as block:
            for eng in self.ENGS:
                stream = self.streams[eng]
                sems = self.sems

                def section(E, stream=stream):
                    hi = {}
                    for item in stream:
                        if item[0] == "wait":
                            ev = item[1]
                            if hi.get(ev.sem, -1) >= ev.val:
                                continue
                            E.wait_ge(sems[ev.sem], ev.val)
                            hi[ev.sem] = ev.val
                        else:
                            _, build, evs, _k = item
                            inst = build(E)
                            for ev in evs:
                                inst.then_inc(sems[ev.sem], 16 if ev.sem.startswith("d_") else 1)

                getattr(block, eng)(section)


class NS:
    pass


def ln_stats(fw, c, x_ap, col, e_x, n=D):
    st = c.sstat
    mu = st[:, col:col + 1]
    r = st[:, col + 1:col + 2]
    t1 = st[:, col + 2:col + 3]
    t2 = st[:, col + 3:col + 4]
    ssum = st[:, col + 4:col + 5]
    ssq = st[:, col + 5:col + 6]
    fw.w("scalar", *e_x)
    e_sq = fw.s(lambda E: E.activation(c.junk[:, 0:n], x_ap, AF.Square, accum_out=ssq))
    fw.w("vector", *e_x)
    e_sm = fw.v(lambda E: E.reduce_sum(ssum, x_ap, AX))
    fw.w("vector", e_sm)
    e_mu = fw.v(lambda E: E.tensor_single_scalar(mu, ssum, 1.0 / n, ALU.mult))
    fw.w("vector", e_mu)
    e1 = fw.v(lambda E: E.tensor_tensor(t1, mu, mu, ALU.mult))
    fw.w("vector", e_sq, e1)
    e2 = fw.v(lambda E: E.tensor_scalar(t2, ssq, 1.0 / n, EPS_LN, ALU.mult, ALU.add))
    fw.w("vector", e2)
    e3 = fw.v(lambda E: E.tensor_tensor(t2, t2, t1, ALU.subtract))
    fw.w("scalar", e3)
    e5 = fw.s(lambda E: E.activation(t1, t2, AF.Ln))
    fw.w("scalar", e5)
    e_r = fw.s(lambda E: E.activation(r, t1, AF.Exp, scale=-0.5))
    return mu, r, e_mu, e_r


MK_PHASES = int(os.environ.get("MK_PHASES", "9"))


def build_nc():
    nc = bass.Bass("TRN2", target_bir_lowering=False)
    es = ExitStack()
    c = NS()

    dvideo = nc.dram_tensor("video", [NT, TS, D], F32, kind="ExternalInput")
    dwq = nc.dram_tensor("wq", [D, D], F32, kind="ExternalInput")
    dwk = nc.dram_tensor("wk", [D, D], F32, kind="ExternalInput")
    dwv = nc.dram_tensor("wv", [D, D], F32, kind="ExternalInput")
    dwo = nc.dram_tensor("wo", [D, D], F32, kind="ExternalInput")
    dw1 = nc.dram_tensor("w1", [FF, D], F32, kind="ExternalInput")
    dw2 = nc.dram_tensor("w2", [D, FF], F32, kind="ExternalInput")
    dw8 = nc.dram_tensor("w8", [D, D], U8, kind="ExternalInput")
    dproj = nc.dram_tensor("proj", [M, DH], F32, kind="ExternalInput")
    dbias = {}
    for nm, sz in [("bq", D), ("bk", D), ("bv", D), ("bo", D), ("b1", FF),
                   ("b2", D), ("g2", D), ("be2", D), ("g3", D), ("be3", D), ("b8", D)]:
        dbias[nm] = nc.dram_tensor(nm, [sz], F32, kind="ExternalInput")
    dws = nc.dram_tensor("ws", [1, 1], F32, kind="ExternalInput")
    dout = nc.dram_tensor("out", [NT, TS, D], F32, kind="ExternalOutput")

    dx1 = nc.dram_tensor("x1s", [NT, TS, D], F32)
    dus = nc.dram_tensor("us", [NT, TS, D], F32)
    dks = nc.dram_tensor("ks", [NT, 128, KC, TS], BF16)
    dvs = nc.dram_tensor("vs", [NT, TS, H, DH + 1], BF16)
    dwb = {}
    for nm, shp in [("wq", [D, D]), ("wk", [D, D]), ("wv", [D, D]), ("wo", [D, D]),
                    ("w1", [FF, D]), ("w2", [D, FF])]:
        dwb[nm] = nc.dram_tensor(nm + "bf", shp, BF16)

    sb = lambda n, s, d: es.enter_context(nc.sbuf_tensor(n, s, d))
    c.ident = sb("ident", [128, 128], BF16)
    c.identf = sb("identf", [128, 128], F32)
    c.ident8 = sb("ident8", [128, 128], FP8)
    c.projn2 = sb("projn2", [128, 2 * M], BF16)
    c.b_fm = sb("b_fm", [128, 2, KC], F32)
    c.b1_fm = sb("b1_fm", [128, NFC], F32)
    c.bcastB = sb("bcastB", [128, 4, D], F32)
    c.wsv = sb("wsv", [128, 2], F32)
    c.sstat = sb("sstat", [128, 32], F32)
    c.junk = sb("junk", [128, D], F32)
    c.st_s1 = sb("st_s1", [128, NT], F32)
    c.st_mu2 = sb("st_mu2", [128, NT], F32)
    c.st_r2 = sb("st_r2", [128, NT], F32)
    c.st_diagk = sb("st_diagk", [128, NT, H], F32)
    c.st_kmax = sb("st_kmax", [128, H], F32)
    c.stabk = sb("stabk", [128, H], F32)
    c.s1k = sb("s1k", [128, NT, H], F32)
    c.s2k = sb("s2k", [128, NT, H], F32)
    c.w8T = sb("w8T", [128, KC, D], FP8)

    esBA = ExitStack()
    c.bcastA = esBA.enter_context(nc.sbuf_tensor("bcastA", [128, 4, D], F32))
    c.ctxT = esBA.enter_context(nc.sbuf_tensor("ctxT", [128, H, 2, DH + 1], BF16))

    fw = FW(nc, es)
    PH = {}

    ev = fw.g(lambda E: E.memset(c.ident[:], 0.0))
    fw.w("gpsimd", ev)
    e_id = fw.g(lambda E: E.affine_select(out=c.ident[:], in_=c.ident[:],
                                          compare_op=ALU.not_equal, fill=1.0, base=0,
                                          pattern=[[-1, 128]], channel_multiplier=1))
    ev = fw.g(lambda E: E.memset(c.identf[:], 0.0))
    fw.w("gpsimd", ev)
    e_idf = fw.g(lambda E: E.affine_select(out=c.identf[:], in_=c.identf[:],
                                           compare_op=ALU.not_equal, fill=1.0, base=0,
                                           pattern=[[-1, 128]], channel_multiplier=1))
    ev = fw.g(lambda E: E.memset(c.ident8[:], 0.0))
    fw.w("gpsimd", ev)
    e_id8 = fw.g(lambda E: E.affine_select(out=c.ident8[:], in_=c.ident8[:],
                                           compare_op=ALU.not_equal, fill=1.0, base=0,
                                           pattern=[[-1, 128]], channel_multiplier=1))
    e_kmax0 = fw.v(lambda E: E.memset(c.st_kmax[:], NEG_BIG))

    e_bq = fw.dma("sync", c.b_fm[:, 0, :], bass.AP(dbias["bq"], 0, [[1, 128], [128, KC]]), "bq", slow=True)
    e_bk = fw.dma("sync", c.b_fm[:, 1, :], bass.AP(dbias["bk"], 0, [[1, 128], [128, KC]]), "bk", slow=True)
    e_b1 = fw.dma("sync", c.b1_fm[:], bass.AP(dbias["b1"], 0, [[1, 128], [128, NFC]]), "b1", slow=True)
    e_bc = []
    for j, nm in enumerate(["g2", "be2", "bv", "bo", "g3", "be3", "b8", "b2"]):
        tgt = c.bcastA[:, j, :] if j < 4 else c.bcastB[:, j - 4, :]
        e_bc.append(fw.dma("sync", tgt,
                           bass.AP(dbias[nm], 0, [[0, 128], [1, D]]), f"bc{j}"))
    fw.w("vector", e_bc[6])
    e_hb8 = fw.v(lambda E: E.tensor_single_scalar(c.bcastB[:, 2, :], c.bcastB[:, 2, :],
                                                  0.5, ALU.mult))
    e_bc[6] = e_hb8
    e_ws = fw.dma("sync", c.wsv[:, 0:1], bass.AP(dws, 0, [[0, 128], [1, 1]]), "ws")
    c.e_ws, c.e_bc, c.e_bq, c.e_bk, c.e_b1 = e_ws, e_bc, e_bq, e_bk, e_b1

    e_cast = {}
    for nm, src in [("wk", dwk), ("wv", dwv), ("wq", dwq), ("wo", dwo),
                    ("w1", dw1), ("w2", dw2)]:
        e_cast[nm] = fw.dma("gpsimd", dwb[nm][:], src[:], "cst_" + nm)

    with nc.sbuf_tensor("projr", [128, 2, DH], F32) as projr, \
         nc.sbuf_tensor("projf", [64, M], F32) as projf, \
         nc.psum_tensor("ppj", [64, M], F32) as ppj:
        e_pl = fw.dma("sync", projr[:], bass.AP(dproj, 0, [[DH, 128], [128 * DH, 2], [1, DH]]), "proj")
        fw.w("tensor", e_pl, e_idf)
        e_t0 = fw.tr(ppj[:, 0:128], projr[:, 0, :], c.identf[:], ev=True)
        e_t1 = fw.tr(ppj[:, 128:256], projr[:, 1, :], c.identf[:], ev=True)
        fw.w("scalar", e_t0, e_t1)
        e_pf = fw.s(lambda E: E.copy(projf[:], ppj[:]))
        fw.w("vector", e_pf)
        e_pz = fw.v(lambda E: E.memset(c.projn2[:], 0.0))
        fw.w("vector", e_pz)
        e_pn0 = fw.v(lambda E: E.tensor_single_scalar(c.projn2[0:64, 0:M], projf[:], NORM, ALU.mult))
        e_pn1 = fw.v(lambda E: E.tensor_single_scalar(c.projn2[64:128, M:2 * M], projf[:], NORM, ALU.mult))

        with nc.sbuf_tensor("w8u", [128, KC, D], U8) as w8u, \
             nc.sbuf_tensor("w8h", [128, KC, D], FP8) as w8h, \
             nc.sbuf_tensor("w8hb", [128, KC, D], BF16) as w8hb, \
             nc.sbuf_tensor("du0", [128, 1024], U32) as du0, \
             nc.sbuf_tensor("du1", [128, 1024], U32) as du1, \
             nc.sbuf_tensor("du2", [128, 1024], U32) as du2, \
             nc.sbuf_tensor("df0", [128, 1024], F32) as df0, \
             nc.sbuf_tensor("df1", [128, 1024], F32) as df1:
            e_w8 = fw.dma("sync", w8u[:], bass.AP(dw8, 0, [[D, 128], [128 * D, KC], [1, D]]), "w8")

            e_dec = []
            for gi in range(KC):
                fw.w("vector", e_w8)
                ev = fw.v(lambda E, gi=gi: E.tensor_copy(du0[:], w8u[:, gi, :]))
                fw.w("vector", ev)
                ev = fw.v(lambda E: E.tensor_single_scalar(du2[:], du0[:], 7,
                                                           ALU.logical_shift_right))
                fw.w("vector", ev)
                ev = fw.v(lambda E: E.tensor_scalar(df0[:], du2[:], -2.0, 1.0,
                                                    ALU.mult, ALU.add))
                e_sign = ev
                ev = fw.v(lambda E: E.tensor_single_scalar(du1[:], du0[:], 0x7F, ALU.bitwise_and))
                fw.w("vector", ev)
                ev = fw.v(lambda E: E.tensor_single_scalar(du2[:], du1[:], 7, ALU.bitwise_and))
                e_m = ev
                ev = fw.v(lambda E: E.tensor_single_scalar(du0[:], du1[:], 8, ALU.is_ge))
                fw.w("vector", ev)
                ev = fw.v(lambda E: E.tensor_single_scalar(du0[:], du0[:], 3, ALU.logical_shift_left))
                fw.w("vector", ev, e_m)
                ev = fw.v(lambda E: E.tensor_tensor(du2[:], du2[:], du0[:], ALU.add))
                fw.w("vector", ev)
                e_af = fw.v(lambda E: E.tensor_copy(df1[:], du2[:]))
                ev = fw.v(lambda E: E.tensor_single_scalar(du1[:], du1[:], 3, ALU.logical_shift_right))
                fw.w("vector", ev)
                ev = fw.v(lambda E: E.tensor_single_scalar(du1[:], du1[:], 1, ALU.max))
                fw.w("vector", ev)
                ev = fw.v(lambda E: E.tensor_single_scalar(du1[:], du1[:], 117, ALU.add))
                fw.w("vector", ev)
                ev = fw.v(lambda E: E.tensor_single_scalar(du1[:], du1[:], 23, ALU.logical_shift_left))
                fw.w("vector", ev, e_af)
                ev = fw.v(lambda E: E.tensor_tensor(df1[:], df1[:], du1.ap().bitcast(F32), ALU.mult))
                fw.w("vector", ev, e_sign)
                ev = fw.v(lambda E: E.tensor_tensor(df1[:], df1[:], df0[:], ALU.mult))
                fw.w("vector", ev)
                ev = fw.v(lambda E, gi=gi: E.tensor_scalar(w8h[:, gi, :], df1[:], 0.5, None, ALU.mult))
                fw.w("vector", ev)
                ev = fw.v(lambda E, gi=gi: E.tensor_copy(w8hb[:, gi, :], w8h[:, gi, :]))
                e_dec.append(ev)
            with nc.psum_tensor("p8t", [128, 512], BF16) as p8t:
                prev = []
                for kc in range(KC):
                    for half in range(2):
                        for j in range(4):
                            dc = half * 4 + j
                            fw.w("tensor", e_dec[dc], e_id, *prev)
                            fw.tr(p8t[:, j * 128:(j + 1) * 128],
                                  w8hb[:, dc, kc * 128:(kc + 1) * 128], c.ident[:])
                        e_tt = fw.mark("tensor")
                        fw.w("scalar", e_tt)
                        prev = [fw.s(lambda E, kc=kc, half=half:
                                     E.copy(c.w8T[:, kc, half * 512:(half + 1) * 512], p8t[:]))]
                e_w8T = prev[0]

    c.e_id8 = e_id8
    PH["setup"] = fw.barrier()

    esP12 = ExitStack()
    c.q8T = esP12.enter_context(nc.sbuf_tensor("q8T", [128, 2, KC, TS], FP8))
    c.WkT = esP12.enter_context(nc.sbuf_tensor("WkT", [128, KC, D], BF16))
    c.WvT = esP12.enter_context(nc.sbuf_tensor("WvT", [128, KC, D], BF16))
    e_wkT, e_wvT = [], []
    for kc in range(KC):
        fw.w("sync", e_cast["wk"], e_cast["wv"])
        e_wkT.append(fw.dmat(c.WkT[:, kc, :], dwb["wk"][:, kc * 128:(kc + 1) * 128], "xb_wk"))
        e_wvT.append(fw.dmat(c.WvT[:, kc, :], dwb["wv"][:, kc * 128:(kc + 1) * 128], "xb_wv"))

    def ffn8_quant(i, vbuf, e_v, statcol, s1_dst, q8T_dst, pq8, e_pq8_free):
        mu, r, e_mu, e_r = ln_stats(fw, c, vbuf, statcol, [e_v])
        fw.w("vector", e_mu, e_r)
        e_ln = fw.v(lambda E: E.tensor_scalar(c.lnx[:], vbuf, mu, r, ALU.subtract, ALU.mult))
        st = c.sstat
        am = st[:, 12:13]
        qs = st[:, 13:14]
        fw.w("vector", e_ln)
        e_am = fw.v(lambda E: E.tensor_reduce(am, c.lnx[:], AX, ALU.max, apply_absolute_value=True))
        fw.w("vector", e_am)
        e_am2 = fw.v(lambda E: E.tensor_single_scalar(am, am, 1e-12, ALU.max))
        fw.w("vector", e_am2)
        e_qs = fw.v(lambda E: E.reciprocal(qs, am))
        fw.w("vector", e_qs)
        e_qs2 = fw.v(lambda E: E.tensor_single_scalar(qs, qs, 224.0, ALU.mult))
        fw.w("vector", c.e_ws)
        e_s1 = fw.v(lambda E: E.tensor_scalar(s1_dst, am, 1.0 / 224.0, c.wsv[:, 0:1],
                                              ALU.mult, ALU.mult))
        fw.w("vector", e_qs2)
        e_q8 = fw.v(lambda E: E.tensor_scalar(c.q8tok[:], c.lnx[:], qs, None, ALU.mult))
        fw.w("vector", e_q8)
        e_q8b = fw.v(lambda E: E.tensor_copy(c.q8bf[:], c.q8tok[:]))
        return e_q8b, e_ln

    def ffn8_trans(e_q8b, q8T_dst, pq8, e_pq8_free):
        fw.w("tensor", e_q8b, e_id, *e_pq8_free)
        for kc in range(KC):
            fw.tr(pq8[:, kc * 128:(kc + 1) * 128], c.q8bf[:, kc * 128:(kc + 1) * 128], c.ident[:])
        e_tt = fw.mark("tensor")
        fw.w("scalar", e_tt)
        e_ev1 = fw.s(lambda E, q8T_dst=q8T_dst, pq8=pq8: E.copy(
            q8T_dst[:, 0:4, :].rearrange("p a b -> p (a b)"), pq8[:, 0:512]))
        e_ev2 = fw.s(lambda E, q8T_dst=q8T_dst, pq8=pq8: E.copy(
            q8T_dst[:, 4:8, :].rearrange("p a b -> p (a b)"), pq8[:, 512:1024]))
        return [e_ev1, e_ev2], e_tt

    c.lnx = esP12.enter_context(nc.sbuf_tensor("lnx", [128, D], F32))
    c.q8tok = esP12.enter_context(nc.sbuf_tensor("q8tok", [128, D], FP8))
    c.q8bf = esP12.enter_context(nc.sbuf_tensor("q8bf", [128, D], BF16))
    PH["p1"] = fw.barrier()

    if MK_PHASES <= 1:
        fw.emit(); esP12.close(); esBA.close(); es.close(); return nc

    with nc.sbuf_tensor("vid2", [128, 2, D], F32) as vid, \
         nc.sbuf_tensor("x1st", [128, 2, D], F32) as x1st, \
         nc.sbuf_tensor("af", [128, 2, D], F32) as af, \
         nc.sbuf_tensor("abf", [128, 2, D], BF16) as abf, \
         nc.sbuf_tensor("aT", [128, 2, KC, TS], BF16) as aT, \
         nc.sbuf_tensor("kst", [128, 2, KC, TS], BF16) as kst, \
         nc.sbuf_tensor("vst", [128, 2, H, DH + 1], BF16) as vst, \
         nc.sbuf_tensor("ksq", [128, D], F32) as ksq, \
         nc.sbuf_tensor("tmp5", [128, 512], F32) as tmp5, \
         nc.psum_tensor("pz0", [128, 512], F32) as pz0, \
         nc.psum_tensor("pz1", [128, 512], F32) as pz1, \
         nc.psum_tensor("pbf", [128, 1024], BF16) as pbf, \
         nc.psum_tensor("pbf2", [128, 1024], BF16) as pbf2, \
         nc.psum_tensor("pk0", [128, 512], F32) as pk0, \
         nc.psum_tensor("pk1", [128, 512], F32) as pk1, \
         nc.psum_tensor("pxk0", [128, 512], F32) as pxk0, \
         nc.psum_tensor("pxk1", [128, 512], F32) as pxk1:
        e_ones0 = fw.v(lambda E: E.memset(vst[:, 0, :, 64:65], 1.0))
        e_ones1 = fw.v(lambda E: E.memset(vst[:, 1, :, 64:65], 1.0))
        ef = NS()
        ef.vid = [None, None]
        ef.x1 = [None, None]
        ef.x1read = [None, None]
        ef.afp = [None, None]
        ef.abfp = [None, None]
        ef.pbf_r = []
        ef.pbf2_r = []
        ef.pq8 = []
        ef.zq = [[], []]
        ef.lnx = None
        ef.q8con = []
        ef.k = [[], []]
        ef.v = [e_ones0, e_ones1]
        ef.z = []
        ef.kmm = []
        ef.vmm = []
        ef.aT_r = [[], []]
        ef.ksq_r = None
        ef.xkmm = [[], []]
        ef.xkred = None
        ef.kst_w = [[], []]

        def quantV(i):
            b = i % 2
            fw.w("sync", ef.vid[b])
            e_v = fw.dma("sync", vid[:, b, :], dvideo[i], f"vid2_{b}")
            fw.w("vector", e_v, ef.lnx, *ef.q8con)
            e_q8b, e_ln = ffn8_quant(i, vid[:, b, :], e_v, 0, c.st_s1[:, i:i + 1],
                                     None, None, None)
            ef.lnx = e_ln
            return e_q8b

        def stageA(i, e_q8b):
            b = i % 2
            evs, _e_tt = ffn8_trans(e_q8b, c.q8T[:, b, :, :], pbf, ef.pq8 + ef.zq[b] + ef.pbf_r)
            ef.q8con = [_e_tt]
            ef.pq8 = evs
            ef.pbf_r = [_e_tt]
            fw.w("tensor", e_w8T, *evs, *ef.z)
            for kc in range(KC):
                fw.mm(pz0[:], c.q8T[:, b, kc, :], c.w8T[:, kc, 0:512],
                      start=(kc == 0), stop=(kc == KC - 1))
            for kc in range(KC):
                fw.mm(pz1[:], c.q8T[:, b, kc, :], c.w8T[:, kc, 512:1024],
                      start=(kc == 0), stop=(kc == KC - 1))
            e_zmm = fw.mark("tensor")
            ef.zq[b] = [e_zmm]
            s1c = c.st_s1[:, i:i + 1]
            fw.w("vector", e_zmm, ef.lnx, ef.x1[b], ef.x1read[b], c.e_bc[6], c.e_bc[2])
            e1 = fw.v(lambda E, s1c=s1c: E.scalar_tensor_tensor(tmp5[:], pz0[:], s1c, c.bcastB[:, 2, 0:512],
                                                       ALU.mult, ALU.add))
            fw.w("vector", e1)
            e2 = fw.v(lambda E, b=b: E.tensor_tensor(x1st[:, b, 0:512], tmp5[:], vid[:, b, 0:512], ALU.add))
            fw.w("vector", e2)
            e3 = fw.v(lambda E, s1c=s1c: E.scalar_tensor_tensor(tmp5[:], pz1[:], s1c, c.bcastB[:, 2, 512:1024],
                                                       ALU.mult, ALU.add))
            fw.w("vector", e3)
            e4 = fw.v(lambda E, b=b: E.tensor_tensor(x1st[:, b, 512:1024], tmp5[:], vid[:, b, 512:1024], ALU.add))
            ef.z = [e2, e4]
            ef.vid[b] = e4
            e_x1 = [e2, e4]
            fw.w("gpsimd", *e_x1)
            e_sp = fw.dma("gpsimd", dx1[i], x1st[:, b, :], f"spx1_{b}")
            ef.x1[b] = e_sp
            mu, r, e_mu, e_r = ln_stats(fw, c, x1st[:, b, :], 6, e_x1)
            fw.w("vector", e_mu, e_r, ef.afp[b])
            e_m2 = fw.v(lambda E, i=i, mu=mu: E.tensor_copy(c.st_mu2[:, i:i + 1], mu))
            e_r2 = fw.v(lambda E, i=i, r=r: E.tensor_copy(c.st_r2[:, i:i + 1], r))
            e_af = fw.v(lambda E, b=b, mu=mu, r=r: E.tensor_scalar(af[:, b, :], x1st[:, b, :], mu, r, ALU.subtract, ALU.mult))
            ef.x1read[b] = e_af
            fw.w("vector", e_af, ef.abfp[b])
            e_abf = fw.v(lambda E, b=b: E.tensor_copy(abf[:, b, :], af[:, b, :]))
            ef.afp[b] = e_abf
            fw.w("tensor", e_abf, *ef.pbf_r)
            for kc in range(KC):
                fw.tr(pbf[:, kc * 128:(kc + 1) * 128], abf[:, b, kc * 128:(kc + 1) * 128], c.ident[:])
            e_aTmm = fw.mark("tensor")
            ef.abfp[b] = e_aTmm
            fw.w("scalar", e_aTmm, *ef.aT_r[b])
            e_aT = fw.s(lambda E, b=b: E.copy(aT[:, b, :, :].rearrange("p a b -> p (a b)"), pbf[:]))
            ef.pbf_r = [e_aT]
            return e_aT

        def stageB(i, e_aT):
            b = i % 2
            e_kev = []
            for half in range(2):
                pk = pk0 if half == 0 else pk1
                fw.w("tensor", e_aT, e_wkT[-1], *ef.kmm, *ef.vmm)
                for j in range(4):
                    dc = half * 4 + j
                    for kc in range(KC):
                        fw.mm(pk[:, j * 128:(j + 1) * 128],
                              c.WkT[:, kc, dc * 128:(dc + 1) * 128], aT[:, b, kc, :],
                              start=(kc == 0), stop=(kc == KC - 1))
                e_kmm = fw.mark("tensor")
                fw.w("scalar", e_kmm, c.e_bk, *ef.k[b], *ef.kst_w[b])
                for j in range(4):
                    dc = half * 4 + j
                    e_kev.append(fw.s(lambda E, b=b, dc=dc, j=j, pk=pk:
                                      E.activation(kst[:, b, dc, :], pk[:, j * 128:(j + 1) * 128],
                                                   AF.Identity, bias=c.b_fm[:, 1, dc:dc + 1])))
                ef.kmm = e_kev[-4:]
            ef.kst_w[b] = e_kev
            fw.w("gpsimd", *e_kev)
            e_ksp = fw.dma("gpsimd", dks[i], kst[:, b, :, :], f"spk_{b}")
            fw.w("tensor", e_kev[-1], e_wvT[-1], *ef.vmm)
            for kc in range(KC):
                fw.mm(pk0[:], aT[:, b, kc, :], c.WvT[:, kc, 0:512],
                      start=(kc == 0), stop=(kc == KC - 1))
            for kc in range(KC):
                fw.mm(pk1[:], aT[:, b, kc, :], c.WvT[:, kc, 512:1024],
                      start=(kc == 0), stop=(kc == KC - 1))
            e_vmm = fw.mark("tensor")
            ef.aT_r[b] = [e_vmm]
            fw.w("vector", e_vmm, ef.v[b])
            e_v0 = fw.v(lambda E, b=b: E.tensor_tensor(
                vst[:, b, 0:8, 0:64], pk0[:].rearrange("p (h d) -> p h d", h=8),
                c.bcastA[:, 2, 0:512].rearrange("p (h d) -> p h d", h=8), ALU.add))
            e_v1 = fw.v(lambda E, b=b: E.tensor_tensor(
                vst[:, b, 8:16, 0:64], pk1[:].rearrange("p (h d) -> p h d", h=8),
                c.bcastA[:, 2, 512:1024].rearrange("p (h d) -> p h d", h=8), ALU.add))
            ef.vmm = [e_v0, e_v1]
            fw.w("gpsimd", e_v0, e_v1)
            e_vsp = fw.dma("gpsimd", dvs[i], vst[:, b, :, :], f"spv_{b}")
            ef.v[b] = e_vsp
            fw.w("tensor", e_kev[-1], *ef.pbf2_r)
            for kc in range(KC):
                fw.tr(pbf2[:, kc * 128:(kc + 1) * 128], kst[:, b, kc, :], c.ident[:])
            e_ktt = fw.mark("tensor")
            fw.w("scalar", e_ktt, ef.ksq_r)
            e_ksq = fw.s(lambda E: E.activation(ksq[:], pbf2[:], AF.Square))
            ef.pbf2_r = [e_ksq]
            fw.w("vector", e_ksq)
            e_dk = fw.v(lambda E, i=i: E.tensor_reduce(
                c.st_diagk[:, i, :, None], ksq[:].rearrange("p (h d) -> p h d", h=H), AX, ALU.add))
            ef.ksq_r = e_dk
            for hp in range(KC):
                pb_ = pxk0 if hp % 2 == 0 else pxk1
                fw.w("tensor", e_kev[hp], *ef.xkmm[hp % 2])
                fw.mm(pb_[:, 0:512], kst[:, b, hp, :], c.projn2[:])
                e_xk = fw.mark("tensor")
                fw.w("vector", e_xk, ef.xkred)
                e_red = fw.v(lambda E, pb_=pb_: E.tensor_reduce(
                    c.sstat[:, 14:16, None], pb_[:].rearrange("p (h m) -> p h m", h=2), AX, ALU.max))
                fw.w("vector", e_red)
                e_max = fw.v(lambda E, hp=hp: E.tensor_tensor(
                    c.st_kmax[:, 2 * hp:2 * hp + 2], c.st_kmax[:, 2 * hp:2 * hp + 2],
                    c.sstat[:, 14:16], ALU.max))
                ef.xkmm[hp % 2] = [e_max]
                ef.xkred = e_max
            e_xk_all = fw.mark("tensor")
            ef.k[b] = [e_ksp, e_xk_all]

        prevA = None
        for i in range(NT):
            e_q8b = quantV(i)
            if prevA is not None:
                stageB(i - 1, prevA)
            prevA = stageA(i, e_q8b)
        stageB(NT - 1, prevA)
        c.e_spills_p2 = [ef.x1[0], ef.x1[1], ef.k[0][0], ef.k[1][0], ef.v[0], ef.v[1]]
        PH["p2"] = fw.barrier()
    esP12.close()
    if MK_PHASES <= 2:
        fw.emit(); esBA.close(); es.close(); return nc

    dstab = nc.dram_tensor("stabs", [H, 1], F32)
    esP3 = ExitStack()
    c.ctx_acc = esP3.enter_context(nc.sbuf_tensor("ctx_acc", [65, KC, 2 * M], F32))
    krd = esP3.enter_context(nc.sbuf_tensor("krd", [128, 2, KC, TS], BF16))
    vrd = esP3.enter_context(nc.sbuf_tensor("vrd", [128, 2, H, DH + 1], BF16))
    E1k = esP3.enter_context(nc.sbuf_tensor("E1k", [128, H, M], F32))
    kp = esP3.enter_context(nc.sbuf_tensor("kp", [128, 2, H, M], BF16))
    stt16 = esP3.enter_context(nc.sbuf_tensor("stt16", [16, 2], F32))
    pxk0 = esP3.enter_context(nc.psum_tensor("pxr0", [128, 512], F32))
    pxk1 = esP3.enter_context(nc.psum_tensor("pxr1", [128, 512], F32))
    pcx0 = esP3.enter_context(nc.psum_tensor("pcx0", [65, 512], F32))
    pcx1 = esP3.enter_context(nc.psum_tensor("pcx1", [65, 512], F32))
    pst = esP3.enter_context(nc.psum_tensor("pst", [16, 128], F32))
    if True:
        e_ctx0 = fw.v(lambda E: E.memset(c.ctx_acc[:], 0.0))
        fw.w("vector", *c.e_spills_p2)
        fw.w("tensor", *c.e_spills_p2)
        fw.w("gpsimd", *c.e_spills_p2)
        fw.w("scalar", *c.e_spills_p2)
        fw.w("tensor", PH["p2"]["vector"], e_idf)
        e_t = fw.tr(pst[:], c.st_kmax[:], c.identf[:], ev=True)
        fw.w("vector", e_t)
        e_red = fw.v(lambda E: E.tensor_reduce(stt16[:, 0:1], pst[:], AX, ALU.max))
        fw.w("gpsimd", e_red)
        e_st1 = fw.dma("gpsimd", dstab[:], stt16[:, 0:1], "stab", slow=True)
        fw.w("sync", e_st1)
        e_st2 = fw.dma("sync", c.stabk[:], bass.AP(dstab, 0, [[0, 128], [1, H]]), "stabb")
        fw.w("vector", e_st2, PH["p2"]["scalar"])
        s1kf = c.s1k.ap().rearrange("p a b -> p (a b)")
        s2kf = c.s2k.ap().rearrange("p a b -> p (a b)")
        e_a = fw.v(lambda E: E.tensor_single_scalar(s1kf, c.st_diagk.ap().rearrange("p a b -> p (a b)"),
                                                    -1.0 / 16.0, ALU.mult))
        fw.w("vector", e_a)
        stabk_b = bass.AP(c.stabk.ap().tensor, 0, [[c.stabk.ap().ap[0][0], 128], [0, NT], [1, H]])
        e_b = fw.v(lambda E: E.tensor_tensor(c.s1k[:], c.s1k[:], stabk_b, ALU.subtract))
        fw.w("scalar", e_b)
        e_c = fw.s(lambda E: E.activation(s1kf, s1kf, AF.Exp))
        fw.w("vector", e_c)
        e_d = fw.v(lambda E: E.reciprocal(s2kf, s1kf))
        fw.w("vector", e_d)
        e_s2k = fw.v(lambda E: E.tensor_single_scalar(s2kf, s2kf, EPS_K, ALU.mult))

        fw.w("sync", *c.e_spills_p2)
        ef = NS()
        ef.kb = [None, None]
        ef.vb = [None, None]
        ef.xk = [[], []]
        ef.E1 = []
        ef.kp_r = [[], []]
        ef.cx = [[], []]
        ctxf = c.ctx_acc

        def stageA(i):
            b = i % 2
            fw.w("sync", ef.kb[b])
            e_k = fw.dma("sync", krd[:, b, :, :], dks[i], f"ldk{b}")
            fw.w("sync", ef.vb[b])
            e_v = fw.dma("sync", vrd[:, b, :, :], dvs[i], f"ldv{b}")
            e_E1s = []
            for hp in range(KC):
                pb_ = pxk0 if hp % 2 == 0 else pxk1
                fw.w("tensor", e_k, *ef.xk[hp % 2])
                fw.mm(pb_[:, 0:512], krd[:, b, hp, :], c.projn2[:])
                e_xk = fw.mark("tensor")
                fw.w("scalar", e_xk, *ef.kp_r[1 - b][2 * hp:2 * hp + 2])
                e_E1 = fw.s(lambda E, hp=hp, pb_=pb_: E.activation(
                    E1k[:, 2 * hp:2 * hp + 2, :].rearrange("p a b -> p (a b)"), pb_[:], AF.Exp))
                ef.xk[hp % 2] = [e_E1]
                e_E1s.append(e_E1)
            e_kps = []
            for h in range(H):
                fw.w("vector", e_E1s[h // 2], e_s2k)
                e_kp = fw.v(lambda E, h=h, i=i, b=b: E.tensor_scalar(
                    kp[:, b, h, :], E1k[:, h, :], c.s2k[:, i, h:h + 1], c.s1k[:, i, h:h + 1],
                    ALU.add, ALU.mult))
                e_kps.append(e_kp)
            ef.kp_r[b] = e_kps
            ef.kb[b] = e_E1s[-1]
            return e_v, e_kps

        def stageB(i, e_v, e_kps):
            b = i % 2
            e_adds = []
            for j in range(KC):
                pb_ = pcx0 if j % 2 == 0 else pcx1
                fw.w("tensor", e_v, e_kps[2 * j], e_kps[2 * j + 1], *ef.cx[j % 2])
                fw.mm(pb_[:, 0:256], vrd[:, b, 2 * j, :], kp[:, b, 2 * j, :])
                fw.mm(pb_[:, 256:512], vrd[:, b, 2 * j + 1, :], kp[:, b, 2 * j + 1, :])
                e_cmm = fw.mark("tensor")
                fw.w("vector", e_cmm, e_ctx0)
                e_add = fw.v(lambda E, j=j, pb_=pb_: E.tensor_tensor(
                    ctxf[:, j, :], ctxf[:, j, :], pb_[:], ALU.add))
                ef.cx[j % 2] = [e_add]
                e_adds.append(e_add)
            ef.vb[b] = e_adds[-1]

        prevB = None
        for i in range(NT):
            av = stageA(i)
            if prevB is not None:
                stageB(i - 1, *prevB)
            prevB = av
        stageB(NT - 1, *prevB)
        with nc.psum_tensor("pct", [128, 144], F32) as pct:
            prev = []
            for j in range(KC):
                for half in range(2):
                    fw.w("tensor", ef.cx[0][0] if ef.cx[0] else None,
                         ef.cx[1][0] if ef.cx[1] else None, *prev)
                    fw.tr(pct[:, 0:65], ctxf[0:65, j, half * 256:half * 256 + 128], c.identf[0:65, 0:65])
                    fw.tr(pct[:, 72:137], ctxf[0:65, j, half * 256 + 128:half * 256 + 256], c.identf[0:65, 0:65])
                    e_t = fw.mark("tensor")
                    fw.w("scalar", e_t)
                    e_e1 = fw.s(lambda E, j=j, half=half: E.copy(
                        c.ctxT[:, 2 * j + half, 0, :], pct[:, 0:65]))
                    e_e2 = fw.s(lambda E, j=j, half=half: E.copy(
                        c.ctxT[:, 2 * j + half, 1, :], pct[:, 72:137]))
                    prev = [e_e1, e_e2]
            e_ctxT = prev
        PH["p3"] = fw.barrier()
    esP3.close()
    if MK_PHASES <= 3:
        fw.emit(); esBA.close(); es.close(); return nc

    esC1w = ExitStack()
    c.WqT = esC1w.enter_context(nc.sbuf_tensor("WqT", [128, KC, D], BF16))
    c.WoT = esC1w.enter_context(nc.sbuf_tensor("WoT", [128, KC, D], BF16))
    c.qstat = esC1w.enter_context(nc.sbuf_tensor("qstat", [128, 64], F32))
    e_wqT, e_woT = [], []
    for kc in range(KC):
        fw.w("sync", e_cast["wq"], e_cast["wo"])
        e_wqT.append(fw.dmat(c.WqT[:, kc, :], dwb["wq"][:, kc * 128:(kc + 1) * 128], "xb_wq"))
        e_woT.append(fw.dmat(c.WoT[:, kc, :], dwb["wo"][:, kc * 128:(kc + 1) * 128], "xb_wo"))

        esC1 = ExitStack()
    x1rd = esC1.enter_context(nc.sbuf_tensor("x1rd", [128, 3, D], F32))
    afc = esC1.enter_context(nc.sbuf_tensor("afc", [128, 5, D], F32))
    abfc = esC1.enter_context(nc.sbuf_tensor("abfc", [128, 2, D], BF16))
    qsqs = esC1.enter_context(nc.sbuf_tensor("qsqs", [128, D], F32))
    aTc = esC1.enter_context(nc.sbuf_tensor("aTc", [128, 2, KC, TS], BF16))
    qfm = esC1.enter_context(nc.sbuf_tensor("qfm", [128, KC, TS], BF16))
    E1q = esC1.enter_context(nc.sbuf_tensor("E1q", [128, 16, M], F32))
    qp = esC1.enter_context(nc.sbuf_tensor("qp", [128, 3, H, M], BF16))
    qpT = esC1.enter_context(nc.sbuf_tensor("qpTs", [128, 4096], BF16))
    atok = esC1.enter_context(nc.sbuf_tensor("atok", [128, H, DH], BF16))
    apre = esC1.enter_context(nc.sbuf_tensor("apre", [128, KC, TS], BF16))
    sfull = esC1.enter_context(nc.sbuf_tensor("sfull", [128, D], F32))
    uful = esC1.enter_context(nc.sbuf_tensor("ufull", [128, 2, D], F32))
    pbfA = esC1.enter_context(nc.psum_tensor("pbfA", [128, 1024], BF16))
    pbfA2 = esC1.enter_context(nc.psum_tensor("pbfA2", [128, 1024], BF16))
    pq0 = esC1.enter_context(nc.psum_tensor("pq0", [128, 512], F32))
    pq1 = esC1.enter_context(nc.psum_tensor("pq1", [128, 512], F32))
    pxq0 = esC1.enter_context(nc.psum_tensor("pxq0", [128, 512], F32))
    pxq1 = esC1.enter_context(nc.psum_tensor("pxq1", [128, 512], F32))
    pxv0 = pxq0.ap().rearrange("p (a b) -> p a b", a=4)
    pxv1 = pxq1.ap().rearrange("p (a b) -> p a b", a=4)
    pzo0 = esC1.enter_context(nc.psum_tensor("pzo0", [128, 512], F32))
    pzo1 = esC1.enter_context(nc.psum_tensor("pzo1", [128, 512], F32))
    if True:
        fw.w("sync", c.e_spills_p2[0], c.e_spills_p2[1])
        ef = NS()
        ef.x1r = [None, None, None]
        ef.e_x1 = [None, None, None]
        ef.abfp = [None, None]
        ef.tc = 0
        ef.bk = [None, None]
        ef.aT_rd = [[], []]
        ef.pq_ev = [[], []]
        ef.zof = [None, None]
        ef.qsq = None
        ef.E1_rd = None
        ef.qpTm = [None, None, None]
        ef.xq = [[], []]
        ef.po = [[], []]
        ef.e_aT = [None, None]
        ef.qev_last = None
        ef.attn_in = []
        ef.e_zomm = None
        ef.qp_ev = [None, None, None]
        ef.sread = None
        ef.u = [None, None]

        def trbank(emit_fn, evac_fn, mm_waits=(), ev_waits=()):
            p = ef.tc % 2
            bank = pbfA if p == 0 else pbfA2
            fw.w("tensor", ef.bk[p], *mm_waits)
            emit_fn(bank)
            e_mark = fw.mark("tensor")
            fw.w("scalar", e_mark, *ev_waits)
            e_ev = evac_fn(bank)
            ef.bk[p] = e_ev
            ef.tc += 1
            return e_mark, e_ev

        def A1load(i):
            b3 = i % 3
            fw.w("sync", ef.x1r[b3])
            ef.e_x1[b3] = fw.dma("sync", x1rd[:, b3, :], dx1[i], f"c1x1{b3}")

        def A1(i):
            b = i % 2
            b3 = i % 3
            ba = i % 5
            fw.w("vector", ef.e_x1[b3], PH["p3"]["scalar"])
            e_af = fw.v(lambda E, b3=b3, ba=ba, i=i: E.tensor_scalar(
                afc[:, ba, :], x1rd[:, b3, :], c.st_mu2[:, i:i + 1], c.st_r2[:, i:i + 1],
                ALU.subtract, ALU.mult))
            ef.x1r[b3] = e_af
            fw.w("vector", e_af, ef.abfp[b])
            e_abf = fw.v(lambda E, b=b, ba=ba: E.tensor_copy(abfc[:, b, :], afc[:, ba, :]))
            e_aTmm, e_aT = trbank(
                lambda bank, b=b: [fw.tr(bank[:, kc * 128:(kc + 1) * 128],
                                         abfc[:, b, kc * 128:(kc + 1) * 128], c.ident[:])
                                   for kc in range(KC)],
                lambda bank, b=b: fw.s(lambda E, bank=bank, b=b: E.copy(
                    aTc[:, b, :, :].rearrange("p a b -> p (a b)"), bank[:])),
                mm_waits=[e_abf], ev_waits=list(ef.aT_rd[b]))
            ef.abfp[b] = e_aTmm
            ef.e_aT[b] = e_aT

        def B1(i):
            ba = i % 3
            e_qpT = []
            for rnd in range(4):
                e_mark, e_ev = trbank(
                    lambda bank, ba=ba, rnd=rnd: [fw.tr(
                        bank[:, t * 128:(t + 1) * 128],
                        qp[:, ba, (rnd * 8 + t) // 2, ((rnd * 8 + t) % 2) * 128:((rnd * 8 + t) % 2) * 128 + 128],
                        c.ident[:]) for t in range(8)],
                    lambda bank, rnd=rnd: fw.s(lambda E, rnd=rnd, bank=bank: E.copy(
                        qpT[:, rnd * 1024:(rnd + 1) * 1024], bank[:])),
                    mm_waits=[ef.qp_ev[ba]] if rnd == 0 else [])
                e_qpT.append(e_ev)
            ef.qpTm[ba] = e_mark
            e_atoks = []
            for grp in range(4):
                pb_ = pxv0 if grp % 2 == 0 else pxv1
                fw.w("tensor", e_qpT[grp], *e_ctxT, *ef.po[grp % 2])
                for j in range(4):
                    h = grp * 4 + j
                    fw.mm(pb_[:, j, 0:65], qpT[:, h * 256:h * 256 + 128], c.ctxT[:, h, 0, :],
                          start=True, stop=False)
                    fw.mm(pb_[:, j, 0:65], qpT[:, h * 256 + 128:h * 256 + 256], c.ctxT[:, h, 1, :],
                          start=False, stop=True)
                e_omm = fw.mark("tensor")
                fw.w("vector", e_omm)
                e_rd = fw.v(lambda E, pb_=pb_, grp=grp: E.reciprocal(
                    c.sstat[:, 16 + 4 * grp:20 + 4 * grp], pb_[:, :, 64]))
                fw.w("vector", e_rd)
                e_as = []
                for j in range(4):
                    h = grp * 4 + j
                    e_a = fw.v(lambda E, h=h, j=j, grp=grp, pb_=pb_: E.tensor_scalar(
                        atok[:, h, :], pb_[:, j, 0:64], c.sstat[:, 16 + 4 * grp + j:17 + 4 * grp + j],
                        None, ALU.mult))
                    e_as.append(e_a)
                ef.po[grp % 2] = e_as
                e_atoks.extend(e_as)
            ef.attn_in = e_atoks

        def A2(i):
            b = i % 2
            e_qev = []
            e_qmm = None
            for half in range(2):
                pk = pq0 if half == 0 else pq1
                fw.w("tensor", ef.e_aT[b], e_wqT[-1], *ef.pq_ev[half])
                for j in range(4):
                    dc = half * 4 + j
                    for kc in range(KC):
                        fw.mm(pk[:, j * 128:(j + 1) * 128],
                              c.WqT[:, kc, dc * 128:(dc + 1) * 128], aTc[:, b, kc, :],
                              start=(kc == 0), stop=(kc == KC - 1))
                e_qmm = fw.mark("tensor")
                fw.w("scalar", e_qmm, c.e_bq)
                for j in range(4):
                    dc = half * 4 + j
                    e_qev.append(fw.s(lambda E, dc=dc, j=j, pk=pk: E.activation(
                        qfm[:, dc, :], pk[:, j * 128:(j + 1) * 128],
                        AF.Identity, bias=c.b_fm[:, 0, dc:dc + 1])))
            ef.aT_rd[b] = [e_qmm]
            ef.pq_ev = [e_qev[0:4], e_qev[4:8]]
            ef.qev_last = e_qev[-1]
            _, e_qsq = trbank(
                lambda bank: [fw.tr(bank[:, kc * 128:(kc + 1) * 128], qfm[:, kc, :], c.ident[:])
                              for kc in range(KC)],
                lambda bank: fw.s(lambda E, bank=bank: E.activation(qsqs[:], bank[:], AF.Square)),
                mm_waits=[e_qev[-1]], ev_waits=[ef.qsq])
            fw.w("vector", e_qsq)
            e_dq = fw.v(lambda E: E.tensor_reduce(
                c.qstat[:, 0:16, None], qsqs[:].rearrange("p (h d) -> p h d", h=H), AX, ALU.add))
            ef.qsq = e_dq
            ef.E1h = []
            for hp in range(KC):
                pb_ = pxq0 if hp % 2 == 0 else pxq1
                fw.w("tensor", e_qev[hp] if hp < 2 else None, *ef.xq[hp % 2],
                     *(ef.po[hp % 2] if hp < 2 else []))
                fw.mm(pb_[:, 0:512], qfm[:, hp, :], c.projn2[:])
                e_xq = fw.mark("tensor")
                fw.w("scalar", e_xq, ef.E1_rd if hp == 0 else None)
                e_E1 = fw.s(lambda E, hp=hp, pb_=pb_: E.activation(
                    E1q[:, 2 * hp:2 * hp + 2, :].rearrange("p a b -> p (a b)"), pb_[:], AF.Exp))
                ef.xq[hp % 2] = [e_E1]
                ef.E1h.append(e_E1)

        def B2(i):
            e_att, e_apre = trbank(
                lambda bank: [fw.tr(bank[:, kc * 128:(kc + 1) * 128],
                                    atok.ap().rearrange("p h d -> p (h d)")[:, kc * 128:(kc + 1) * 128],
                                    c.ident[:]) for kc in range(KC)],
                lambda bank: fw.s(lambda E, bank=bank: E.copy(
                    apre[:].rearrange("p a b -> p (a b)"), bank[:])),
                mm_waits=list(ef.attn_in))
            fw.w("tensor", e_apre, e_woT[-1], ef.zof[0], ef.zof[1])
            for kc in range(KC):
                fw.mm(pzo0[:], apre[:, kc, :], c.WoT[:, kc, 0:512],
                      start=(kc == 0), stop=(kc == KC - 1))
            for kc in range(KC):
                fw.mm(pzo1[:], apre[:, kc, :], c.WoT[:, kc, 512:1024],
                      start=(kc == 0), stop=(kc == KC - 1))
            ef.e_zomm = fw.mark("tensor")

        def A3(i):
            ba = i % 3
            fw.w("vector", ef.E1h[3])
            e_mx0 = fw.v(lambda E: E.tensor_reduce(c.qstat[:, 16:24, None], E1q[:, 0:8, :], AX, ALU.max))
            fw.w("vector", ef.E1h[7])
            e_mx1 = fw.v(lambda E: E.tensor_reduce(c.qstat[:, 24:32, None], E1q[:, 8:16, :], AX, ALU.max))
            fw.w("scalar", ef.qsq, e_mx1)
            e_ed = fw.s(lambda E: E.activation(
                c.qstat[:, 32:48], c.qstat[:, 0:16], AF.Exp, scale=-1.0 / 16.0))
            fw.w("vector", e_mx0, e_mx1)
            e_mr = fw.v(lambda E: E.reciprocal(c.qstat[:, 16:32], c.qstat[:, 16:32]))
            fw.w("vector", e_mr, e_ed)
            e_s1 = fw.v(lambda E: E.tensor_tensor(
                c.qstat[:, 32:48], c.qstat[:, 32:48], c.qstat[:, 16:32], ALU.mult))
            fw.w("vector", e_s1)
            e_s2 = fw.v(lambda E: E.reciprocal(c.qstat[:, 48:64], c.qstat[:, 32:48]))
            fw.w("vector", e_s2)
            e_s2b = fw.v(lambda E: E.tensor_single_scalar(
                c.qstat[:, 48:64], c.qstat[:, 48:64], EPS_K, ALU.mult))
            fw.w("vector", e_s2b, ef.qpTm[ba])
            e_qps = []
            for h in range(H):
                e_qp = fw.v(lambda E, h=h, ba=ba: E.tensor_scalar(
                    qp[:, ba, h, :], E1q[:, h, :],
                    c.qstat[:, 48 + h:49 + h], c.qstat[:, 32 + h:33 + h],
                    ALU.add, ALU.mult))
                e_qps.append(e_qp)
            ef.qp_ev[ba] = e_qps[-1]
            ef.E1_rd = e_qps[-1]

        def B3(i):
            b = i % 2
            ba = i % 5
            fw.w("vector", ef.e_zomm, ef.sread, c.e_bc[3])
            e_sA = fw.v(lambda E, ba=ba: E.tensor_tensor(sfull[:, 0:512], pzo0[:], afc[:, ba, 0:512], ALU.add))
            e_sB = fw.v(lambda E, ba=ba: E.tensor_tensor(sfull[:, 512:1024], pzo1[:], afc[:, ba, 512:1024], ALU.add))
            ef.zof = [e_sA, e_sB]
            fw.w("vector", e_sA, e_sB)
            e_sC = fw.v(lambda E: E.tensor_tensor(sfull[:], sfull[:], c.bcastA[:, 3, :], ALU.add))
            mu, r, e_mu, e_r = ln_stats(fw, c, sfull[:], 6, [e_sC])
            fw.w("vector", e_mu, e_r, ef.u[b], c.e_bc[0], c.e_bc[1])
            e_u1 = fw.v(lambda E, b=b, mu=mu, r=r: E.tensor_scalar(uful[:, b, :], sfull[:], mu, r,
                                                       ALU.subtract, ALU.mult))
            fw.w("vector", e_u1)
            e_u2 = fw.v(lambda E, b=b: E.tensor_tensor(uful[:, b, :], uful[:, b, :],
                                                       c.bcastA[:, 0, :], ALU.mult))
            fw.w("vector", e_u2)
            e_u3 = fw.v(lambda E, b=b: E.tensor_tensor(uful[:, b, :], uful[:, b, :],
                                                       c.bcastA[:, 1, :], ALU.add))
            ef.sread = e_u1
            fw.w("gpsimd", e_u3)
            e_usp = fw.dma("gpsimd", dus[i], uful[:, b, :], f"spu{b}")
            ef.u[b] = e_usp

        A1load(0)
        A1load(1)
        A1(0)
        A1load(2)
        A1(1)
        for i in range(NT):
            A2(i)
            if i + 3 < NT:
                A1load(i + 3)
            if i + 2 < NT:
                A1(i + 2)
            if i > 1:
                B1(i - 2)
                B2(i - 2)
            A3(i)
            if i > 1:
                B3(i - 2)
        for i in (NT - 2, NT - 1):
            B1(i)
            B2(i)
            B3(i)
        c.e_spills_c1 = [ef.u[0], ef.u[1]]
        PH["c1"] = fw.barrier()
    esC1.close()
    esC1w.close()
    esBA.close()
    if MK_PHASES <= 4:
        fw.emit(); es.close(); return nc

    c.W1T = es.enter_context(nc.sbuf_tensor("W1T", [128, KC, FF], BF16))
    c.W2T = es.enter_context(nc.sbuf_tensor("W2T", [128, NFC, D], BF16))
    fw.w("sync", c.e_spills_c1[0], c.e_spills_c1[1])
    e_w1T = []
    for kc in range(KC):
        fw.w("sync", e_cast["w1"])
        e_w1T.append(fw.dmat(c.W1T[:, kc, :], dwb["w1"][:, kc * 128:(kc + 1) * 128], "xb_w1"))
    e_w2T = []
    for fc in range(NFC):
        fw.w("sync", e_cast["w2"])
        e_w2T.append(fw.dmat(c.W2T[:, fc, :], dwb["w2"][:, fc * 128:(fc + 1) * 128], "xb_w2"))

        esC2 = ExitStack()
    urd = esC2.enter_context(nc.sbuf_tensor("urd", [128, 3, D], F32))
    x1rd2 = esC2.enter_context(nc.sbuf_tensor("x1rd2", [128, D], F32))
    ubf = esC2.enter_context(nc.sbuf_tensor("ubf", [128, D], BF16))
    uT2 = esC2.enter_context(nc.sbuf_tensor("uT2", [128, KC, 256], BF16))
    q8cT = esC2.enter_context(nc.sbuf_tensor("q8cTs", [128, KC * TS], FP8))
    hsb = esC2.enter_context(nc.sbuf_tensor("hsb", [128, 2, 256], BF16))
    wful2 = esC2.enter_context(nc.sbuf_tensor("wful2", [128, 2, D], F32))
    x2f = esC2.enter_context(nc.sbuf_tensor("x2f", [128, D], F32))
    q8c = esC2.enter_context(nc.sbuf_tensor("q8c", [128, D], FP8))
    q8cb = esC2.enter_context(nc.sbuf_tensor("q8cb", [128, D], BF16))
    pbfB = esC2.enter_context(nc.psum_tensor("pbfB", [128, 1024], BF16))
    ph0 = esC2.enter_context(nc.psum_tensor("ph0", [128, 512], F32))
    ph1 = esC2.enter_context(nc.psum_tensor("ph1", [128, 512], F32))
    pz2 = [esC2.enter_context(nc.psum_tensor(f"pz2_{j}", [128, 512], F32)) for j in range(4)]
    pq8c = esC2.enter_context(nc.psum_tensor("pq8c", [128, 1024], BF16))
    if True:
        fw.w("sync", c.e_spills_c1[0], c.e_spills_c1[1],
             c.e_spills_p2[0], c.e_spills_p2[1])
        ef = NS()
        ef.u = [None] * 3
        ef.x1r = None
        ef.ubf = None
        ef.uT = []
        ef.pbf = []
        ef.h = [[], []]
        ef.hsb = [[], []]
        ef.pz2free = []
        ef.q8 = []
        ef.pq8 = []
        ef.wslot = [None, None]
        ef.wout = [None, None]
        ef.lnx2 = None
        ef.x2 = None
        ef.w = None
        NB = NT // 2

        def loads(j):
            e_us = []
            for s in range(2):
                i = 2 * j + s
                sl = i % 3
                fw.w("sync", ef.u[sl])
                e_us.append(fw.dma("sync", urd[:, sl, :], dus[i], f"c2u{sl}"))
            return e_us

        def uTstage(j, e_us):
            e_uTs = []
            for s in range(2):
                i = 2 * j + s
                sl = i % 3
                fw.w("vector", e_us[s], ef.ubf)
                e_ubf = fw.v(lambda E, sl=sl: E.tensor_copy(ubf[:], urd[:, sl, :]))
                fw.w("tensor", e_ubf, *ef.pbf)
                for kc in range(KC):
                    fw.tr(pbfB[:, kc * 128:(kc + 1) * 128], ubf[:, kc * 128:(kc + 1) * 128], c.ident[:])
                e_uTmm = fw.mark("tensor")
                ef.ubf = e_uTmm
                fw.w("scalar", e_uTmm, *ef.uT)
                e_uT = fw.s(lambda E, s=s: E.copy(uT2[:, :, s * 128:(s + 1) * 128],
                                                  pbfB[:].rearrange("p (a b) -> p a b", a=KC)))
                e_uTs.append(e_uT)
                ef.pbf = [e_uT]
            ef.uT = e_uTs
            return e_uTs

        def ff(j, e_uTs):
            pend = None
            for fc in range(NFC):
                pb_ = ph0 if fc % 2 == 0 else ph1
                off = ((fc // 2) % 2) * 256
                fw.w("tensor", e_uTs[1], e_w1T[-1], *ef.h[fc % 2])
                if fc == 0:
                    fw.w("tensor", *ef.pz2free, e_w2T[-1])
                for kc in range(KC):
                    fw.mm(pb_[:, off:off + 256], c.W1T[:, kc, fc * 128:(fc + 1) * 128],
                          uT2[:, kc, :], start=(kc == 0), stop=(kc == KC - 1))
                e_hmm = fw.mark("tensor")
                slot = fc % 2
                if fc % 2 == 0:
                    fw.w("scalar", e_hmm, c.e_b1, *ef.hsb[slot])
                    e_hev = fw.s(lambda E, slot=slot, off=off, pb_=pb_, fc=fc: E.activation(
                        hsb[:, slot, :], pb_[:, off:off + 256], AF.Relu,
                        bias=c.b1_fm[:, fc:fc + 1]))
                else:
                    fw.w("vector", e_hmm, c.e_b1, *ef.hsb[slot])
                    e_hev = fw.v(lambda E, slot=slot, off=off, pb_=pb_, fc=fc: E.tensor_scalar(
                        hsb[:, slot, :], pb_[:, off:off + 256],
                        c.b1_fm[:, fc:fc + 1], 0.0, ALU.add, ALU.max))
                ef.h[fc % 2] = [e_hev]
                if pend is not None:
                    pfc, pslot, pev = pend
                    fw.w("tensor", pev)
                    for s in range(2):
                        for hh in range(2):
                            fw.mm(pz2[2 * s + hh][:], hsb[:, pslot, s * 128:(s + 1) * 128],
                                  c.W2T[:, pfc, hh * 512:(hh + 1) * 512],
                                  start=(pfc == 0), stop=False)
                    ef.hsb[pslot] = [fw.mark("tensor")]
                pend = (fc, slot, e_hev)
            pfc, pslot, pev = pend
            fw.w("tensor", pev)
            for s in range(2):
                for hh in range(2):
                    fw.mm(pz2[2 * s + hh][:], hsb[:, pslot, s * 128:(s + 1) * 128],
                          c.W2T[:, pfc, hh * 512:(hh + 1) * 512],
                          start=False, stop=True)
            ef.hsb[pslot] = [fw.mark("tensor")]
            return ef.hsb[pslot][0]

        def w_builds(j, e_z2_all, e_us):
            evs = []
            for s in range(2):
                i = 2 * j + s
                sl = i % 3
                fw.w("vector", e_z2_all, ef.wslot[s], c.e_bc[7], e_us[s])
                e_w1 = fw.v(lambda E, s=s, sl=sl: E.tensor_tensor(
                    wful2[:, s, 0:512], pz2[2 * s][:], urd[:, sl, 0:512], ALU.add))
                e_w2 = fw.v(lambda E, s=s, sl=sl: E.tensor_tensor(
                    wful2[:, s, 512:1024], pz2[2 * s + 1][:], urd[:, sl, 512:1024], ALU.add))
                fw.w("vector", e_w1, e_w2)
                e_w3 = fw.v(lambda E, s=s: E.tensor_tensor(
                    wful2[:, s, :], wful2[:, s, :], c.bcastB[:, 3, :], ALU.add))
                ef.u[sl] = e_w3
                evs.extend([e_w1, e_w2])
                ef.wout[s] = e_w3
            ef.pz2free = evs
            return evs

        def tail(j):
            for s in range(2):
                i = 2 * j + s
                wv = wful2[:, s, :]
                fw.w("sync", ef.x1r)
                e_x1 = fw.dma("sync", x1rd2[:], dx1[i], "c2x1")
                mu, r, e_mu, e_r = ln_stats(fw, c, wv, 0, [ef.wout[s]])
                fw.w("vector", e_mu, e_r, ef.x2, c.e_bc[4], c.e_bc[5], e_x1)
                e_x2a = fw.v(lambda E, mu=mu, r=r, wv=wv: E.tensor_scalar(
                    x2f[:], wv, mu, r, ALU.subtract, ALU.mult))
                fw.w("vector", e_x2a)
                e_x2b = fw.v(lambda E: E.tensor_tensor(x2f[:], x2f[:], c.bcastB[:, 0, :], ALU.mult))
                fw.w("vector", e_x2b)
                e_x2c = fw.v(lambda E: E.tensor_tensor(x2f[:], x2f[:], c.bcastB[:, 1, :], ALU.add))
                fw.w("vector", e_x2c)
                e_x2 = fw.v(lambda E: E.tensor_tensor(x2f[:], x2f[:], x1rd2[:], ALU.add))
                ef.x1r = e_x2
                mu2_, r2_, e_mu2, e_r2 = ln_stats(fw, c, x2f[:], 6, [e_x2])
                fw.w("vector", e_mu2, e_r2, ef.lnx2)
                e_ln = fw.v(lambda E, mu2_=mu2_, r2_=r2_, wv=wv: E.tensor_scalar(
                    wv, x2f[:], mu2_, r2_, ALU.subtract, ALU.mult))
                ef.x2 = e_ln
                st = c.sstat
                am = st[:, 12:13]
                qs = st[:, 13:14]
                fw.w("vector", e_ln)
                e_am = fw.v(lambda E, wv=wv: E.tensor_reduce(am, wv, AX, ALU.max,
                                                             apply_absolute_value=True))
                fw.w("vector", e_am)
                e_am2 = fw.v(lambda E: E.tensor_single_scalar(am, am, 1e-12, ALU.max))
                fw.w("vector", e_am2)
                e_qs = fw.v(lambda E: E.reciprocal(qs, am))
                fw.w("vector", e_qs)
                e_qs2 = fw.v(lambda E: E.tensor_single_scalar(qs, qs, 224.0, ALU.mult))
                fw.w("vector", c.e_ws)
                e_s1 = fw.v(lambda E: E.tensor_scalar(st[:, 14:15], am, 1.0 / 224.0,
                                                      c.wsv[:, 0:1], ALU.mult, ALU.mult))
                fw.w("vector", e_qs2, *ef.q8)
                e_q8 = fw.v(lambda E, wv=wv: E.tensor_scalar(q8c[:], wv, qs, None, ALU.mult))
                fw.w("vector", e_q8)
                e_q8b = fw.v(lambda E: E.tensor_copy(q8cb[:], q8c[:]))
                ef.lnx2 = e_q8
                fw.w("tensor", e_q8b, *ef.pq8)
                for kc in range(KC):
                    fw.tr(pq8c[:, kc * 128:(kc + 1) * 128], q8cb[:, kc * 128:(kc + 1) * 128],
                          c.ident[:])
                e_tt8 = fw.mark("tensor")
                ef.q8 = [e_tt8]
                fw.w("scalar", e_tt8, *ef.pq8)
                e_ev8a = fw.s(lambda E: E.copy(q8cT[:, 0:512], pq8c[:, 0:512]))
                e_ev8b = fw.s(lambda E: E.copy(q8cT[:, 512:1024], pq8c[:, 512:1024]))
                fw.w("tensor", e_ev8a, e_ev8b, *(ef.h[0] + ef.h[1]))
                for kc in range(KC):
                    fw.mm(ph0[:], q8cT[:, kc * 128:(kc + 1) * 128], c.w8T[:, kc, 0:512],
                          start=(kc == 0), stop=(kc == KC - 1))
                for kc in range(KC):
                    fw.mm(ph1[:], q8cT[:, kc * 128:(kc + 1) * 128], c.w8T[:, kc, 512:1024],
                          start=(kc == 0), stop=(kc == KC - 1))
                e_zp = fw.mark("tensor")
                ef.pq8 = [e_zp]
                fw.w("vector", e_zp, e_q8b)
                e_o1 = fw.v(lambda E, s=s: E.scalar_tensor_tensor(
                    wful2[:, s, 0:512], ph0[:], st[:, 14:15],
                    c.bcastB[:, 2, 0:512], ALU.mult, ALU.add))
                e_o2 = fw.v(lambda E, s=s: E.scalar_tensor_tensor(
                    wful2[:, s, 512:1024], ph1[:], st[:, 14:15],
                    c.bcastB[:, 2, 512:1024], ALU.mult, ALU.add))
                fw.w("vector", e_o1, e_o2)
                e_o3 = fw.v(lambda E, s=s: E.tensor_tensor(wful2[:, s, :], wful2[:, s, :],
                                                           x2f[:], ALU.add))
                ef.h[0] = [e_o1]
                ef.h[1] = [e_o2]
                fw.w("gpsimd", e_o3)
                e_osp = fw.dma("gpsimd", dout[i], wful2[:, s, :], f"out{s}")
                ef.wslot[s] = e_osp
        prev = None
        for j in range(NB):
            e_us = loads(j)
            e_uTs = uTstage(j, e_us)
            e_z2_all = ff(j, e_uTs)
            if prev is not None:
                tail(prev)
            w_builds(j, e_z2_all, e_us)
            prev = j
        tail(prev)
        fw.w("gpsimd", ef.wslot[0], ef.wslot[1])
        fw.g(lambda E: E.nop())
    esC2.close()
    fw.emit()
    es.close()
    return nc


from concourse.bass_utils import run_bass_kernel_spmd

_NCORES = 8

def kernel(**inputs):
    nc = build_nc()
    video = np.asarray(inputs["L0_attn_video"])
    base = {
        "wq": np.asarray(inputs["L0_attn_Wq"]),
        "wk": np.asarray(inputs["L0_attn_Wk"]),
        "wv": np.asarray(inputs["L0_attn_Wv"]),
        "wo": np.asarray(inputs["L0_attn_Wo"]),
        "w1": np.asarray(inputs["L0_attn_W1"]),
        "w2": np.asarray(inputs["L0_attn_W2"]),
        "w8": np.asarray(inputs["L0_ffn_weight"]).view(np.uint8),
        "proj": np.asarray(inputs["L0_attn_proj"]),
        "bq": np.asarray(inputs["L0_attn_bq"]),
        "bk": np.asarray(inputs["L0_attn_bk"]),
        "bv": np.asarray(inputs["L0_attn_bv"]),
        "bo": np.asarray(inputs["L0_attn_bo"]),
        "b1": np.asarray(inputs["L0_attn_b1"]),
        "b2": np.asarray(inputs["L0_attn_b2"]),
        "g2": np.asarray(inputs["L0_attn_g2"]),
        "be2": np.asarray(inputs["L0_attn_be2"]),
        "g3": np.asarray(inputs["L0_attn_g3"]),
        "be3": np.asarray(inputs["L0_attn_be3"]),
        "ws": np.asarray(inputs["L0_ffn_weight_scale"]).reshape(1, 1),
        "b8": np.asarray(inputs["L0_ffn_bias"]),
    }
    in_maps = []
    for core in range(_NCORES):
        m = dict(base)
        m["video"] = np.ascontiguousarray(video[core]).reshape(NT, TS, D)
        in_maps.append(m)
    res = run_bass_kernel_spmd(nc, in_maps, core_ids=list(range(_NCORES)))
    out = np.stack([np.asarray(res.results[core]["out"]).reshape(T, D)
                    for core in range(_NCORES)], axis=0)
    return out


# version 25
# speedup vs baseline: 1.0237x; 1.0045x over previous
import os
import numpy as np
import concourse.bass as bass
import concourse.mybir as mybir
from contextlib import ExitStack

F32 = mybir.dt.float32
BF16 = mybir.dt.bfloat16
FP8 = mybir.dt.float8e4
U8 = mybir.dt.uint8
U32 = mybir.dt.uint32
AF = mybir.ActivationFunctionType
ALU = mybir.AluOpType
AX = mybir.AxisListType.X

T, NT, TS = 4096, 32, 128
D, KC = 1024, 8
H, DH, M = 16, 64, 256
FF, NFC = 4096, 32
EPS_LN = 1e-5
EPS_K = 1e-4
NORM = float(64.0 ** (-0.25))
C_DIAG = float(0.5 * (NORM * NORM))
NEG_BIG = -3.0e38


class Ev:
    __slots__ = ("sem", "val")
    def __init__(self, sem, val):
        self.sem, self.val = sem, val


class FW:
    ENGS = ("sync", "gpsimd", "tensor", "vector", "scalar")
    DSEM = {"sync": "syncd", "gpsimd": "gpsimdd"}

    def __init__(self, nc, es):
        self.nc = nc
        self.es = es
        self.streams = {e: [] for e in self.ENGS}
        self.sems = {}
        self.cnt = {}
        for s in self.ENGS:
            self.sems[s] = es.enter_context(nc.semaphore(f"S_{s}"))
            self.cnt[s] = 0

    def _dma_key(self, key):
        k = "d_" + key
        if k not in self.sems:
            self.sems[k] = self.es.enter_context(self.nc.semaphore(k))
            self.cnt[k] = 0
        return k

    def _rec(self, eng, build, key):
        evs = []
        if key is not None:
            step = 16 if key.startswith("d_") else 1
            self.cnt[key] += step
            evs.append(Ev(key, self.cnt[key]))
        is_dma = bool(key is not None and key.startswith("d_"))
        self.streams[eng].append(["op", build, evs, is_dma])
        return evs[0] if evs else None

    def w(self, eng, *evs):
        for ev in evs:
            if ev is not None:
                self.streams[eng].append(("wait", ev))

    def dma(self, eng, out, in_, key, slow=False):
        k = self._dma_key(key)
        if slow:
            ev = self._rec(eng, lambda E: E.dma_start(out, in_, allow_slow_non_contiguous=True), k)
        else:
            ev = self._rec(eng, lambda E: E.dma_start(out, in_), k)
        self.streams[eng][-1][3] = True
        return ev

    def dmat(self, out, in_, key):
        ev = self._rec("sync", lambda E: E.dma_start_transpose(out, in_),
                       self._dma_key(key))
        self.streams["sync"][-1][3] = True
        return ev

    def mm(self, out, lhsT, rhs, start=True, stop=True, ev=False):
        return self._rec("tensor",
                         lambda E: E.matmul(out, lhsT, rhs, start=start, stop=stop),
                         "tensor" if ev else None)

    def tr(self, out, in_, ident, ev=False):
        return self._rec("tensor", lambda E: E.transpose(out, in_, ident),
                         "tensor" if ev else None)

    def v(self, build, ev=True):
        return self._rec("vector", build, "vector" if ev else None)

    def s(self, build, ev=True):
        return self._rec("scalar", build, "scalar" if ev else None)

    def g(self, build, ev=True):
        if self.cnt["gpsimd"] > 0:
            self.w("gpsimd", Ev("gpsimd", self.cnt["gpsimd"]))
        return self._rec("gpsimd", build, "gpsimd" if ev else None)

    def mark(self, eng):
        for item in reversed(self.streams[eng]):
            if item[0] == "op":
                if item[3] or len(item[2]) >= 1:
                    key = eng
                    if self.cnt[key] > 0:
                        self.w(eng, Ev(key, self.cnt[key]))
                    self.cnt[key] += 1
                    ev = Ev(key, self.cnt[key])
                    self.streams[eng].append(["op", lambda E: E.nop(), [ev], False])
                    return ev
                key = eng
                self.cnt[key] += 1
                ev = Ev(key, self.cnt[key])
                item[2].append(ev)
                return ev
        raise RuntimeError(f"mark({eng}) with no prior op")

    def barrier(self):
        marks = {e: self.mark(e) for e in self.ENGS}
        for e in self.ENGS:
            self.w(e, *[marks[o] for o in self.ENGS if o != e])
        return marks

    def emit(self):
        with self.nc.Block() as block:
            for eng in self.ENGS:
                stream = self.streams[eng]
                sems = self.sems

                def section(E, stream=stream):
                    hi = {}
                    for item in stream:
                        if item[0] == "wait":
                            ev = item[1]
                            if hi.get(ev.sem, -1) >= ev.val:
                                continue
                            E.wait_ge(sems[ev.sem], ev.val)
                            hi[ev.sem] = ev.val
                        else:
                            _, build, evs, _k = item
                            inst = build(E)
                            for ev in evs:
                                inst.then_inc(sems[ev.sem], 16 if ev.sem.startswith("d_") else 1)

                getattr(block, eng)(section)


class NS:
    pass


def ln_stats(fw, c, x_ap, col, e_x, n=D):
    st = c.sstat
    mu = st[:, col:col + 1]
    r = st[:, col + 1:col + 2]
    t1 = st[:, col + 2:col + 3]
    t2 = st[:, col + 3:col + 4]
    ssum = st[:, col + 4:col + 5]
    ssq = st[:, col + 5:col + 6]
    fw.w("scalar", *e_x)
    e_sq = fw.s(lambda E: E.activation(c.junk[:, 0:n], x_ap, AF.Square, accum_out=ssq))
    fw.w("vector", *e_x)
    e_sm = fw.v(lambda E: E.reduce_sum(ssum, x_ap, AX))
    fw.w("vector", e_sm)
    e_mu = fw.v(lambda E: E.tensor_single_scalar(mu, ssum, 1.0 / n, ALU.mult))
    fw.w("vector", e_mu)
    e1 = fw.v(lambda E: E.tensor_tensor(t1, mu, mu, ALU.mult))
    fw.w("vector", e_sq, e1)
    e2 = fw.v(lambda E: E.tensor_scalar(t2, ssq, 1.0 / n, EPS_LN, ALU.mult, ALU.add))
    fw.w("vector", e2)
    e3 = fw.v(lambda E: E.tensor_tensor(t2, t2, t1, ALU.subtract))
    fw.w("scalar", e3)
    e5 = fw.s(lambda E: E.activation(t1, t2, AF.Ln))
    fw.w("scalar", e5)
    e_r = fw.s(lambda E: E.activation(r, t1, AF.Exp, scale=-0.5))
    return mu, r, e_mu, e_r


MK_PHASES = int(os.environ.get("MK_PHASES", "9"))


def build_nc():
    nc = bass.Bass("TRN2", target_bir_lowering=False)
    es = ExitStack()
    c = NS()

    dvideo = nc.dram_tensor("video", [NT, TS, D], F32, kind="ExternalInput")
    dwq = nc.dram_tensor("wq", [D, D], F32, kind="ExternalInput")
    dwk = nc.dram_tensor("wk", [D, D], F32, kind="ExternalInput")
    dwv = nc.dram_tensor("wv", [D, D], F32, kind="ExternalInput")
    dwo = nc.dram_tensor("wo", [D, D], F32, kind="ExternalInput")
    dw1 = nc.dram_tensor("w1", [FF, D], F32, kind="ExternalInput")
    dw2 = nc.dram_tensor("w2", [D, FF], F32, kind="ExternalInput")
    dw8 = nc.dram_tensor("w8", [D, D], U8, kind="ExternalInput")
    dproj = nc.dram_tensor("proj", [M, DH], F32, kind="ExternalInput")
    dbias = {}
    for nm, sz in [("bq", D), ("bk", D), ("bv", D), ("bo", D), ("b1", FF),
                   ("b2", D), ("g2", D), ("be2", D), ("g3", D), ("be3", D), ("b8", D)]:
        dbias[nm] = nc.dram_tensor(nm, [sz], F32, kind="ExternalInput")
    dws = nc.dram_tensor("ws", [1, 1], F32, kind="ExternalInput")
    dout = nc.dram_tensor("out", [NT, TS, D], F32, kind="ExternalOutput")

    dx1 = nc.dram_tensor("x1s", [NT, TS, D], F32)
    dus = nc.dram_tensor("us", [NT, TS, D], F32)
    dks = nc.dram_tensor("ks", [NT, 128, KC, TS], BF16)
    dvs = nc.dram_tensor("vs", [NT, TS, H, DH + 1], BF16)
    dwb = {}
    for nm, shp in [("wq", [D, D]), ("wk", [D, D]), ("wv", [D, D]), ("wo", [D, D]),
                    ("w1", [FF, D]), ("w2", [D, FF])]:
        dwb[nm] = nc.dram_tensor(nm + "bf", shp, BF16)

    sb = lambda n, s, d: es.enter_context(nc.sbuf_tensor(n, s, d))
    c.ident = sb("ident", [128, 128], BF16)
    c.identf = sb("identf", [128, 128], F32)
    c.ident8 = sb("ident8", [128, 128], FP8)
    c.projn2 = sb("projn2", [128, 2 * M], BF16)
    c.b_fm = sb("b_fm", [128, 2, KC], F32)
    c.b1_fm = sb("b1_fm", [128, NFC], F32)
    c.bcastB = sb("bcastB", [128, 4, D], F32)
    c.wsv = sb("wsv", [128, 2], F32)
    c.sstat = sb("sstat", [128, 32], F32)
    c.junk = sb("junk", [128, D], F32)
    c.st_s1 = sb("st_s1", [128, NT], F32)
    c.st_mu2 = sb("st_mu2", [128, NT], F32)
    c.st_r2 = sb("st_r2", [128, NT], F32)
    c.st_diagk = sb("st_diagk", [128, NT, H], F32)
    c.st_kmax = sb("st_kmax", [128, H], F32)
    c.stabk = sb("stabk", [128, H], F32)
    c.s1k = sb("s1k", [128, NT, H], F32)
    c.s2k = sb("s2k", [128, NT, H], F32)
    c.w8T = sb("w8T", [128, KC, D], FP8)

    esBA = ExitStack()
    c.bcastA = esBA.enter_context(nc.sbuf_tensor("bcastA", [128, 4, D], F32))
    c.ctxT = esBA.enter_context(nc.sbuf_tensor("ctxT", [128, H, 2, DH + 1], BF16))

    fw = FW(nc, es)
    PH = {}

    ev = fw.g(lambda E: E.memset(c.ident[:], 0.0))
    fw.w("gpsimd", ev)
    e_id = fw.g(lambda E: E.affine_select(out=c.ident[:], in_=c.ident[:],
                                          compare_op=ALU.not_equal, fill=1.0, base=0,
                                          pattern=[[-1, 128]], channel_multiplier=1))
    ev = fw.g(lambda E: E.memset(c.identf[:], 0.0))
    fw.w("gpsimd", ev)
    e_idf = fw.g(lambda E: E.affine_select(out=c.identf[:], in_=c.identf[:],
                                           compare_op=ALU.not_equal, fill=1.0, base=0,
                                           pattern=[[-1, 128]], channel_multiplier=1))
    ev = fw.g(lambda E: E.memset(c.ident8[:], 0.0))
    fw.w("gpsimd", ev)
    e_id8 = fw.g(lambda E: E.affine_select(out=c.ident8[:], in_=c.ident8[:],
                                           compare_op=ALU.not_equal, fill=1.0, base=0,
                                           pattern=[[-1, 128]], channel_multiplier=1))
    e_kmax0 = fw.v(lambda E: E.memset(c.st_kmax[:], NEG_BIG))

    e_bq = fw.dma("sync", c.b_fm[:, 0, :], bass.AP(dbias["bq"], 0, [[1, 128], [128, KC]]), "bq", slow=True)
    e_bk = fw.dma("sync", c.b_fm[:, 1, :], bass.AP(dbias["bk"], 0, [[1, 128], [128, KC]]), "bk", slow=True)
    e_b1 = fw.dma("sync", c.b1_fm[:], bass.AP(dbias["b1"], 0, [[1, 128], [128, NFC]]), "b1", slow=True)
    e_bc = []
    for j, nm in enumerate(["g2", "be2", "bv", "bo", "g3", "be3", "b8", "b2"]):
        tgt = c.bcastA[:, j, :] if j < 4 else c.bcastB[:, j - 4, :]
        e_bc.append(fw.dma("sync", tgt,
                           bass.AP(dbias[nm], 0, [[0, 128], [1, D]]), f"bc{j}"))
    fw.w("vector", e_bc[6])
    e_hb8 = fw.v(lambda E: E.tensor_single_scalar(c.bcastB[:, 2, :], c.bcastB[:, 2, :],
                                                  0.5, ALU.mult))
    e_bc[6] = e_hb8
    e_ws = fw.dma("sync", c.wsv[:, 0:1], bass.AP(dws, 0, [[0, 128], [1, 1]]), "ws")
    c.e_ws, c.e_bc, c.e_bq, c.e_bk, c.e_b1 = e_ws, e_bc, e_bq, e_bk, e_b1

    e_cast = {}
    for nm, src in [("wk", dwk), ("wv", dwv), ("wq", dwq), ("wo", dwo),
                    ("w1", dw1), ("w2", dw2)]:
        e_cast[nm] = fw.dma("gpsimd", dwb[nm][:], src[:], "cst_" + nm)

    with nc.sbuf_tensor("projr", [128, 2, DH], F32) as projr, \
         nc.sbuf_tensor("projf", [64, M], F32) as projf, \
         nc.psum_tensor("ppj", [64, M], F32) as ppj:
        e_pl = fw.dma("sync", projr[:], bass.AP(dproj, 0, [[DH, 128], [128 * DH, 2], [1, DH]]), "proj")
        fw.w("tensor", e_pl, e_idf)
        e_t0 = fw.tr(ppj[:, 0:128], projr[:, 0, :], c.identf[:], ev=True)
        e_t1 = fw.tr(ppj[:, 128:256], projr[:, 1, :], c.identf[:], ev=True)
        fw.w("scalar", e_t0, e_t1)
        e_pf = fw.s(lambda E: E.copy(projf[:], ppj[:]))
        fw.w("vector", e_pf)
        e_pz = fw.v(lambda E: E.memset(c.projn2[:], 0.0))
        fw.w("vector", e_pz)
        e_pn0 = fw.v(lambda E: E.tensor_single_scalar(c.projn2[0:64, 0:M], projf[:], NORM, ALU.mult))
        e_pn1 = fw.v(lambda E: E.tensor_single_scalar(c.projn2[64:128, M:2 * M], projf[:], NORM, ALU.mult))

        with nc.sbuf_tensor("w8u", [128, KC, D], U8) as w8u, \
             nc.sbuf_tensor("w8h", [128, KC, D], FP8) as w8h, \
             nc.sbuf_tensor("w8hb", [128, KC, D], BF16) as w8hb, \
             nc.sbuf_tensor("du0", [128, 1024], U32) as du0, \
             nc.sbuf_tensor("du1", [128, 1024], U32) as du1, \
             nc.sbuf_tensor("du2", [128, 1024], U32) as du2, \
             nc.sbuf_tensor("df0", [128, 1024], F32) as df0, \
             nc.sbuf_tensor("df1", [128, 1024], F32) as df1:
            e_w8 = fw.dma("sync", w8u[:], bass.AP(dw8, 0, [[D, 128], [128 * D, KC], [1, D]]), "w8")

            e_dec = []
            for gi in range(KC):
                fw.w("vector", e_w8)
                ev = fw.v(lambda E, gi=gi: E.tensor_copy(du0[:], w8u[:, gi, :]))
                fw.w("vector", ev)
                ev = fw.v(lambda E: E.tensor_single_scalar(du2[:], du0[:], 7,
                                                           ALU.logical_shift_right))
                fw.w("vector", ev)
                ev = fw.v(lambda E: E.tensor_scalar(df0[:], du2[:], -2.0, 1.0,
                                                    ALU.mult, ALU.add))
                e_sign = ev
                ev = fw.v(lambda E: E.tensor_single_scalar(du1[:], du0[:], 0x7F, ALU.bitwise_and))
                fw.w("vector", ev)
                ev = fw.v(lambda E: E.tensor_single_scalar(du2[:], du1[:], 7, ALU.bitwise_and))
                e_m = ev
                ev = fw.v(lambda E: E.tensor_single_scalar(du0[:], du1[:], 8, ALU.is_ge))
                fw.w("vector", ev)
                ev = fw.v(lambda E: E.tensor_single_scalar(du0[:], du0[:], 3, ALU.logical_shift_left))
                fw.w("vector", ev, e_m)
                ev = fw.v(lambda E: E.tensor_tensor(du2[:], du2[:], du0[:], ALU.add))
                fw.w("vector", ev)
                e_af = fw.v(lambda E: E.tensor_copy(df1[:], du2[:]))
                ev = fw.v(lambda E: E.tensor_single_scalar(du1[:], du1[:], 3, ALU.logical_shift_right))
                fw.w("vector", ev)
                ev = fw.v(lambda E: E.tensor_single_scalar(du1[:], du1[:], 1, ALU.max))
                fw.w("vector", ev)
                ev = fw.v(lambda E: E.tensor_single_scalar(du1[:], du1[:], 117, ALU.add))
                fw.w("vector", ev)
                ev = fw.v(lambda E: E.tensor_single_scalar(du1[:], du1[:], 23, ALU.logical_shift_left))
                fw.w("vector", ev, e_af)
                ev = fw.v(lambda E: E.tensor_tensor(df1[:], df1[:], du1.ap().bitcast(F32), ALU.mult))
                fw.w("vector", ev, e_sign)
                ev = fw.v(lambda E: E.tensor_tensor(df1[:], df1[:], df0[:], ALU.mult))
                fw.w("vector", ev)
                ev = fw.v(lambda E, gi=gi: E.tensor_scalar(w8h[:, gi, :], df1[:], 0.5, None, ALU.mult))
                fw.w("vector", ev)
                ev = fw.v(lambda E, gi=gi: E.tensor_copy(w8hb[:, gi, :], w8h[:, gi, :]))
                e_dec.append(ev)
            with nc.psum_tensor("p8t", [128, 512], BF16) as p8t:
                prev = []
                for kc in range(KC):
                    for half in range(2):
                        for j in range(4):
                            dc = half * 4 + j
                            fw.w("tensor", e_dec[dc], e_id, *prev)
                            fw.tr(p8t[:, j * 128:(j + 1) * 128],
                                  w8hb[:, dc, kc * 128:(kc + 1) * 128], c.ident[:])
                        e_tt = fw.mark("tensor")
                        fw.w("scalar", e_tt)
                        prev = [fw.s(lambda E, kc=kc, half=half:
                                     E.copy(c.w8T[:, kc, half * 512:(half + 1) * 512], p8t[:]))]
                e_w8T = prev[0]

    c.e_id8 = e_id8
    PH["setup"] = fw.barrier()

    esP12 = ExitStack()
    c.q8T = esP12.enter_context(nc.sbuf_tensor("q8T", [128, 2, KC, TS], FP8))
    c.WkT = esP12.enter_context(nc.sbuf_tensor("WkT", [128, KC, D], BF16))
    c.WvT = esP12.enter_context(nc.sbuf_tensor("WvT", [128, KC, D], BF16))
    e_wkT, e_wvT = [], []
    for kc in range(KC):
        fw.w("sync", e_cast["wk"], e_cast["wv"])
        e_wkT.append(fw.dmat(c.WkT[:, kc, :], dwb["wk"][:, kc * 128:(kc + 1) * 128], "xb_wk"))
        e_wvT.append(fw.dmat(c.WvT[:, kc, :], dwb["wv"][:, kc * 128:(kc + 1) * 128], "xb_wv"))

    def ffn8_quant(i, vbuf, e_v, statcol, s1_dst, q8T_dst, pq8, e_pq8_free):
        mu, r, e_mu, e_r = ln_stats(fw, c, vbuf, statcol, [e_v])
        fw.w("vector", e_mu, e_r)
        e_ln = fw.v(lambda E: E.tensor_scalar(c.lnx[:], vbuf, mu, r, ALU.subtract, ALU.mult))
        st = c.sstat
        am = st[:, 12:13]
        qs = st[:, 13:14]
        fw.w("vector", e_ln)
        e_am = fw.v(lambda E: E.tensor_reduce(am, c.lnx[:], AX, ALU.max, apply_absolute_value=True))
        fw.w("vector", e_am)
        e_am2 = fw.v(lambda E: E.tensor_single_scalar(am, am, 1e-12, ALU.max))
        fw.w("vector", e_am2)
        e_qs = fw.v(lambda E: E.reciprocal(qs, am))
        fw.w("vector", e_qs)
        e_qs2 = fw.v(lambda E: E.tensor_single_scalar(qs, qs, 224.0, ALU.mult))
        fw.w("vector", c.e_ws)
        e_s1 = fw.v(lambda E: E.tensor_scalar(s1_dst, am, 1.0 / 224.0, c.wsv[:, 0:1],
                                              ALU.mult, ALU.mult))
        fw.w("vector", e_qs2)
        e_q8 = fw.v(lambda E: E.tensor_scalar(c.q8tok[:], c.lnx[:], qs, None, ALU.mult))
        fw.w("vector", e_q8)
        e_q8b = fw.v(lambda E: E.tensor_copy(c.q8bf[:], c.q8tok[:]))
        return e_q8b, e_ln

    def ffn8_trans(e_q8b, q8T_dst, pq8, e_pq8_free):
        fw.w("tensor", e_q8b, e_id, *e_pq8_free)
        for kc in range(KC):
            fw.tr(pq8[:, kc * 128:(kc + 1) * 128], c.q8bf[:, kc * 128:(kc + 1) * 128], c.ident[:])
        e_tt = fw.mark("tensor")
        fw.w("scalar", e_tt)
        e_ev1 = fw.s(lambda E, q8T_dst=q8T_dst, pq8=pq8: E.copy(
            q8T_dst[:, 0:4, :].rearrange("p a b -> p (a b)"), pq8[:, 0:512]))
        e_ev2 = fw.s(lambda E, q8T_dst=q8T_dst, pq8=pq8: E.copy(
            q8T_dst[:, 4:8, :].rearrange("p a b -> p (a b)"), pq8[:, 512:1024]))
        return [e_ev1, e_ev2], e_tt

    c.lnx = esP12.enter_context(nc.sbuf_tensor("lnx", [128, D], F32))
    c.q8tok = esP12.enter_context(nc.sbuf_tensor("q8tok", [128, D], FP8))
    c.q8bf = esP12.enter_context(nc.sbuf_tensor("q8bf", [128, D], BF16))
    PH["p1"] = fw.barrier()

    if MK_PHASES <= 1:
        fw.emit(); esP12.close(); esBA.close(); es.close(); return nc

    with nc.sbuf_tensor("vid2", [128, 2, D], F32) as vid, \
         nc.sbuf_tensor("x1st", [128, 2, D], F32) as x1st, \
         nc.sbuf_tensor("af", [128, 2, D], F32) as af, \
         nc.sbuf_tensor("abf", [128, 2, D], BF16) as abf, \
         nc.sbuf_tensor("aT", [128, 2, KC, TS], BF16) as aT, \
         nc.sbuf_tensor("kst", [128, 2, KC, TS], BF16) as kst, \
         nc.sbuf_tensor("vst", [128, 2, H, DH + 1], BF16) as vst, \
         nc.sbuf_tensor("ksq", [128, D], F32) as ksq, \
         nc.sbuf_tensor("tmp5", [128, 512], F32) as tmp5, \
         nc.psum_tensor("pz0", [128, 512], F32) as pz0, \
         nc.psum_tensor("pz1", [128, 512], F32) as pz1, \
         nc.psum_tensor("pbf", [128, 1024], BF16) as pbf, \
         nc.psum_tensor("pbf2", [128, 1024], BF16) as pbf2, \
         nc.psum_tensor("pk0", [128, 512], F32) as pk0, \
         nc.psum_tensor("pk1", [128, 512], F32) as pk1, \
         nc.psum_tensor("pxk0", [128, 512], F32) as pxk0, \
         nc.psum_tensor("pxk1", [128, 512], F32) as pxk1:
        e_ones0 = fw.v(lambda E: E.memset(vst[:, 0, :, 64:65], 1.0))
        e_ones1 = fw.v(lambda E: E.memset(vst[:, 1, :, 64:65], 1.0))
        ef = NS()
        ef.vid = [None, None]
        ef.x1 = [None, None]
        ef.x1read = [None, None]
        ef.afp = [None, None]
        ef.abfp = [None, None]
        ef.pbf_r = []
        ef.pbf2_r = []
        ef.pq8 = []
        ef.zq = [[], []]
        ef.lnx = None
        ef.q8con = []
        ef.k = [[], []]
        ef.v = [e_ones0, e_ones1]
        ef.z = []
        ef.kmm = []
        ef.vmm = []
        ef.aT_r = [[], []]
        ef.ksq_r = None
        ef.xkmm = [[], []]
        ef.xkred = None
        ef.kst_w = [[], []]

        def quantV(i):
            b = i % 2
            fw.w("sync", ef.vid[b])
            e_v = fw.dma("sync", vid[:, b, :], dvideo[i], f"vid2_{b}")
            fw.w("vector", e_v, ef.lnx, *ef.q8con)
            e_q8b, e_ln = ffn8_quant(i, vid[:, b, :], e_v, 0, c.st_s1[:, i:i + 1],
                                     None, None, None)
            ef.lnx = e_ln
            return e_q8b

        def stageA(i, e_q8b):
            b = i % 2
            evs, _e_tt = ffn8_trans(e_q8b, c.q8T[:, b, :, :], pbf, ef.pq8 + ef.zq[b] + ef.pbf_r)
            ef.q8con = [_e_tt]
            ef.pq8 = evs
            ef.pbf_r = [_e_tt]
            fw.w("tensor", e_w8T, *evs, *ef.z)
            for kc in range(KC):
                fw.mm(pz0[:], c.q8T[:, b, kc, :], c.w8T[:, kc, 0:512],
                      start=(kc == 0), stop=(kc == KC - 1))
            for kc in range(KC):
                fw.mm(pz1[:], c.q8T[:, b, kc, :], c.w8T[:, kc, 512:1024],
                      start=(kc == 0), stop=(kc == KC - 1))
            e_zmm = fw.mark("tensor")
            ef.zq[b] = [e_zmm]
            s1c = c.st_s1[:, i:i + 1]
            fw.w("vector", e_zmm, ef.lnx, ef.x1[b], ef.x1read[b], c.e_bc[6], c.e_bc[2])
            e1 = fw.v(lambda E, s1c=s1c: E.scalar_tensor_tensor(tmp5[:], pz0[:], s1c, c.bcastB[:, 2, 0:512],
                                                       ALU.mult, ALU.add))
            fw.w("vector", e1)
            e2 = fw.v(lambda E, b=b: E.tensor_tensor(x1st[:, b, 0:512], tmp5[:], vid[:, b, 0:512], ALU.add))
            fw.w("vector", e2)
            e3 = fw.v(lambda E, s1c=s1c: E.scalar_tensor_tensor(tmp5[:], pz1[:], s1c, c.bcastB[:, 2, 512:1024],
                                                       ALU.mult, ALU.add))
            fw.w("vector", e3)
            e4 = fw.v(lambda E, b=b: E.tensor_tensor(x1st[:, b, 512:1024], tmp5[:], vid[:, b, 512:1024], ALU.add))
            ef.z = [e2, e4]
            ef.vid[b] = e4
            e_x1 = [e2, e4]
            fw.w("gpsimd", *e_x1)
            e_sp = fw.dma("gpsimd", dx1[i], x1st[:, b, :], f"spx1_{b}")
            ef.x1[b] = e_sp
            mu, r, e_mu, e_r = ln_stats(fw, c, x1st[:, b, :], 6, e_x1)
            fw.w("vector", e_mu, e_r, ef.afp[b])
            e_m2 = fw.v(lambda E, i=i, mu=mu: E.tensor_copy(c.st_mu2[:, i:i + 1], mu))
            e_r2 = fw.v(lambda E, i=i, r=r: E.tensor_copy(c.st_r2[:, i:i + 1], r))
            e_af = fw.v(lambda E, b=b, mu=mu, r=r: E.tensor_scalar(af[:, b, :], x1st[:, b, :], mu, r, ALU.subtract, ALU.mult))
            ef.x1read[b] = e_af
            fw.w("vector", e_af, ef.abfp[b])
            e_abf = fw.v(lambda E, b=b: E.tensor_copy(abf[:, b, :], af[:, b, :]))
            ef.afp[b] = e_abf
            fw.w("tensor", e_abf, *ef.pbf_r)
            for kc in range(KC):
                fw.tr(pbf[:, kc * 128:(kc + 1) * 128], abf[:, b, kc * 128:(kc + 1) * 128], c.ident[:])
            e_aTmm = fw.mark("tensor")
            ef.abfp[b] = e_aTmm
            fw.w("scalar", e_aTmm, *ef.aT_r[b])
            e_aT = fw.s(lambda E, b=b: E.copy(aT[:, b, :, :].rearrange("p a b -> p (a b)"), pbf[:]))
            ef.pbf_r = [e_aT]
            return e_aT

        def stageB(i, e_aT):
            b = i % 2
            e_kev = []
            for half in range(2):
                pk = pk0 if half == 0 else pk1
                fw.w("tensor", e_aT, e_wkT[-1], *ef.kmm, *ef.vmm)
                for j in range(4):
                    dc = half * 4 + j
                    for kc in range(KC):
                        fw.mm(pk[:, j * 128:(j + 1) * 128],
                              c.WkT[:, kc, dc * 128:(dc + 1) * 128], aT[:, b, kc, :],
                              start=(kc == 0), stop=(kc == KC - 1))
                e_kmm = fw.mark("tensor")
                fw.w("scalar", e_kmm, c.e_bk, *ef.k[b], *ef.kst_w[b])
                for j in range(4):
                    dc = half * 4 + j
                    e_kev.append(fw.s(lambda E, b=b, dc=dc, j=j, pk=pk:
                                      E.activation(kst[:, b, dc, :], pk[:, j * 128:(j + 1) * 128],
                                                   AF.Identity, bias=c.b_fm[:, 1, dc:dc + 1])))
                ef.kmm = e_kev[-4:]
            ef.kst_w[b] = e_kev
            fw.w("gpsimd", *e_kev)
            e_ksp = fw.dma("gpsimd", dks[i], kst[:, b, :, :], f"spk_{b}")
            fw.w("tensor", e_kev[-1], e_wvT[-1], *ef.vmm)
            for kc in range(KC):
                fw.mm(pk0[:], aT[:, b, kc, :], c.WvT[:, kc, 0:512],
                      start=(kc == 0), stop=(kc == KC - 1))
            for kc in range(KC):
                fw.mm(pk1[:], aT[:, b, kc, :], c.WvT[:, kc, 512:1024],
                      start=(kc == 0), stop=(kc == KC - 1))
            e_vmm = fw.mark("tensor")
            ef.aT_r[b] = [e_vmm]
            fw.w("vector", e_vmm, ef.v[b])
            e_v0 = fw.v(lambda E, b=b: E.tensor_tensor(
                vst[:, b, 0:8, 0:64], pk0[:].rearrange("p (h d) -> p h d", h=8),
                c.bcastA[:, 2, 0:512].rearrange("p (h d) -> p h d", h=8), ALU.add))
            e_v1 = fw.v(lambda E, b=b: E.tensor_tensor(
                vst[:, b, 8:16, 0:64], pk1[:].rearrange("p (h d) -> p h d", h=8),
                c.bcastA[:, 2, 512:1024].rearrange("p (h d) -> p h d", h=8), ALU.add))
            ef.vmm = [e_v0, e_v1]
            fw.w("gpsimd", e_v0, e_v1)
            e_vsp = fw.dma("gpsimd", dvs[i], vst[:, b, :, :], f"spv_{b}")
            ef.v[b] = e_vsp
            fw.w("tensor", e_kev[-1], *ef.pbf2_r)
            for kc in range(KC):
                fw.tr(pbf2[:, kc * 128:(kc + 1) * 128], kst[:, b, kc, :], c.ident[:])
            e_ktt = fw.mark("tensor")
            fw.w("scalar", e_ktt, ef.ksq_r)
            e_ksq = fw.s(lambda E: E.activation(ksq[:], pbf2[:], AF.Square))
            ef.pbf2_r = [e_ksq]
            fw.w("vector", e_ksq)
            e_dk = fw.v(lambda E, i=i: E.tensor_reduce(
                c.st_diagk[:, i, :, None], ksq[:].rearrange("p (h d) -> p h d", h=H), AX, ALU.add))
            ef.ksq_r = e_dk
            for hp in range(KC):
                pb_ = pxk0 if hp % 2 == 0 else pxk1
                fw.w("tensor", e_kev[hp], *ef.xkmm[hp % 2])
                fw.mm(pb_[:, 0:512], kst[:, b, hp, :], c.projn2[:])
                e_xk = fw.mark("tensor")
                fw.w("vector", e_xk, ef.xkred)
                e_red = fw.v(lambda E, pb_=pb_: E.tensor_reduce(
                    c.sstat[:, 14:16, None], pb_[:].rearrange("p (h m) -> p h m", h=2), AX, ALU.max))
                fw.w("vector", e_red)
                e_max = fw.v(lambda E, hp=hp: E.tensor_tensor(
                    c.st_kmax[:, 2 * hp:2 * hp + 2], c.st_kmax[:, 2 * hp:2 * hp + 2],
                    c.sstat[:, 14:16], ALU.max))
                ef.xkmm[hp % 2] = [e_max]
                ef.xkred = e_max
            e_xk_all = fw.mark("tensor")
            ef.k[b] = [e_ksp, e_xk_all]

        prevA = None
        for i in range(NT):
            e_q8b = quantV(i)
            if prevA is not None:
                stageB(i - 1, prevA)
            prevA = stageA(i, e_q8b)
        stageB(NT - 1, prevA)
        c.e_spills_p2 = [ef.x1[0], ef.x1[1], ef.k[0][0], ef.k[1][0], ef.v[0], ef.v[1]]
        PH["p2"] = fw.barrier()
    esP12.close()
    if MK_PHASES <= 2:
        fw.emit(); esBA.close(); es.close(); return nc

    dstab = nc.dram_tensor("stabs", [H, 1], F32)
    esP3 = ExitStack()
    c.ctx_acc = esP3.enter_context(nc.sbuf_tensor("ctx_acc", [65, KC, 2 * M], F32))
    krd = esP3.enter_context(nc.sbuf_tensor("krd", [128, 2, KC, TS], BF16))
    vrd = esP3.enter_context(nc.sbuf_tensor("vrd", [128, 2, H, DH + 1], BF16))
    E1k = esP3.enter_context(nc.sbuf_tensor("E1k", [128, H, M], F32))
    kp = esP3.enter_context(nc.sbuf_tensor("kp", [128, 2, H, M], BF16))
    stt16 = esP3.enter_context(nc.sbuf_tensor("stt16", [16, 2], F32))
    pxk0 = esP3.enter_context(nc.psum_tensor("pxr0", [128, 512], F32))
    pxk1 = esP3.enter_context(nc.psum_tensor("pxr1", [128, 512], F32))
    pcx0 = esP3.enter_context(nc.psum_tensor("pcx0", [65, 512], F32))
    pcx1 = esP3.enter_context(nc.psum_tensor("pcx1", [65, 512], F32))
    pst = esP3.enter_context(nc.psum_tensor("pst", [16, 128], F32))
    if True:
        e_ctx0 = fw.v(lambda E: E.memset(c.ctx_acc[:], 0.0))
        fw.w("vector", *c.e_spills_p2)
        fw.w("tensor", *c.e_spills_p2)
        fw.w("gpsimd", *c.e_spills_p2)
        fw.w("scalar", *c.e_spills_p2)
        fw.w("tensor", PH["p2"]["vector"], e_idf)
        e_t = fw.tr(pst[:], c.st_kmax[:], c.identf[:], ev=True)
        fw.w("vector", e_t)
        e_red = fw.v(lambda E: E.tensor_reduce(stt16[:, 0:1], pst[:], AX, ALU.max))
        fw.w("gpsimd", e_red)
        e_st1 = fw.dma("gpsimd", dstab[:], stt16[:, 0:1], "stab", slow=True)
        fw.w("sync", e_st1)
        e_st2 = fw.dma("sync", c.stabk[:], bass.AP(dstab, 0, [[0, 128], [1, H]]), "stabb")
        fw.w("vector", e_st2, PH["p2"]["scalar"])
        s1kf = c.s1k.ap().rearrange("p a b -> p (a b)")
        s2kf = c.s2k.ap().rearrange("p a b -> p (a b)")
        e_a = fw.v(lambda E: E.tensor_single_scalar(s1kf, c.st_diagk.ap().rearrange("p a b -> p (a b)"),
                                                    -1.0 / 16.0, ALU.mult))
        fw.w("vector", e_a)
        stabk_b = bass.AP(c.stabk.ap().tensor, 0, [[c.stabk.ap().ap[0][0], 128], [0, NT], [1, H]])
        e_b = fw.v(lambda E: E.tensor_tensor(c.s1k[:], c.s1k[:], stabk_b, ALU.subtract))
        fw.w("scalar", e_b)
        e_c = fw.s(lambda E: E.activation(s1kf, s1kf, AF.Exp))
        fw.w("vector", e_c)
        e_d = fw.v(lambda E: E.reciprocal(s2kf, s1kf))
        fw.w("vector", e_d)
        e_s2k = fw.v(lambda E: E.tensor_single_scalar(s2kf, s2kf, EPS_K, ALU.mult))

        fw.w("sync", *c.e_spills_p2)
        ef = NS()
        ef.kb = [None, None]
        ef.vb = [None, None]
        ef.xk = [[], []]
        ef.E1 = []
        ef.kp_r = [[], []]
        ef.cx = [[], []]
        ctxf = c.ctx_acc

        def stageA(i):
            b = i % 2
            fw.w("sync", ef.kb[b])
            e_k = fw.dma("sync", krd[:, b, :, :], dks[i], f"ldk{b}")
            fw.w("sync", ef.vb[b])
            e_v = fw.dma("sync", vrd[:, b, :, :], dvs[i], f"ldv{b}")
            e_E1s = []
            for hp in range(KC):
                pb_ = pxk0 if hp % 2 == 0 else pxk1
                fw.w("tensor", e_k, *ef.xk[hp % 2])
                fw.mm(pb_[:, 0:512], krd[:, b, hp, :], c.projn2[:])
                e_xk = fw.mark("tensor")
                fw.w("scalar", e_xk, *ef.kp_r[1 - b][2 * hp:2 * hp + 2])
                e_E1 = fw.s(lambda E, hp=hp, pb_=pb_: E.activation(
                    E1k[:, 2 * hp:2 * hp + 2, :].rearrange("p a b -> p (a b)"), pb_[:], AF.Exp))
                ef.xk[hp % 2] = [e_E1]
                e_E1s.append(e_E1)
            e_kps = []
            for h in range(H):
                fw.w("vector", e_E1s[h // 2], e_s2k)
                e_kp = fw.v(lambda E, h=h, i=i, b=b: E.tensor_scalar(
                    kp[:, b, h, :], E1k[:, h, :], c.s2k[:, i, h:h + 1], c.s1k[:, i, h:h + 1],
                    ALU.add, ALU.mult))
                e_kps.append(e_kp)
            ef.kp_r[b] = e_kps
            ef.kb[b] = e_E1s[-1]
            return e_v, e_kps

        def stageB(i, e_v, e_kps):
            b = i % 2
            e_adds = []
            for j in range(KC):
                pb_ = pcx0 if j % 2 == 0 else pcx1
                fw.w("tensor", e_v, e_kps[2 * j], e_kps[2 * j + 1], *ef.cx[j % 2])
                fw.mm(pb_[:, 0:256], vrd[:, b, 2 * j, :], kp[:, b, 2 * j, :])
                fw.mm(pb_[:, 256:512], vrd[:, b, 2 * j + 1, :], kp[:, b, 2 * j + 1, :])
                e_cmm = fw.mark("tensor")
                fw.w("vector", e_cmm, e_ctx0)
                e_add = fw.v(lambda E, j=j, pb_=pb_: E.tensor_tensor(
                    ctxf[:, j, :], ctxf[:, j, :], pb_[:], ALU.add))
                ef.cx[j % 2] = [e_add]
                e_adds.append(e_add)
            ef.vb[b] = e_adds[-1]

        prevB = None
        for i in range(NT):
            av = stageA(i)
            if prevB is not None:
                stageB(i - 1, *prevB)
            prevB = av
        stageB(NT - 1, *prevB)
        with nc.psum_tensor("pct", [128, 144], F32) as pct:
            prev = []
            for j in range(KC):
                for half in range(2):
                    fw.w("tensor", ef.cx[0][0] if ef.cx[0] else None,
                         ef.cx[1][0] if ef.cx[1] else None, *prev)
                    fw.tr(pct[:, 0:65], ctxf[0:65, j, half * 256:half * 256 + 128], c.identf[0:65, 0:65])
                    fw.tr(pct[:, 72:137], ctxf[0:65, j, half * 256 + 128:half * 256 + 256], c.identf[0:65, 0:65])
                    e_t = fw.mark("tensor")
                    fw.w("scalar", e_t)
                    e_e1 = fw.s(lambda E, j=j, half=half: E.copy(
                        c.ctxT[:, 2 * j + half, 0, :], pct[:, 0:65]))
                    e_e2 = fw.s(lambda E, j=j, half=half: E.copy(
                        c.ctxT[:, 2 * j + half, 1, :], pct[:, 72:137]))
                    prev = [e_e1, e_e2]
            e_ctxT = prev
        PH["p3"] = fw.barrier()
    esP3.close()
    if MK_PHASES <= 3:
        fw.emit(); esBA.close(); es.close(); return nc

    esC1w = ExitStack()
    c.WqT = esC1w.enter_context(nc.sbuf_tensor("WqT", [128, KC, D], BF16))
    c.WoT = esC1w.enter_context(nc.sbuf_tensor("WoT", [128, KC, D], BF16))
    c.qstat = esC1w.enter_context(nc.sbuf_tensor("qstat", [128, 64], F32))
    e_wqT, e_woT = [], []
    for kc in range(KC):
        fw.w("sync", e_cast["wq"], e_cast["wo"])
        e_wqT.append(fw.dmat(c.WqT[:, kc, :], dwb["wq"][:, kc * 128:(kc + 1) * 128], "xb_wq"))
        e_woT.append(fw.dmat(c.WoT[:, kc, :], dwb["wo"][:, kc * 128:(kc + 1) * 128], "xb_wo"))

        esC1 = ExitStack()
    x1rd = esC1.enter_context(nc.sbuf_tensor("x1rd", [128, 2, D], F32))
    afc = esC1.enter_context(nc.sbuf_tensor("afc", [128, 5, D], F32))
    abfc = esC1.enter_context(nc.sbuf_tensor("abfc", [128, 2, D], BF16))
    qsqs = esC1.enter_context(nc.sbuf_tensor("qsqs", [128, D], F32))
    aTc = esC1.enter_context(nc.sbuf_tensor("aTc", [128, 2, KC, TS], BF16))
    qfm = esC1.enter_context(nc.sbuf_tensor("qfm", [128, KC, TS], BF16))
    E1q = esC1.enter_context(nc.sbuf_tensor("E1q", [128, 16, M], F32))
    qp = esC1.enter_context(nc.sbuf_tensor("qp", [128, 3, H, M], BF16))
    qpT = esC1.enter_context(nc.sbuf_tensor("qpTs", [128, 4096], BF16))
    atok = esC1.enter_context(nc.sbuf_tensor("atok", [128, H, DH], BF16))
    apre = esC1.enter_context(nc.sbuf_tensor("apre", [128, KC, TS], BF16))
    sfull = esC1.enter_context(nc.sbuf_tensor("sfull", [128, D], F32))
    uful = esC1.enter_context(nc.sbuf_tensor("ufull", [128, 2, D], F32))
    pbfA = esC1.enter_context(nc.psum_tensor("pbfA", [128, 1024], BF16))
    pbfA2 = esC1.enter_context(nc.psum_tensor("pbfA2", [128, 1024], BF16))
    pq0 = esC1.enter_context(nc.psum_tensor("pq0", [128, 512], F32))
    pq1 = esC1.enter_context(nc.psum_tensor("pq1", [128, 512], F32))
    pxq0 = esC1.enter_context(nc.psum_tensor("pxq0", [128, 512], F32))
    pxq1 = esC1.enter_context(nc.psum_tensor("pxq1", [128, 512], F32))
    pxv0 = pxq0.ap().rearrange("p (a b) -> p a b", a=4)
    pxv1 = pxq1.ap().rearrange("p (a b) -> p a b", a=4)
    pzo0 = esC1.enter_context(nc.psum_tensor("pzo0", [128, 512], F32))
    pzo1 = esC1.enter_context(nc.psum_tensor("pzo1", [128, 512], F32))
    if True:
        fw.w("sync", c.e_spills_p2[0], c.e_spills_p2[1])
        ef = NS()
        ef.x1r = [None, None]
        ef.abfp = [None, None]
        ef.tc = 0
        ef.bk = [None, None]
        ef.aT_rd = [[], []]
        ef.pq_ev = [[], []]
        ef.zof = [None, None]
        ef.qsq = None
        ef.E1_rd = None
        ef.qpTm = [None, None, None]
        ef.xq = [[], []]
        ef.po = [[], []]
        ef.e_aT = [None, None]
        ef.qev_last = None
        ef.attn_in = []
        ef.e_zomm = None
        ef.qp_ev = [None, None, None]
        ef.sread = None
        ef.u = [None, None]

        def trbank(emit_fn, evac_fn, mm_waits=(), ev_waits=()):
            p = ef.tc % 2
            bank = pbfA if p == 0 else pbfA2
            fw.w("tensor", ef.bk[p], *mm_waits)
            emit_fn(bank)
            e_mark = fw.mark("tensor")
            fw.w("scalar", e_mark, *ev_waits)
            e_ev = evac_fn(bank)
            ef.bk[p] = e_ev
            ef.tc += 1
            return e_mark, e_ev

        def A1(i):
            b = i % 2
            ba = i % 5
            fw.w("sync", ef.x1r[b])
            e_x1 = fw.dma("sync", x1rd[:, b, :], dx1[i], f"c1x1{b}")
            fw.w("vector", e_x1, PH["p3"]["scalar"])
            e_af = fw.v(lambda E, b=b, ba=ba, i=i: E.tensor_scalar(
                afc[:, ba, :], x1rd[:, b, :], c.st_mu2[:, i:i + 1], c.st_r2[:, i:i + 1],
                ALU.subtract, ALU.mult))
            ef.x1r[b] = e_af
            fw.w("vector", e_af, ef.abfp[b])
            e_abf = fw.v(lambda E, b=b, ba=ba: E.tensor_copy(abfc[:, b, :], afc[:, ba, :]))
            e_aTmm, e_aT = trbank(
                lambda bank, b=b: [fw.tr(bank[:, kc * 128:(kc + 1) * 128],
                                         abfc[:, b, kc * 128:(kc + 1) * 128], c.ident[:])
                                   for kc in range(KC)],
                lambda bank, b=b: fw.s(lambda E, bank=bank, b=b: E.copy(
                    aTc[:, b, :, :].rearrange("p a b -> p (a b)"), bank[:])),
                mm_waits=[e_abf], ev_waits=list(ef.aT_rd[b]))
            ef.abfp[b] = e_aTmm
            ef.e_aT[b] = e_aT

        def B1(i):
            ba = i % 3
            e_qpT = []
            e_atoks = []

            def out_grp(grp):
                pb_ = pxv0 if grp % 2 == 0 else pxv1
                fw.w("tensor", e_qpT[grp], *e_ctxT, *ef.po[grp % 2])
                for j in range(4):
                    h = grp * 4 + j
                    fw.mm(pb_[:, j, 0:65], qpT[:, h * 256:h * 256 + 128], c.ctxT[:, h, 0, :],
                          start=True, stop=False)
                    fw.mm(pb_[:, j, 0:65], qpT[:, h * 256 + 128:h * 256 + 256], c.ctxT[:, h, 1, :],
                          start=False, stop=True)
                e_omm = fw.mark("tensor")
                fw.w("vector", e_omm)
                e_rd = fw.v(lambda E, pb_=pb_, grp=grp: E.reciprocal(
                    c.sstat[:, 16 + 4 * grp:20 + 4 * grp], pb_[:, :, 64]))
                fw.w("vector", e_rd)
                e_as = []
                for j in range(4):
                    h = grp * 4 + j
                    e_a = fw.v(lambda E, h=h, j=j, grp=grp, pb_=pb_: E.tensor_scalar(
                        atok[:, h, :], pb_[:, j, 0:64], c.sstat[:, 16 + 4 * grp + j:17 + 4 * grp + j],
                        None, ALU.mult))
                    e_as.append(e_a)
                ef.po[grp % 2] = e_as
                e_atoks.extend(e_as)

            for rnd in range(4):
                e_mark, e_ev = trbank(
                    lambda bank, ba=ba, rnd=rnd: [fw.tr(
                        bank[:, t * 128:(t + 1) * 128],
                        qp[:, ba, (rnd * 8 + t) // 2, ((rnd * 8 + t) % 2) * 128:((rnd * 8 + t) % 2) * 128 + 128],
                        c.ident[:]) for t in range(8)],
                    lambda bank, rnd=rnd: fw.s(lambda E, rnd=rnd, bank=bank: E.copy(
                        qpT[:, rnd * 1024:(rnd + 1) * 1024], bank[:])),
                    mm_waits=[ef.qp_ev[ba]] if rnd == 0 else [])
                e_qpT.append(e_ev)
                if rnd >= 1:
                    out_grp(rnd - 1)
            ef.qpTm[ba] = e_mark
            out_grp(3)
            ef.attn_in = e_atoks

        def A2(i):
            b = i % 2
            e_qev = []
            e_qmm = None
            for half in range(2):
                pk = pq0 if half == 0 else pq1
                fw.w("tensor", ef.e_aT[b], e_wqT[-1], *ef.pq_ev[half])
                for j in range(4):
                    dc = half * 4 + j
                    for kc in range(KC):
                        fw.mm(pk[:, j * 128:(j + 1) * 128],
                              c.WqT[:, kc, dc * 128:(dc + 1) * 128], aTc[:, b, kc, :],
                              start=(kc == 0), stop=(kc == KC - 1))
                e_qmm = fw.mark("tensor")
                fw.w("scalar", e_qmm, c.e_bq)
                for j in range(4):
                    dc = half * 4 + j
                    e_qev.append(fw.s(lambda E, dc=dc, j=j, pk=pk: E.activation(
                        qfm[:, dc, :], pk[:, j * 128:(j + 1) * 128],
                        AF.Identity, bias=c.b_fm[:, 0, dc:dc + 1])))
            ef.aT_rd[b] = [e_qmm]
            ef.pq_ev = [e_qev[0:4], e_qev[4:8]]
            ef.qev_last = e_qev[-1]
            _, e_qsq = trbank(
                lambda bank: [fw.tr(bank[:, kc * 128:(kc + 1) * 128], qfm[:, kc, :], c.ident[:])
                              for kc in range(KC)],
                lambda bank: fw.s(lambda E, bank=bank: E.activation(qsqs[:], bank[:], AF.Square)),
                mm_waits=[e_qev[-1]], ev_waits=[ef.qsq])
            fw.w("vector", e_qsq)
            e_dq = fw.v(lambda E: E.tensor_reduce(
                c.qstat[:, 0:16, None], qsqs[:].rearrange("p (h d) -> p h d", h=H), AX, ALU.add))
            ef.qsq = e_dq
            ef.E1h = []
            for hp in range(KC):
                pb_ = pxq0 if hp % 2 == 0 else pxq1
                fw.w("tensor", e_qev[hp] if hp < 2 else None, *ef.xq[hp % 2],
                     *(ef.po[hp % 2] if hp < 2 else []))
                fw.mm(pb_[:, 0:512], qfm[:, hp, :], c.projn2[:])
                e_xq = fw.mark("tensor")
                fw.w("scalar", e_xq, ef.E1_rd if hp == 0 else None)
                e_E1 = fw.s(lambda E, hp=hp, pb_=pb_: E.activation(
                    E1q[:, 2 * hp:2 * hp + 2, :].rearrange("p a b -> p (a b)"), pb_[:], AF.Exp))
                ef.xq[hp % 2] = [e_E1]
                ef.E1h.append(e_E1)

        def B2(i):
            e_att, e_apre = trbank(
                lambda bank: [fw.tr(bank[:, kc * 128:(kc + 1) * 128],
                                    atok.ap().rearrange("p h d -> p (h d)")[:, kc * 128:(kc + 1) * 128],
                                    c.ident[:]) for kc in range(KC)],
                lambda bank: fw.s(lambda E, bank=bank: E.copy(
                    apre[:].rearrange("p a b -> p (a b)"), bank[:])),
                mm_waits=list(ef.attn_in))
            fw.w("tensor", e_apre, e_woT[-1], ef.zof[0], ef.zof[1])
            for kc in range(KC):
                fw.mm(pzo0[:], apre[:, kc, :], c.WoT[:, kc, 0:512],
                      start=(kc == 0), stop=(kc == KC - 1))
            for kc in range(KC):
                fw.mm(pzo1[:], apre[:, kc, :], c.WoT[:, kc, 512:1024],
                      start=(kc == 0), stop=(kc == KC - 1))
            ef.e_zomm = fw.mark("tensor")

        def A3(i):
            ba = i % 3
            fw.w("vector", ef.E1h[3])
            e_mx0 = fw.v(lambda E: E.tensor_reduce(c.qstat[:, 16:24, None], E1q[:, 0:8, :], AX, ALU.max))
            fw.w("vector", ef.E1h[7])
            e_mx1 = fw.v(lambda E: E.tensor_reduce(c.qstat[:, 24:32, None], E1q[:, 8:16, :], AX, ALU.max))
            fw.w("scalar", ef.qsq, e_mx1)
            e_ed = fw.s(lambda E: E.activation(
                c.qstat[:, 32:48], c.qstat[:, 0:16], AF.Exp, scale=-1.0 / 16.0))
            fw.w("vector", e_mx0, e_mx1)
            e_mr = fw.v(lambda E: E.reciprocal(c.qstat[:, 16:32], c.qstat[:, 16:32]))
            fw.w("vector", e_mr, e_ed)
            e_s1 = fw.v(lambda E: E.tensor_tensor(
                c.qstat[:, 32:48], c.qstat[:, 32:48], c.qstat[:, 16:32], ALU.mult))
            fw.w("vector", e_s1)
            e_s2 = fw.v(lambda E: E.reciprocal(c.qstat[:, 48:64], c.qstat[:, 32:48]))
            fw.w("vector", e_s2)
            e_s2b = fw.v(lambda E: E.tensor_single_scalar(
                c.qstat[:, 48:64], c.qstat[:, 48:64], EPS_K, ALU.mult))
            fw.w("vector", e_s2b, ef.qpTm[ba])
            e_qps = []
            for h in range(H):
                e_qp = fw.v(lambda E, h=h, ba=ba: E.tensor_scalar(
                    qp[:, ba, h, :], E1q[:, h, :],
                    c.qstat[:, 48 + h:49 + h], c.qstat[:, 32 + h:33 + h],
                    ALU.add, ALU.mult))
                e_qps.append(e_qp)
            ef.qp_ev[ba] = e_qps[-1]
            ef.E1_rd = e_qps[-1]

        def B3(i):
            b = i % 2
            ba = i % 5
            fw.w("vector", ef.e_zomm, ef.sread, c.e_bc[3])
            e_sA = fw.v(lambda E, ba=ba: E.tensor_tensor(sfull[:, 0:512], pzo0[:], afc[:, ba, 0:512], ALU.add))
            e_sB = fw.v(lambda E, ba=ba: E.tensor_tensor(sfull[:, 512:1024], pzo1[:], afc[:, ba, 512:1024], ALU.add))
            ef.zof = [e_sA, e_sB]
            fw.w("vector", e_sA, e_sB)
            e_sC = fw.v(lambda E: E.tensor_tensor(sfull[:], sfull[:], c.bcastA[:, 3, :], ALU.add))
            mu, r, e_mu, e_r = ln_stats(fw, c, sfull[:], 6, [e_sC])
            fw.w("vector", e_mu, e_r, ef.u[b], c.e_bc[0], c.e_bc[1])
            e_u1 = fw.v(lambda E, b=b, mu=mu, r=r: E.tensor_scalar(uful[:, b, :], sfull[:], mu, r,
                                                       ALU.subtract, ALU.mult))
            fw.w("vector", e_u1)
            e_u2 = fw.v(lambda E, b=b: E.tensor_tensor(uful[:, b, :], uful[:, b, :],
                                                       c.bcastA[:, 0, :], ALU.mult))
            fw.w("vector", e_u2)
            e_u3 = fw.v(lambda E, b=b: E.tensor_tensor(uful[:, b, :], uful[:, b, :],
                                                       c.bcastA[:, 1, :], ALU.add))
            ef.sread = e_u1
            fw.w("gpsimd", e_u3)
            e_usp = fw.dma("gpsimd", dus[i], uful[:, b, :], f"spu{b}")
            ef.u[b] = e_usp

        A1(0)
        A1(1)
        for i in range(NT):
            A2(i)
            if i + 2 < NT:
                A1(i + 2)
            if i > 1:
                B1(i - 2)
                B2(i - 2)
            A3(i)
            if i > 1:
                B3(i - 2)
        for i in (NT - 2, NT - 1):
            B1(i)
            B2(i)
            B3(i)
        c.e_spills_c1 = [ef.u[0], ef.u[1]]
        PH["c1"] = fw.barrier()
    esC1.close()
    esC1w.close()
    esBA.close()
    if MK_PHASES <= 4:
        fw.emit(); es.close(); return nc

    c.W1T = es.enter_context(nc.sbuf_tensor("W1T", [128, KC, FF], BF16))
    c.W2T = es.enter_context(nc.sbuf_tensor("W2T", [128, NFC, D], BF16))
    fw.w("sync", c.e_spills_c1[0], c.e_spills_c1[1])
    e_w1T = []
    for kc in range(KC):
        fw.w("sync", e_cast["w1"])
        e_w1T.append(fw.dmat(c.W1T[:, kc, :], dwb["w1"][:, kc * 128:(kc + 1) * 128], "xb_w1"))
    e_w2T = []
    for fc in range(NFC):
        fw.w("sync", e_cast["w2"])
        e_w2T.append(fw.dmat(c.W2T[:, fc, :], dwb["w2"][:, fc * 128:(fc + 1) * 128], "xb_w2"))

        esC2 = ExitStack()
    urd = esC2.enter_context(nc.sbuf_tensor("urd", [128, 3, D], F32))
    x1rd2 = esC2.enter_context(nc.sbuf_tensor("x1rd2", [128, D], F32))
    ubf = esC2.enter_context(nc.sbuf_tensor("ubf", [128, D], BF16))
    uT2 = esC2.enter_context(nc.sbuf_tensor("uT2", [128, KC, 256], BF16))
    q8cT = esC2.enter_context(nc.sbuf_tensor("q8cTs", [128, KC * TS], FP8))
    hsb = esC2.enter_context(nc.sbuf_tensor("hsb", [128, 2, 256], BF16))
    wful2 = esC2.enter_context(nc.sbuf_tensor("wful2", [128, 2, D], F32))
    x2f = esC2.enter_context(nc.sbuf_tensor("x2f", [128, D], F32))
    q8c = esC2.enter_context(nc.sbuf_tensor("q8c", [128, D], FP8))
    q8cb = esC2.enter_context(nc.sbuf_tensor("q8cb", [128, D], BF16))
    pbfB = esC2.enter_context(nc.psum_tensor("pbfB", [128, 1024], BF16))
    ph0 = esC2.enter_context(nc.psum_tensor("ph0", [128, 512], F32))
    ph1 = esC2.enter_context(nc.psum_tensor("ph1", [128, 512], F32))
    pz2 = [esC2.enter_context(nc.psum_tensor(f"pz2_{j}", [128, 512], F32)) for j in range(4)]
    pq8c = esC2.enter_context(nc.psum_tensor("pq8c", [128, 1024], BF16))
    if True:
        fw.w("sync", c.e_spills_c1[0], c.e_spills_c1[1],
             c.e_spills_p2[0], c.e_spills_p2[1])
        ef = NS()
        ef.u = [None] * 3
        ef.x1r = None
        ef.ubf = None
        ef.uT = []
        ef.pbf = []
        ef.h = [[], []]
        ef.hsb = [[], []]
        ef.pz2free = []
        ef.q8 = []
        ef.pq8 = []
        ef.wslot = [None, None]
        ef.wout = [None, None]
        ef.lnx2 = None
        ef.x2 = None
        ef.w = None
        NB = NT // 2

        def loads(j):
            e_us = []
            for s in range(2):
                i = 2 * j + s
                sl = i % 3
                fw.w("sync", ef.u[sl])
                e_us.append(fw.dma("sync", urd[:, sl, :], dus[i], f"c2u{sl}"))
            return e_us

        def uTstage(j, e_us):
            e_uTs = []
            for s in range(2):
                i = 2 * j + s
                sl = i % 3
                fw.w("vector", e_us[s], ef.ubf)
                e_ubf = fw.v(lambda E, sl=sl: E.tensor_copy(ubf[:], urd[:, sl, :]))
                fw.w("tensor", e_ubf, *ef.pbf)
                for kc in range(KC):
                    fw.tr(pbfB[:, kc * 128:(kc + 1) * 128], ubf[:, kc * 128:(kc + 1) * 128], c.ident[:])
                e_uTmm = fw.mark("tensor")
                ef.ubf = e_uTmm
                fw.w("scalar", e_uTmm, *ef.uT)
                e_uT = fw.s(lambda E, s=s: E.copy(uT2[:, :, s * 128:(s + 1) * 128],
                                                  pbfB[:].rearrange("p (a b) -> p a b", a=KC)))
                e_uTs.append(e_uT)
                ef.pbf = [e_uT]
            ef.uT = e_uTs
            return e_uTs

        def ff(j, e_uTs):
            pend = None
            for fc in range(NFC):
                pb_ = ph0 if fc % 2 == 0 else ph1
                off = ((fc // 2) % 2) * 256
                fw.w("tensor", e_uTs[1], e_w1T[-1], *ef.h[fc % 2])
                if fc == 0:
                    fw.w("tensor", *ef.pz2free, e_w2T[-1])
                for kc in range(KC):
                    fw.mm(pb_[:, off:off + 256], c.W1T[:, kc, fc * 128:(fc + 1) * 128],
                          uT2[:, kc, :], start=(kc == 0), stop=(kc == KC - 1))
                e_hmm = fw.mark("tensor")
                slot = fc % 2
                if fc % 2 == 0:
                    fw.w("scalar", e_hmm, c.e_b1, *ef.hsb[slot])
                    e_hev = fw.s(lambda E, slot=slot, off=off, pb_=pb_, fc=fc: E.activation(
                        hsb[:, slot, :], pb_[:, off:off + 256], AF.Relu,
                        bias=c.b1_fm[:, fc:fc + 1]))
                else:
                    fw.w("vector", e_hmm, c.e_b1, *ef.hsb[slot])
                    e_hev = fw.v(lambda E, slot=slot, off=off, pb_=pb_, fc=fc: E.tensor_scalar(
                        hsb[:, slot, :], pb_[:, off:off + 256],
                        c.b1_fm[:, fc:fc + 1], 0.0, ALU.add, ALU.max))
                ef.h[fc % 2] = [e_hev]
                if pend is not None:
                    pfc, pslot, pev = pend
                    fw.w("tensor", pev)
                    for s in range(2):
                        for hh in range(2):
                            fw.mm(pz2[2 * s + hh][:], hsb[:, pslot, s * 128:(s + 1) * 128],
                                  c.W2T[:, pfc, hh * 512:(hh + 1) * 512],
                                  start=(pfc == 0), stop=False)
                    ef.hsb[pslot] = [fw.mark("tensor")]
                pend = (fc, slot, e_hev)
            pfc, pslot, pev = pend
            fw.w("tensor", pev)
            for s in range(2):
                for hh in range(2):
                    fw.mm(pz2[2 * s + hh][:], hsb[:, pslot, s * 128:(s + 1) * 128],
                          c.W2T[:, pfc, hh * 512:(hh + 1) * 512],
                          start=False, stop=True)
            ef.hsb[pslot] = [fw.mark("tensor")]
            return ef.hsb[pslot][0]

        def w_builds(j, e_z2_all, e_us):
            evs = []
            for s in range(2):
                i = 2 * j + s
                sl = i % 3
                fw.w("vector", e_z2_all, ef.wslot[s], c.e_bc[7], e_us[s])
                e_w1 = fw.v(lambda E, s=s, sl=sl: E.tensor_tensor(
                    wful2[:, s, 0:512], pz2[2 * s][:], urd[:, sl, 0:512], ALU.add))
                e_w2 = fw.v(lambda E, s=s, sl=sl: E.tensor_tensor(
                    wful2[:, s, 512:1024], pz2[2 * s + 1][:], urd[:, sl, 512:1024], ALU.add))
                fw.w("vector", e_w1, e_w2)
                e_w3 = fw.v(lambda E, s=s: E.tensor_tensor(
                    wful2[:, s, :], wful2[:, s, :], c.bcastB[:, 3, :], ALU.add))
                ef.u[sl] = e_w3
                evs.extend([e_w1, e_w2])
                ef.wout[s] = e_w3
            ef.pz2free = evs
            return evs

        def tail(j):
            for s in range(2):
                i = 2 * j + s
                wv = wful2[:, s, :]
                fw.w("sync", ef.x1r)
                e_x1 = fw.dma("sync", x1rd2[:], dx1[i], "c2x1")
                mu, r, e_mu, e_r = ln_stats(fw, c, wv, 0, [ef.wout[s]])
                fw.w("vector", e_mu, e_r, ef.x2, c.e_bc[4], c.e_bc[5], e_x1)
                e_x2a = fw.v(lambda E, mu=mu, r=r, wv=wv: E.tensor_scalar(
                    x2f[:], wv, mu, r, ALU.subtract, ALU.mult))
                fw.w("vector", e_x2a)
                e_x2b = fw.v(lambda E: E.tensor_tensor(x2f[:], x2f[:], c.bcastB[:, 0, :], ALU.mult))
                fw.w("vector", e_x2b)
                e_x2c = fw.v(lambda E: E.tensor_tensor(x2f[:], x2f[:], c.bcastB[:, 1, :], ALU.add))
                fw.w("vector", e_x2c)
                e_x2 = fw.v(lambda E: E.tensor_tensor(x2f[:], x2f[:], x1rd2[:], ALU.add))
                ef.x1r = e_x2
                mu2_, r2_, e_mu2, e_r2 = ln_stats(fw, c, x2f[:], 6, [e_x2])
                fw.w("vector", e_mu2, e_r2, ef.lnx2)
                e_ln = fw.v(lambda E, mu2_=mu2_, r2_=r2_, wv=wv: E.tensor_scalar(
                    wv, x2f[:], mu2_, r2_, ALU.subtract, ALU.mult))
                ef.x2 = e_ln
                st = c.sstat
                am = st[:, 12:13]
                qs = st[:, 13:14]
                fw.w("vector", e_ln)
                e_am = fw.v(lambda E, wv=wv: E.tensor_reduce(am, wv, AX, ALU.max,
                                                             apply_absolute_value=True))
                fw.w("vector", e_am)
                e_am2 = fw.v(lambda E: E.tensor_single_scalar(am, am, 1e-12, ALU.max))
                fw.w("vector", e_am2)
                e_qs = fw.v(lambda E: E.reciprocal(qs, am))
                fw.w("vector", e_qs)
                e_qs2 = fw.v(lambda E: E.tensor_single_scalar(qs, qs, 224.0, ALU.mult))
                fw.w("vector", c.e_ws)
                e_s1 = fw.v(lambda E: E.tensor_scalar(st[:, 14:15], am, 1.0 / 224.0,
                                                      c.wsv[:, 0:1], ALU.mult, ALU.mult))
                fw.w("vector", e_qs2, *ef.q8)
                e_q8 = fw.v(lambda E, wv=wv: E.tensor_scalar(q8c[:], wv, qs, None, ALU.mult))
                fw.w("vector", e_q8)
                e_q8b = fw.v(lambda E: E.tensor_copy(q8cb[:], q8c[:]))
                ef.lnx2 = e_q8
                fw.w("tensor", e_q8b, *ef.pq8)
                for kc in range(KC):
                    fw.tr(pq8c[:, kc * 128:(kc + 1) * 128], q8cb[:, kc * 128:(kc + 1) * 128],
                          c.ident[:])
                e_tt8 = fw.mark("tensor")
                ef.q8 = [e_tt8]
                fw.w("scalar", e_tt8, *ef.pq8)
                e_ev8a = fw.s(lambda E: E.copy(q8cT[:, 0:512], pq8c[:, 0:512]))
                e_ev8b = fw.s(lambda E: E.copy(q8cT[:, 512:1024], pq8c[:, 512:1024]))
                fw.w("tensor", e_ev8a, e_ev8b, *(ef.h[0] + ef.h[1]))
                for kc in range(KC):
                    fw.mm(ph0[:], q8cT[:, kc * 128:(kc + 1) * 128], c.w8T[:, kc, 0:512],
                          start=(kc == 0), stop=(kc == KC - 1))
                for kc in range(KC):
                    fw.mm(ph1[:], q8cT[:, kc * 128:(kc + 1) * 128], c.w8T[:, kc, 512:1024],
                          start=(kc == 0), stop=(kc == KC - 1))
                e_zp = fw.mark("tensor")
                ef.pq8 = [e_zp]
                fw.w("vector", e_zp, e_q8b)
                e_o1 = fw.v(lambda E, s=s: E.scalar_tensor_tensor(
                    wful2[:, s, 0:512], ph0[:], st[:, 14:15],
                    c.bcastB[:, 2, 0:512], ALU.mult, ALU.add))
                e_o2 = fw.v(lambda E, s=s: E.scalar_tensor_tensor(
                    wful2[:, s, 512:1024], ph1[:], st[:, 14:15],
                    c.bcastB[:, 2, 512:1024], ALU.mult, ALU.add))
                fw.w("vector", e_o1, e_o2)
                e_o3 = fw.v(lambda E, s=s: E.tensor_tensor(wful2[:, s, :], wful2[:, s, :],
                                                           x2f[:], ALU.add))
                ef.h[0] = [e_o1]
                ef.h[1] = [e_o2]
                fw.w("gpsimd", e_o3)
                e_osp = fw.dma("gpsimd", dout[i], wful2[:, s, :], f"out{s}")
                ef.wslot[s] = e_osp
        prev = None
        for j in range(NB):
            e_us = loads(j)
            e_uTs = uTstage(j, e_us)
            e_z2_all = ff(j, e_uTs)
            if prev is not None:
                tail(prev)
            w_builds(j, e_z2_all, e_us)
            prev = j
        tail(prev)
        fw.w("gpsimd", ef.wslot[0], ef.wslot[1])
        fw.g(lambda E: E.nop())
    esC2.close()
    fw.emit()
    es.close()
    return nc


from concourse.bass_utils import run_bass_kernel_spmd

_NCORES = 8

def kernel(**inputs):
    nc = build_nc()
    video = np.asarray(inputs["L0_attn_video"])
    base = {
        "wq": np.asarray(inputs["L0_attn_Wq"]),
        "wk": np.asarray(inputs["L0_attn_Wk"]),
        "wv": np.asarray(inputs["L0_attn_Wv"]),
        "wo": np.asarray(inputs["L0_attn_Wo"]),
        "w1": np.asarray(inputs["L0_attn_W1"]),
        "w2": np.asarray(inputs["L0_attn_W2"]),
        "w8": np.asarray(inputs["L0_ffn_weight"]).view(np.uint8),
        "proj": np.asarray(inputs["L0_attn_proj"]),
        "bq": np.asarray(inputs["L0_attn_bq"]),
        "bk": np.asarray(inputs["L0_attn_bk"]),
        "bv": np.asarray(inputs["L0_attn_bv"]),
        "bo": np.asarray(inputs["L0_attn_bo"]),
        "b1": np.asarray(inputs["L0_attn_b1"]),
        "b2": np.asarray(inputs["L0_attn_b2"]),
        "g2": np.asarray(inputs["L0_attn_g2"]),
        "be2": np.asarray(inputs["L0_attn_be2"]),
        "g3": np.asarray(inputs["L0_attn_g3"]),
        "be3": np.asarray(inputs["L0_attn_be3"]),
        "ws": np.asarray(inputs["L0_ffn_weight_scale"]).reshape(1, 1),
        "b8": np.asarray(inputs["L0_ffn_bias"]),
    }
    in_maps = []
    for core in range(_NCORES):
        m = dict(base)
        m["video"] = np.ascontiguousarray(video[core]).reshape(NT, TS, D)
        in_maps.append(m)
    res = run_bass_kernel_spmd(nc, in_maps, core_ids=list(range(_NCORES)))
    out = np.stack([np.asarray(res.results[core]["out"]).reshape(T, D)
                    for core in range(_NCORES)], axis=0)
    return out
